# version 05
# baseline (speedup 1.0000x reference)
import os as _os
import numpy as np
import ml_dtypes
from contextlib import ExitStack
from collections import defaultdict

import concourse.bass as bass
import concourse.mybir as mybir
from concourse.bass_utils import run_bass_kernel_spmd

F32 = mybir.dt.float32
BF16 = mybir.dt.bfloat16
AF = mybir.ActivationFunctionType
ALU = mybir.AluOpType
AX = mybir.AxisListType

MAGIC = float(np.float32(1.5 * 2 ** 23))


class PB:
    ENGS = ("sync", "vector", "scalar", "tensor", "gpsimd")

    def __init__(self, nc, ctx):
        self.nc = nc
        self.ctx = ctx
        self.items = {e: [] for e in self.ENGS}
        self.esem = {e: ctx.enter_context(nc.semaphore(f"S_{e}")) for e in self.ENGS}
        self.dsem = {}
        self.ecount = {e: 0 for e in self.ENGS}
        self.dcount = defaultdict(int)
        self.res_w = {}
        self.res_r = defaultdict(list)

    def dma_sem(self, name):
        if name not in self.dsem:
            self.dsem[name] = self.ctx.enter_context(self.nc.semaphore(f"D_{name}"))
        return self.dsem[name]

    def op(self, eng, fn, reads=(), writes=(), extra_deps=(), dma=None, tag=None):
        deps = set()
        for r in reads:
            t = self.res_w.get(r)
            if t is not None:
                deps.add(t)
        for r in writes:
            t = self.res_w.get(r)
            if t is not None:
                deps.add(t)
            for t2 in self.res_r.get(r, ()):
                deps.add(t2)
        for t in extra_deps:
            if t is not None:
                deps.add(t)
        if dma is None:
            sem = self.esem[eng]
            self.ecount[eng] += 1
            tok = (sem, self.ecount[eng], eng, False)
        else:
            step = 1 if dma.startswith("ccol") else 16
            sem = self.dma_sem(dma)
            self.dcount[dma] += step
            tok = (sem, self.dcount[dma], eng, True)
        deps.discard(tok)
        self.items[eng].append((fn, sorted(deps, key=lambda t: (t[0].name, t[1])),
                               sem, (1 if (dma and dma.startswith("ccol")) else (16 if dma else 1)), tag))
        for r in reads:
            self.res_r[r].append(tok)
        for r in writes:
            self.res_w[r] = tok
            self.res_r[r] = [tok]
        return tok

    def barrier(self, prefix):
        toks = set()
        for r in set(list(self.res_w) + list(self.res_r)):
            if r.startswith(prefix):
                t = self.res_w.get(r)
                if t is not None:
                    toks.add(t)
                for t2 in self.res_r.get(r, ()):
                    toks.add(t2)
        return list(toks)

    def emit(self):
        nc = self.nc
        stats = {e: {"ops": len(self.items[e]), "waits": 0} for e in self.ENGS}
        with nc.Block() as block:
            for eng in self.ENGS:
                items = self.items[eng]
                if not items:
                    continue

                def make(eng=eng, items=items):
                    def f(proxy):
                        waited = {}
                        for (fn, deps, sem, inc, tag) in items:
                            for (dsm, val, peng, isdma) in deps:
                                if peng == "tensor" and eng == "tensor" and not isdma:
                                    continue
                                if waited.get(dsm.name, 0) >= val:
                                    continue
                                proxy.wait_ge(dsm, val)
                                waited[dsm.name] = val
                                stats[eng]["waits"] += 1
                            fn(proxy).then_inc(sem, inc)
                    return f

                getattr(block, eng)(make())
        return stats


class CFG:
    def __init__(self, D=2048, T=1024, NH=16, DFF=8192, debug=False):
        self.P = 128
        self.D, self.T, self.NH, self.HD, self.DFF = D, T, NH, 128, DFF
        self.BLK = 256
        self.FTS = T // 2
        self.n_cores = 8
        self.debug = debug
        self.FC = D // 128
        self.KC = DFF // 128
        self.NB = T // self.BLK
        self.NTC = T // 128
        self.MW = min(512, T)
        self.NMW = T // self.MW
        assert T % self.BLK == 0


def build_nc(cfg, upto='all'):
    P, D, T, NH, HD, DFF = cfg.P, cfg.D, cfg.T, cfg.NH, cfg.HD, cfg.DFF
    FC, KC, NB, BLK, NTC = cfg.FC, cfg.KC, cfg.NB, cfg.BLK, cfg.NTC
    FTS, MW, NMW = cfg.FTS, cfg.MW, cfg.NMW
    CCW = NH * HD

    nc = bass.Bass("TRN2", target_bir_lowering=False)

    x_in = nc.dram_tensor("x_fm", [D, T], F32, kind="ExternalInput")
    wq_in = nc.dram_tensor("wq_t", [NH, P, FC * 128], F32, kind="ExternalInput")
    wk_in = nc.dram_tensor("wk_t", [NH, P, FC * 128], F32, kind="ExternalInput")
    wv_in = nc.dram_tensor("wv_t", [NH, P, FC * 128], F32, kind="ExternalInput")
    wg_in = nc.dram_tensor("wg_t", [FC, P, FC * 128], F32, kind="ExternalInput")
    wo_in = nc.dram_tensor("wo_t", [FC, P, FC * 128], F32, kind="ExternalInput")
    w1_in = nc.dram_tensor("w1_t", [KC, P, FC * 128], F32, kind="ExternalInput")
    w2_in = nc.dram_tensor("w2_t", [FC * 4, P, KC * 32], F32, kind="ExternalInput")
    n1t = max(1, (DFF * D // 8) // (128 * 2048))
    c1 = (DFF * D // 8) // (128 * n1t)
    w1sl_in = nc.dram_tensor("w1sl", [n1t, P, c1], F32, kind="ExternalInput")
    w2sl_in = nc.dram_tensor("w2sl", [n1t, P, c1], F32, kind="ExternalInput")
    b1_in = nc.dram_tensor("b1_t", [P, KC], F32, kind="ExternalInput")
    b2_in = nc.dram_tensor("b2_t", [P, FC], F32, kind="ExternalInput")
    nw_in = nc.dram_tensor("nw_t", [P, FC], F32, kind="ExternalInput")
    slo_in = nc.dram_tensor("slopes", [1, NH], F32, kind="ExternalInput")
    kvsel_in = nc.dram_tensor("kvsel", [1, 8], F32, kind="ExternalInput")
    out_t = nc.dram_tensor("out_fm", [D, T], F32, kind="ExternalOutput")

    w1q_s = nc.dram_tensor("w1q_s", [KC, P, FC * 128], BF16, kind="Internal")
    w2q_s = nc.dram_tensor("w2q_s", [FC * 4, P, (KC // 4) * 128], BF16, kind="Internal")
    cc_in = nc.dram_tensor("cc_in", [P, CCW], BF16, kind="Internal")
    cc_out = nc.dram_tensor("cc_out", [8 * P, CCW], BF16, kind="Internal")
    cc_in2 = nc.dram_tensor("cc_in2", [16, 128], F32, kind="Internal")
    cc_out2 = nc.dram_tensor("cc_out2", [128, 128], F32, kind="Internal")

    arrq_np = (np.arange(BLK) + 1).astype(np.float32)
    arrk_np = (BLK - 1 - np.arange(BLK)).astype(np.float32)
    dm = np.arange(BLK)[None, :] - np.arange(P)[:, None]
    didx_np = np.concatenate([np.maximum(dm - j * P, 0) for j in range(BLK // P)], 1)
    dmsk_np = np.concatenate([(dm - j * P >= 0) for j in range(BLK // P)], 1)
    cb_np = np.concatenate([
        np.broadcast_to(arrq_np, (P, BLK)),
        np.broadcast_to(arrk_np, (P, BLK)),
        didx_np.astype(np.float32),
        dmsk_np.astype(np.float32),
        np.eye(P, dtype=np.float32),
        np.ones((P, 1), dtype=np.float32),
    ], axis=1).astype(ml_dtypes.bfloat16)
    cf_np = np.concatenate([
        np.ones((P, P), dtype=np.float32),
        np.broadcast_to((np.arange(NB) * BLK).astype(np.float32), (P, NB)),
        np.full((P, 1), float(BLK), np.float32),
        np.full((P, 1), -MAGIC, np.float32),
    ], axis=1)
    cb_t = nc.inline_tensor(cb_np, name="c_bf")
    cf_t = nc.inline_tensor(np.ascontiguousarray(cf_np), name="c_f32")

    dbg = {}
    if cfg.debug:
        for nm, shp in [("d_xn", [D, T]), ("d_o1", [NH * HD, T]), ("d_og", [NH * HD, T]),
                        ("d_s", [D, T]), ("d_cc", [P, CCW + 2]), ("d_rstd", [1, T]),
                        ("d_invs", [1, 2]), ("d_kvin", [P, CCW])]:
            dbg[nm] = nc.dram_tensor(nm, shp, F32, kind="ExternalOutput")

    ctx = ExitStack()
    SB = lambda n, s, d=F32: ctx.enter_context(nc.sbuf_tensor(n, s, d))

    XN = SB("XN", [P, FC * T], BF16)
    O = SB("O", [P, NH * T], BF16)
    NA = 2048 + max(FC * FTS // 2, 6 * T)
    POOLA = SB("POOLA", [P, NA], F32)
    KVc = 0
    Sc = 2048

    def S_ap(c0, w):
        return POOLA[:, :].bitcast(BF16)[:, 2 * Sc + c0:2 * Sc + c0 + w]
    NBC = max(KC * FTS, NH * T + 16 * T + max(T, 2 * NB * HD, NMW * MW), 2048)
    POOLB = SB("POOLB", [P, NBC], BF16)
    QCb = 0
    TR0 = NH * T
    TAb = [TR0 + pb * 8 * T for pb in range(2)]
    TBb = [TR0 + pb * 8 * T + T for pb in range(2)]
    TCb = [TR0 + pb * 8 * T + 2 * T for pb in range(2)]
    TDb = [TR0 + pb * 8 * T + 3 * T for pb in range(2)]
    TEb = [TR0 + pb * 8 * T + 4 * T for pb in range(2)]
    TGb = [TR0 + pb * 8 * T + 5 * T for pb in range(2)]
    QKb = [TR0 + pb * 8 * T + 6 * T for pb in range(2)]
    SQb = TR0 + 16 * T
    GTb = SQb
    KVIb = QKb[0]
    XQb = 0
    Hb = 0
    ENDB = TR0 + 16 * T + max(T, 2 * NB * HD, NMW * MW)
    assert ENDB <= NBC, ENDB

    NWC = SB("NWC", [P, FC], F32)
    B1C = SB("B1C", [P, KC], F32)
    B2C = SB("B2C", [P, FC], F32)
    SLO = SB("SLO", [1, NH], F32)
    NEGR = SB("NEGR", [1, NH], F32)
    NEGSB = SB("NEGSB", [P, NH], F32)
    BDB = SB("BDB", [P, NH], F32)
    KVSEL = SB("KVSEL", [1, 8], F32)
    KVSELB = SB("KVSELB", [P, 8], F32)
    SCAL = SB("SCAL", [1, 4], F32)
    PART8 = SB("PART8", [8, 2], F32)
    INVSB = SB("INVSB", [P, 2], F32)
    WSUM = SB("WSUM", [P, 4], F32)
    KVBF = SB("KVBF", [P, HD], BF16)
    PSTG = SB("PSTG", [P, 128], F32)
    NSLOT = 10
    SLOTW = 512
    WLBUF = SB("WLBUF", [P, NSLOT * SLOTW], F32)
    WBBUF = SB("WBBUF", [P, NSLOT * SLOTW], BF16)
    WL = [WLBUF[:, i * SLOTW:(i + 1) * SLOTW] for i in range(NSLOT)]
    WB = [WBBUF[:, i * SLOTW:(i + 1) * SLOTW] for i in range(NSLOT)]
    CONSTB = SB("CONSTB", [P, 6 * BLK + P + 1], BF16)
    CONSTF = SB("CONSTF", [P, P + NB + 2], F32)
    ARRQ = CONSTB[:, 0:BLK]
    ARRK = CONSTB[:, BLK:2 * BLK]
    DIDX = CONSTB[:, 2 * BLK:4 * BLK]
    DMSK = CONSTB[:, 4 * BLK:6 * BLK]
    IDENT = CONSTB[:, 6 * BLK:6 * BLK + P]
    ONESC = CONSTB[:, 6 * BLK + P:6 * BLK + P + 1]
    ONESR = CONSTF[0:1, 0:P]
    ONESCF = CONSTF[:, 0:1]
    BLKI = CONSTF[:, P:P + NB]
    BLKCOL = CONSTF[:, P + NB:P + NB + 1]
    NEGM = CONSTF[:, P + NB + 1:P + NB + 2]
    DECQT = SB("DECQT", [P, 2 * BLK], BF16)
    DECKT = SB("DECKT", [P, 2 * BLK], BF16)
    DIAGT = SB("DIAGT", [P, 4 * BLK], BF16)
    BDPT = SB("BDPT", [P, 2 * NB], F32)
    DECQ = [DECQT[:, i * BLK:(i + 1) * BLK] for i in range(2)]
    DECK = [DECKT[:, i * BLK:(i + 1) * BLK] for i in range(2)]
    DIAG = [DIAGT[:, i * 2 * BLK:(i + 1) * 2 * BLK] for i in range(2)]
    BDP = [BDPT[:, i * NB:(i + 1) * NB] for i in range(2)]

    PS = [ctx.enter_context(nc.psum_tensor(f"PS{i}", [P, 512], F32)) for i in range(7)]
    PT = ctx.enter_context(nc.psum_tensor("PT", [P, 512], F32))

    B = PB(nc, ctx)
    op = B.op

    def row(k):
        return POOLA[0:1, Sc + k * T: Sc + (k + 1) * T]

    def rowm(k, mt):
        return POOLA[0:1, Sc + k * T + mt * MW: Sc + k * T + mt * MW + MW]

    def _finish():
        toks = []
        for ch, cnt in list(B.dcount.items()):
            if cnt:
                toks.append((B.dsem[ch], cnt, "sync", True))
        for eng in B.ENGS:
            if B.ecount[eng] and eng != "gpsimd":
                toks.append((B.esem[eng], B.ecount[eng], eng, False))
        op("gpsimd", lambda e: e.nop(), extra_deps=toks)
        stats = B.emit()
        ctx.close()
        return nc, stats

    op("gpsimd", lambda e: e.dma_start(out=CONSTB[:, :], in_=cb_t[:, :]),
       writes=["cb"], dma="ccb")
    op("gpsimd", lambda e: e.dma_start(out=CONSTF[:, :], in_=cf_t[:, :]),
       writes=["cf"], dma="ccf")
    for (dst, srct, res) in [(NWC, nw_in, "nwc"), (B1C, b1_in, "b1c"), (B2C, b2_in, "b2c"),
                             (SLO, slo_in, "slo"), (KVSEL, kvsel_in, "kvsel")]:
        op("gpsimd", lambda e, d=dst, s=srct: e.dma_start(out=d[:, :], in_=s[:, :]),
           writes=[res], dma=f"c_{res}")

    op("vector", lambda e: e.tensor_scalar_mul(NEGR[:, :], SLO[:, :], -1.0),
       reads=["slo"], writes=["negr"])
    for h in range(NH):
        op("tensor", lambda e, h=h: e.matmul(PS[5][:, 0:1], ONESR,
                                             NEGR[0:1, h:h + 1], start=True, stop=True),
           reads=["negr", "cf"], writes=["PS5"])
        op("vector", lambda e, h=h: e.tensor_copy(NEGSB[:, h:h + 1], PS[5][:, 0:1]),
           reads=["PS5"], writes=["negsb"])
    op("tensor", lambda e: e.matmul(PS[5][:, 0:8], ONESR, KVSEL[0:1, :],
                                    start=True, stop=True),
       reads=["kvsel", "cf"], writes=["PS5"])
    op("vector", lambda e: e.tensor_copy(KVSELB[:, :], PS[5][:, 0:8]),
       reads=["PS5"], writes=["kvselb"])
    for h in range(NH):
        op("scalar", lambda e, h=h: e.activation(BDB[:, h:h + 1], BLKCOL, AF.Exp,
                                                 bias=0.0, scale=NEGSB[:, h:h + 1]),
           reads=["negsb", "cf", "bdb"], writes=["bdb"])

    XQs = [XQb + i * 2 * T for i in range(2)]

    def xq_ap(bi, half):
        c = XQs[bi] + half * T
        return POOLB[:, c:c + T]

    def xbuf(bi):
        return POOLA[:, KVc + bi * MW: KVc + (bi + 1) * MW]

    def xrow(bi):
        return POOLA[:, KVc + bi * T: KVc + (bi + 1) * T]

    for fcc in range(FC):
        bi = fcc % 2
        op("gpsimd", lambda e, b=bi, f=fcc: e.dma_start(
            out=xrow(b), in_=x_in[f * 128:(f + 1) * 128, :]),
           writes=[f"xr{bi}"], dma=f"x{bi}")
        op("vector", lambda e, b=bi: e.tensor_copy(xq_ap(b, 0), xrow(b)),
           reads=[f"xr{bi}"], writes=[f"xc{bi}"])
        op("scalar", lambda e, b=bi: e.activation(xq_ap(b, 1), xrow(b), AF.Square,
                                                  bias=0.0, scale=1.0),
           reads=[f"xr{bi}"], writes=[f"xq{bi}"])
        for mt in range(NMW):
            op("tensor", lambda e, b=bi, f=fcc, m=mt: e.matmul(
                PS[0 + m][0:1, 0:MW], ONESC, xq_ap(b, 0)[:, m * MW:(m + 1) * MW],
                start=(f == 0), stop=(f == FC - 1)),
               reads=[f"xc{bi}", "cb"], writes=[f"PS{0 + mt}"])
            op("tensor", lambda e, b=bi, f=fcc, m=mt: e.matmul(
                PS[2 + m][0:1, 0:MW], ONESC, xq_ap(b, 1)[:, m * MW:(m + 1) * MW],
                start=(f == 0), stop=(f == FC - 1)),
               reads=[f"xq{bi}", "cb"], writes=[f"PS{2 + mt}"])
    for mt in range(NMW):
        op("vector", lambda e, m=mt: e.tensor_scalar_mul(rowm(0, m), PS[0 + m][0:1, 0:MW], 1.0 / D),
           reads=[f"PS{0 + mt}"], writes=["row_mean"])
        op("vector", lambda e, m=mt: e.tensor_scalar_mul(rowm(1, m), PS[2 + m][0:1, 0:MW], 1.0 / D),
           reads=[f"PS{2 + mt}"], writes=["row_msq"])
    op("vector", lambda e: e.tensor_mul(row(2), row(0), row(0)),
       reads=["row_mean"], writes=["row_tmp"])
    op("vector", lambda e: e.tensor_sub(row(1), row(1), row(2)),
       reads=["row_msq", "row_tmp"], writes=["row_msq"])
    op("vector", lambda e: e.tensor_scalar_add(row(1), row(1), 1e-5),
       reads=["row_msq"], writes=["row_msq"])

    def rsqrt_row(src_k, dst_k, tmp_k, src_res, dst_res, tmp_res):
        op("scalar", lambda e: e.activation(row(tmp_k), row(src_k), AF.Sqrt,
                                            bias=0.0, scale=1.0),
           reads=[src_res], writes=[tmp_res])
        op("vector", lambda e: e.reciprocal(row(dst_k), row(tmp_k)),
           reads=[tmp_res], writes=[dst_res])
        op("vector", lambda e: e.tensor_mul(row(tmp_k), row(dst_k), row(dst_k)),
           reads=[dst_res], writes=[tmp_res])
        op("vector", lambda e: e.tensor_mul(row(tmp_k), row(tmp_k), row(src_k)),
           reads=[tmp_res, src_res], writes=[tmp_res])
        op("vector", lambda e: e.tensor_scalar(row(tmp_k), row(tmp_k), -0.5, 1.5,
                                               ALU.mult, ALU.add),
           reads=[tmp_res], writes=[tmp_res])
        op("vector", lambda e: e.tensor_mul(row(dst_k), row(dst_k), row(tmp_k)),
           reads=[dst_res, tmp_res], writes=[dst_res])

    rsqrt_row(1, 3, 2, "row_msq", "row_arow", "row_tmp")
    op("vector", lambda e: e.scalar_tensor_tensor(row(4), row(0), -1.0, row(3),
                                                  ALU.mult, ALU.mult),
       reads=["row_mean", "row_arow"], writes=["row_brow"])
    for mt in range(NMW):
        op("tensor", lambda e, m=mt: e.matmul(PS[4 + m][:, 0:MW], ONESR, rowm(3, m),
                                              start=True, stop=True),
           reads=["row_arow", "cf"], writes=[f"PS{4 + mt}"])
        op("tensor", lambda e, m=mt: e.matmul(PS[0 + m][:, 0:MW], ONESR, rowm(4, m),
                                              start=True, stop=True),
           reads=["row_brow", "cf"], writes=[f"PS{0 + mt}"])
    for fcc in range(FC):
        bi = fcc % 2
        op("gpsimd", lambda e, b=bi, f=fcc: e.dma_start(
            out=xrow(b), in_=x_in[f * 128:(f + 1) * 128, :]),
           writes=[f"xr{bi}"], dma=f"x{bi}")
        for mt in range(NMW):
            op("vector", lambda e, b=bi, m=mt: e.tensor_mul(
                xrow(b)[:, m * MW:(m + 1) * MW], xrow(b)[:, m * MW:(m + 1) * MW],
                PS[4 + m][:, 0:MW]),
               reads=[f"xr{bi}", f"PS{4 + mt}"], writes=[f"xr{bi}"])
            op("vector", lambda e, b=bi, f=fcc, m=mt: e.tensor_add(
                XN[:, f * T + m * MW: f * T + (m + 1) * MW],
                xrow(b)[:, m * MW:(m + 1) * MW], PS[0 + m][:, 0:MW]),
               reads=[f"xr{bi}", f"PS{0 + mt}"], writes=[f"xn{fcc}"])

    if cfg.debug:
        def dump_fm(dramt, src_ap_fn, nchunk, width, res_of, chan="dbg"):
            for ccc in range(nchunk):
                for w0 in range(0, width, MW):
                    op("scalar", lambda e, ccc=ccc, w0=w0: e.activation(
                        xbuf(3), src_ap_fn(ccc, w0), AF.Identity, bias=0.0, scale=1.0),
                       reads=[res_of(ccc)], writes=["xdbg"])
                    op("gpsimd", lambda e, ccc=ccc, w0=w0, d=dramt: e.dma_start(
                        out=d[ccc * 128:(ccc + 1) * 128, w0:w0 + MW], in_=xbuf(3)),
                       reads=["xdbg"], dma=chan)

        dump_fm(dbg["d_xn"], lambda ccc, w0: XN[:, ccc * T + w0: ccc * T + w0 + MW],
                FC, T, lambda ccc: f"xn{ccc}")

    if upto == "p0":
        return _finish()

    wuc = [0]
    op("gpsimd", lambda e: e.memset(WSUM[:, :], 0.0), writes=["wsum", "wsumt"])
    for wi, wsl in enumerate([w1sl_in, w2sl_in]):
        for t in range(n1t):
            for c0 in range(0, c1, SLOTW):
                w = min(SLOTW, c1 - c0)
                s = wuc[0] % NSLOT
                wuc[0] += 1
                op("scalar", lambda e, s=s, t=t, wt=wsl, c0=c0, w=w: e.dma_start(
                    out=WL[s][:, 0:w], in_=wt[t, :, c0:c0 + w]),
                   writes=[f"wl{s}"], dma=f"wsl{s}")
                op("vector", lambda e, s=s, wi=wi, w=w: e.tensor_reduce(
                    WSUM[:, 2 + wi:3 + wi], WL[s][:, 0:w], AX.X, ALU.add,
                    apply_absolute_value=True),
                   reads=[f"wl{s}"], writes=["wsumt"])
                op("vector", lambda e, wi=wi: e.tensor_add(
                    WSUM[:, wi:wi + 1], WSUM[:, wi:wi + 1], WSUM[:, 2 + wi:3 + wi]),
                   reads=["wsum", "wsumt"], writes=["wsum"])
    op("tensor", lambda e: e.matmul(PS[6][0:1, 0:1], WSUM[:, 0:1], ONESCF,
                                    start=True, stop=True),
       reads=["wsum", "cf"], writes=["PS6"])
    op("tensor", lambda e: e.matmul(PS[6][0:1, 1:2], WSUM[:, 1:2], ONESCF,
                                    start=False, stop=True, skip_group_check=True),
       reads=["wsum", "cf"], writes=["PS6"])
    op("vector", lambda e: e.tensor_copy(SCAL[0:1, 0:2], PS[6][0:1, 0:2]),
       reads=["PS6"], writes=["scal"])
    op("gpsimd", lambda e: e.memset(PSTG[:, :], 0.0), writes=["pstg"])
    op("vector", lambda e: e.tensor_copy(PSTG[0:1, 0:2], SCAL[0:1, 0:2]),
       reads=["scal", "pstg"], writes=["pstg"])
    op("gpsimd", lambda e: e.dma_start(out=cc_in2[:, :], in_=PSTG[0:16, :]),
       reads=["pstg"], writes=["ccin2"], dma="cc2")
    if _os.environ.get("NOCC"):
        cc2tok = op("gpsimd", lambda e: e.dma_start(out=cc_out2[0:16, :], in_=cc_in2[:, :]),
                    reads=["ccin2"], writes=["ccout2"], dma="nocc2")
    else:
        cc2tok = op("gpsimd", lambda e: e.collective_compute(
            "AllGather", ALU.bypass, replica_groups=[list(range(8))],
            ins=[cc_in2.ap()], outs=[cc_out2.ap()]),
            reads=["ccin2"], writes=["ccout2"], dma="ccol2")
    op("gpsimd", lambda e: e.dma_start(
        out=PART8[:, :], in_=cc_out2.ap().rearrange("(r p) c -> r p c", p=16)[:, 0, 0:2]),
       reads=["ccout2"], extra_deps=[cc2tok], writes=["part8"], dma="p8")
    op("tensor", lambda e: e.matmul(PS[6][0:1, 0:1], PART8[:, 0:1], ONESCF[0:8, 0:1],
                                    start=True, stop=True),
       reads=["part8", "cf"], writes=["PS6"])
    op("tensor", lambda e: e.matmul(PS[6][0:1, 1:2], PART8[:, 1:2], ONESCF[0:8, 0:1],
                                    start=False, stop=True, skip_group_check=True),
       reads=["part8", "cf"], writes=["PS6"])
    op("vector", lambda e, n=float(DFF * D): e.tensor_scalar_mul(
        SCAL[0:1, 2:4], PS[6][0:1, 0:2], 1.0 / n),
       reads=["PS6"], writes=["scal2"])
    op("vector", lambda e: e.tensor_scalar_max(SCAL[0:1, 2:4], SCAL[0:1, 2:4], 1e-8),
       reads=["scal2"], writes=["scal2"])
    op("vector", lambda e: e.reciprocal(SCAL[0:1, 2:4], SCAL[0:1, 2:4]),
       reads=["scal2"], writes=["scal2"])
    op("tensor", lambda e: e.matmul(PS[6][:, 0:2], ONESR, SCAL[0:1, 2:4],
                                    start=True, stop=True),
       reads=["scal2", "cf"], writes=["PS6"])
    op("vector", lambda e: e.tensor_copy(INVSB[:, :], PS[6][:, 0:2]),
       reads=["PS6"], writes=["invsb"])

    xqbar = B.barrier("xc") + B.barrier("xq")

    def wstream(src_ap, quant=0, piece=1024, spill_ap=None, bf16_src=False):
        cols = src_ap.shape[-1]
        spp = piece // SLOTW
        parts = []
        for c0 in range(0, cols, piece):
            w = min(piece, cols - c0)
            if (wuc[0] % NSLOT) + spp > NSLOT:
                wuc[0] += NSLOT - (wuc[0] % NSLOT)
            s = wuc[0] % NSLOT
            wuc[0] += spp
            wres = [f"wl{s + k}" for k in range(spp)]
            bres = [f"wb{s + k}" for k in range(spp)]
            wap = WLBUF[:, s * SLOTW:s * SLOTW + w]
            bap = WBBUF[:, s * SLOTW:s * SLOTW + w]
            if bf16_src:
                op("sync", lambda e, a=src_ap, c0=c0, w=w, d=bap: e.dma_start(
                    out=d, in_=a[:, c0:c0 + w]),
                   writes=bres, dma=f"w{s}")
                parts.append((s, c0, w, bres))
                continue
            op("sync", lambda e, a=src_ap, c0=c0, w=w, d=wap: e.dma_start(
                out=d, in_=a[:, c0:c0 + w]),
               writes=wres, dma=f"w{s}")
            if not quant:
                op("gpsimd", lambda e, d=bap, sv=wap: e.tensor_copy(d, sv),
                   reads=wres, writes=bres)
            else:
                op("vector", lambda e, d=wap, q=quant: e.tensor_scalar(
                    d, d, INVSB[:, q - 1:q], MAGIC, ALU.mult, ALU.add),
                   reads=wres + ["invsb"], writes=wres)
                if wuc[0] % 2 == 0:
                    op("gpsimd", lambda e, d=wap: e.tensor_scalar_max(d, d, MAGIC - 1.0),
                       reads=wres, writes=wres)
                    op("gpsimd", lambda e, d=wap: e.tensor_scalar_min(d, d, MAGIC + 1.0),
                       reads=wres, writes=wres)
                else:
                    op("vector", lambda e, d=wap: e.tensor_scalar(
                        d, d, MAGIC - 1.0, MAGIC + 1.0, ALU.max, ALU.min),
                       reads=wres, writes=wres)
                op("scalar", lambda e, d=bap, sv=wap: e.activation(
                    d, sv, AF.Identity, bias=NEGM, scale=1.0),
                   reads=wres + ["cf"], writes=bres)
                if spill_ap is not None:
                    op("gpsimd", lambda e, sp=spill_ap, c0=c0, w=w, sv=bap: e.dma_start(
                        out=sp[:, c0:c0 + w], in_=sv),
                       reads=bres, dma=f"sp{s}")
            parts.append((s, c0, w, bres))

        def tile(j, parts=parts):
            c = j * 128
            for (s, c0, w, bres) in parts:
                if c0 <= c < c0 + w:
                    return WBBUF[:, s * SLOTW + (c - c0):s * SLOTW + (c - c0) + 128], bres
            raise IndexError(c)
        return tile

    def proj(col_dst, res_dst, w_src, psb, act=AF.Silu):
        wt = wstream(w_src, piece=2048)
        for fcc in range(FC):
            wap, wres = wt(fcc)
            for mt in range(NMW):
                op("tensor", lambda e, w=wap, f=fcc, m=mt: e.matmul(
                    PS[psb + m][:, 0:MW], w,
                    XN[:, f * T + m * MW:f * T + m * MW + MW],
                    start=(f == 0), stop=(f == FC - 1)),
                   reads=wres + [f"xn{fcc}"], writes=[f"PS{psb + mt}"])
        for mt in range(NMW):
            op("scalar", lambda e, m=mt, d=col_dst: e.activation(
                POOLB[:, d + m * MW:d + (m + 1) * MW], PS[psb + m][:, 0:MW], act,
                bias=0.0, scale=1.0),
               reads=[f"PS{psb + mt}"], writes=[res_dst])

    KVSB = [SQb + pb * NB * HD for pb in range(2)]
    _skip = set(_os.environ.get("ATTN_SKIP", "").split(","))

    def stageA(h):
        pb = h % 2
        TAp, TBp, TCp, TEp, TGp, QKp = (TAb[pb], TBb[pb], TCb[pb],
                                        TEb[pb], TGb[pb], QKb[pb])
        TDp = TDb[pb]
        op("scalar", lambda e, h=h, pb=pb: e.activation(DECQ[pb], ARRQ, AF.Exp,
                                                        bias=0.0, scale=NEGSB[:, h:h + 1]),
           reads=["cb", "negsb", f"decq{pb}"], writes=[f"decq{pb}"])
        op("scalar", lambda e, h=h, pb=pb: e.activation(DECK[pb], ARRK, AF.Exp,
                                                        bias=0.0, scale=NEGSB[:, h:h + 1]),
           reads=["cb", "negsb", f"deck{pb}"], writes=[f"deck{pb}"])
        op("scalar", lambda e, h=h, pb=pb: e.activation(DIAG[pb], DIDX, AF.Exp,
                                                        bias=0.0, scale=NEGSB[:, h:h + 1]),
           reads=["cb", "negsb", f"diag{pb}"], writes=[f"diag{pb}"])
        op("vector", lambda e, pb=pb: e.tensor_mul(DIAG[pb], DIAG[pb], DMSK),
           reads=[f"diag{pb}", "cb"], writes=[f"diag{pb}"])
        op("scalar", lambda e, h=h, pb=pb: e.activation(BDP[pb], BLKI, AF.Exp,
                                                        bias=0.0, scale=NEGSB[:, h:h + 1]),
           reads=["cf", "negsb", f"bdp{pb}"], writes=[f"bdp{pb}"])
        proj(TAp, f"ta{pb}", wq_in[h, :, :], 0)
        for i in range(NB):
            op("vector", lambda e, i=i, pb=pb, TEp=TEp, TAp=TAp: e.tensor_mul(
                POOLB[:, TEp + i * BLK:TEp + (i + 1) * BLK],
                POOLB[:, TAp + i * BLK:TAp + (i + 1) * BLK], DECQ[pb]),
               reads=[f"ta{pb}", f"decq{pb}"], writes=[f"te{pb}"])
            op("vector", lambda e, i=i, h=h, pb=pb, TEp=TEp: e.tensor_scalar_mul(
                POOLB[:, QCb + h * T + i * BLK:QCb + h * T + (i + 1) * BLK],
                POOLB[:, TEp + i * BLK:TEp + (i + 1) * BLK], BDP[pb][:, i:i + 1]),
               reads=[f"te{pb}", f"bdp{pb}"],
               extra_deps=(xqbar if h < 4 else []), writes=[f"qc{h}"])
        proj(TBp, f"tb{pb}", wk_in[h, :, :], 2)
        for i in range(NB):
            for j in range(2):
                bank = 5 + j
                op("tensor", lambda e, i=i, j=j, bank=bank, TBp=TBp, TAp=TAp: e.matmul(
                    PS[bank][:, 0:BLK],
                    POOLB[:, TBp + (2 * i + j) * 128:TBp + (2 * i + j + 1) * 128],
                    POOLB[:, TAp + i * BLK:TAp + (i + 1) * BLK],
                    start=True, stop=True),
                   reads=[f"tb{pb}", f"ta{pb}"], writes=[f"PS{bank}"])
                op("vector", lambda e, i=i, j=j, bank=bank, pb=pb, QKp=QKp: e.tensor_mul(
                    POOLB[:, QKp + (2 * i + j) * BLK:QKp + (2 * i + j + 1) * BLK],
                    PS[bank][:, 0:BLK], DIAG[pb][:, j * BLK:(j + 1) * BLK]),
                   reads=[f"PS{bank}", f"diag{pb}"], writes=[f"qkd{pb}"])
        for i in range(NB):
            op("vector", lambda e, i=i, pb=pb, TCp=TCp, TBp=TBp: e.tensor_mul(
                POOLB[:, TCp + i * BLK:TCp + (i + 1) * BLK],
                POOLB[:, TBp + i * BLK:TBp + (i + 1) * BLK], DECK[pb]),
               reads=[f"tb{pb}", f"deck{pb}"], writes=[f"tc{pb}"])
        proj(TAp, f"ta{pb}", wv_in[h, :, :], 0)
        if "trans" in _skip:
            return
        for tt in range(NTC):
            bnk = PT if (tt // 2) % 2 == 0 else PS[4]
            bres = "PTb" if (tt // 2) % 2 == 0 else "PS4"
            op("tensor", lambda e, tt=tt, b=bnk, TCp=TCp: e.matmul(
                b[:, (tt % 2) * 128:(tt % 2 + 1) * 128],
                POOLB[:, TCp + tt * 128:TCp + (tt + 1) * 128], IDENT,
                start=True, stop=True, skip_group_check=True),
               reads=[f"tc{pb}", "cb"], writes=[bres], tag="ktr")
            if tt % 2 == 1:
                op("vector", lambda e, tt=tt, b=bnk, TGp=TGp: e.tensor_copy(
                    POOLB[:, TGp + (tt - 1) * 128:TGp + (tt + 1) * 128],
                    b[:, 0:256]),
                   reads=[bres], writes=[f"tg{pb}"])
        for tt in range(NTC):
            bnk = PT if (tt // 2) % 2 == 0 else PS[4]
            bres = "PTb" if (tt // 2) % 2 == 0 else "PS4"
            op("tensor", lambda e, tt=tt, b=bnk, TAp=TAp: e.matmul(
                b[:, (tt % 2) * 128:(tt % 2 + 1) * 128],
                POOLB[:, TAp + tt * 128:TAp + (tt + 1) * 128], IDENT,
                start=True, stop=True, skip_group_check=True),
               reads=[f"ta{pb}", "cb"], writes=[bres], tag="vtr")
            if tt % 2 == 1:
                op("vector", lambda e, tt=tt, b=bnk, TDp=TDp: e.tensor_copy(
                    POOLB[:, TDp + (tt - 1) * 128:TDp + (tt + 1) * 128],
                    b[:, 0:256]),
                   reads=[bres], writes=[f"td{pb}"])

    def stageC(h):
        pb = h % 2
        TDp, TGp = TDb[pb], TGb[pb]
        kb = KVSB[pb]
        for i in range(NB):
            dbk = 2 + i % 2
            for j in range(2):
                op("tensor", lambda e, i=i, j=j, TGp=TGp, TDp=TDp, dbk=dbk: e.matmul(
                    PS[dbk][:, 0:HD],
                    POOLB[:, TGp + (2 * i + j) * 128:TGp + (2 * i + j + 1) * 128],
                    POOLB[:, TDp + (2 * i + j) * 128:TDp + (2 * i + j + 1) * 128],
                    start=(j == 0), stop=(j == 1)),
                   reads=[f"tg{pb}", f"td{pb}"], writes=[f"PS{dbk}"])
            if i == 0:
                op("vector", lambda e, h=h, dbk=dbk: e.tensor_copy(
                    POOLA[:, KVc + h * HD:KVc + (h + 1) * HD], PS[dbk][:, 0:HD]),
                   reads=[f"PS{dbk}"], writes=[f"kv{h}"])
            else:
                op("vector", lambda e, h=h, dbk=dbk: e.scalar_tensor_tensor(
                    POOLA[:, KVc + h * HD:KVc + (h + 1) * HD],
                    POOLA[:, KVc + h * HD:KVc + (h + 1) * HD], BDB[:, h:h + 1],
                    PS[dbk][:, 0:HD], ALU.mult, ALU.add),
                   reads=[f"kv{h}", "bdb", f"PS{dbk}"], writes=[f"kv{h}"])
            if i < NB - 1:
                op("vector", lambda e, h=h, i=i, kb=kb: e.tensor_copy(
                    POOLB[:, kb + i * HD:kb + (i + 1) * HD],
                    POOLA[:, KVc + h * HD:KVc + (h + 1) * HD]),
                   reads=[f"kv{h}"], writes=[f"kvsb{pb}"],
                   extra_deps=(sqgbar if h < 2 else []))

    def stageD(h):
        pb = h % 2
        TDp, TEp, QKp = TDb[pb], TEb[pb], QKb[pb]
        kb = KVSB[pb]
        for i in range(NB):
            bank = 5 if i % 2 == 0 else 6
            if i > 0:
                op("tensor", lambda e, i=i, TEp=TEp, kb=kb, bank=bank: e.matmul(
                    PS[bank][:, 0:BLK], POOLB[:, kb + (i - 1) * HD:kb + i * HD],
                    POOLB[:, TEp + i * BLK:TEp + (i + 1) * BLK],
                    start=True, stop=False),
                   reads=[f"kvsb{pb}", f"te{pb}"], writes=[f"PS{bank}"])
            for j in range(2):
                op("tensor", lambda e, i=i, j=j, TDp=TDp, QKp=QKp, bank=bank: e.matmul(
                    PS[bank][:, 0:BLK],
                    POOLB[:, TDp + (2 * i + j) * 128:TDp + (2 * i + j + 1) * 128],
                    POOLB[:, QKp + (2 * i + j) * BLK:QKp + (2 * i + j + 1) * BLK],
                    start=(i == 0 and j == 0), stop=(j == 1)),
                   reads=[f"td{pb}", f"qkd{pb}"], writes=[f"PS{bank}"])
            op("vector", lambda e, h=h, i=i, bank=bank: e.tensor_copy(
                O[:, h * T + i * BLK:h * T + (i + 1) * BLK], PS[bank][:, 0:BLK]),
               reads=[f"PS{bank}"], writes=[f"o{h}"])

    sqgbar = B.barrier("xdbg")
    for h in range(NH):
        stageA(h)
        if h >= 1 and "cd" not in _skip:
            stageC(h - 1)
            stageD(h - 1)
    if "cd" not in _skip:
        stageC(NH - 1)
        stageD(NH - 1)

    if cfg.debug:
        dump_fm(dbg["d_o1"], lambda ccc, w0: O[:, ccc * T + w0: ccc * T + w0 + MW],
                NH, T, lambda ccc: f"o{ccc}", chan="dbg2")

    if upto == "attn":
        return _finish()

    def kcbar(kc):
        c0, c1_ = Hb + kc * FTS, Hb + (kc + 1) * FTS
        bars = []
        if c0 < QCb + NH * T and c1_ > QCb:
            for hh in range(max(0, (c0 - QCb) // T), min(NH, (c1_ - QCb + T - 1) // T)):
                bars += B.barrier(f"qc{hh}")
        if c0 < TR0 + 16 * T and c1_ > TR0:
            for rn in ("ta", "tb", "tc", "td", "te", "tg", "qkd"):
                bars += B.barrier(rn)
            bars += B.barrier("xdbg")
        if c0 < KVIb + CCW and c1_ > KVIb:
            bars += B.barrier("kvin")
        return bars

    def emit_f1(th, kcs):
        tw = th * FTS
        for kc in kcs:
            bank = 2 + kc % 2
            wt = wstream(w1_in[kc, :, :], quant=1, piece=512)
            for fcc in range(FC):
                wap, wres = wt(fcc)
                op("tensor", lambda e, w=wap, f=fcc, bank=bank, tw=tw: e.matmul(
                    PS[bank][:, 0:FTS], w,
                    XN[:, f * T + tw:f * T + tw + FTS],
                    start=(f == 0), stop=(f == FC - 1)),
                   reads=wres + [f"xn{fcc}"], writes=[f"PS{bank}"])
            op("scalar", lambda e, kc=kc, bank=bank: e.activation(
                POOLB[:, Hb + kc * FTS:Hb + (kc + 1) * FTS], PS[bank][:, 0:FTS], AF.Gelu,
                bias=B1C[:, kc:kc + 1], scale=1.0),
               reads=[f"PS{bank}", "b1c"], extra_deps=(kcbar(kc) if th == 0 else []),
               writes=[f"h{kc}"])

    F1SAFE = [kc for kc in range(KC)
              if (Hb + (kc + 1) * FTS <= QCb or Hb + kc * FTS >= QCb + NH * T)
              and (Hb + (kc + 1) * FTS <= KVIb or Hb + kc * FTS >= KVIb + CCW)]
    F1LATE = [kc for kc in range(KC) if kc not in F1SAFE]

    op("vector", lambda e: e.tensor_copy(POOLB[:, KVIb:KVIb + CCW], POOLA[:, KVc:KVc + CCW]),
       reads=[f"kv{h}" for h in range(NH)],
       extra_deps=B.barrier("qkd") + B.barrier("tg") + B.barrier("td"),
       writes=["kvin"])
    op("gpsimd", lambda e: e.dma_start(out=cc_in[:, :], in_=POOLB[:, KVIb:KVIb + CCW]),
       reads=["kvin"], writes=["ccin"], dma="cc")
    if _os.environ.get("NOCC"):
        cctok = op("gpsimd", lambda e: e.dma_start(out=cc_out[0:P, :], in_=cc_in[:, :]),
                   reads=["ccin"], writes=["ccout"], dma="nocc1")
    else:
        cctok = op("gpsimd", lambda e: e.collective_compute(
            "AllGather", ALU.bypass, replica_groups=[list(range(8))],
            ins=[cc_in.ap()], outs=[cc_out.ap()]),
            reads=["ccin"], writes=["ccout"], dma="ccol1")
    emit_f1(0, F1SAFE[:8])
    if upto == "cc":
        return _finish()
    if cfg.debug:
        op("vector", lambda e: e.tensor_copy(xbuf(3)[0:1, 0:2], SCAL[0:1, 2:4]),
           reads=["scal2"], writes=["xdbg"])
        op("gpsimd", lambda e: e.dma_start(out=dbg["d_invs"][:, :], in_=xbuf(3)[0:1, 0:2]),
           reads=["xdbg"], dma="dbg3")

    kvbar = B.barrier("kv")
    KPW = SLOTW * 2
    NKH = (CCW + KPW - 1) // KPW
    for r in range(8):
        for hf in range(NKH):
            w = min(KPW, CCW - hf * KPW)
            s = (r * NKH + hf) % NSLOT
            op("gpsimd", lambda e, r=r, s=s, hf=hf, w=w: e.dma_start(
                out=WL[s].bitcast(BF16)[:, 0:w],
                in_=cc_out[r * P:(r + 1) * P, hf * KPW:hf * KPW + w]),
               writes=[f"wl{s}"], extra_deps=[cctok], reads=["ccout"], dma=f"kv{s}")
            acc_dst = POOLA[:, KVc + (r % 2) * CCW + hf * KPW: KVc + (r % 2) * CCW + hf * KPW + w]
            acc_src = POOLA[:, KVc + ((r + 1) % 2) * CCW + hf * KPW: KVc + ((r + 1) % 2) * CCW + hf * KPW + w]
            if r == 0:
                op("vector", lambda e, s=s, d=acc_dst, w=w: e.tensor_scalar_mul(
                    d, WL[s].bitcast(BF16)[:, 0:w], KVSELB[:, 0:1]),
                   reads=[f"wl{s}", "kvselb"], extra_deps=kvbar,
                   writes=["acc0"])
            else:
                op("vector", lambda e, s=s, r=r, d=acc_dst, a=acc_src, w=w: e.scalar_tensor_tensor(
                    d, WL[s].bitcast(BF16)[:, 0:w], KVSELB[:, r:r + 1], a,
                    ALU.mult, ALU.add),
                   reads=[f"wl{s}", "kvselb", f"acc{(r + 1) % 2}"], writes=[f"acc{r % 2}"])
    op("vector", lambda e: e.tensor_copy(POOLB[:, KVIb:KVIb + CCW],
                                         POOLA[:, KVc + CCW:KVc + 2 * CCW]),
       reads=["acc1", "kvin"], writes=["kvin"])
    emit_f1(0, F1SAFE[8:])
    if upto == "kvin":
        return _finish()
    if cfg.debug:
        pass

    for h in range(NH):
        for mt in range(NMW):
            bank = mt
            op("tensor", lambda e, h=h, m=mt, bank=bank: e.matmul(
                PS[bank][:, 0:MW], POOLB[:, KVIb + h * HD:KVIb + (h + 1) * HD],
                POOLB[:, QCb + h * T + m * MW:QCb + h * T + (m + 1) * MW],
                start=True, stop=True),
               reads=["kvin", f"qc{h}"], writes=[f"PS{bank}"])
            op("vector", lambda e, h=h, m=mt, bank=bank: e.tensor_add(
                O[:, h * T + m * MW:h * T + (m + 1) * MW],
                O[:, h * T + m * MW:h * T + (m + 1) * MW], PS[bank][:, 0:MW]),
               reads=[f"PS{bank}", f"o{h}"], writes=[f"o{h}"])

    if upto == "corr":
        return _finish()

    for h in range(NH):
        op("scalar", lambda e, h=h: e.activation(POOLB[:, SQb:SQb + T], O[:, h * T:(h + 1) * T],
                                                 AF.Square, bias=0.0, scale=1.0),
           reads=[f"o{h}", "sqb"], writes=["sqb"],
           extra_deps=(B.barrier("kvsb") if h == 0 else []))
        for mt in range(NMW):
            op("tensor", lambda e, h=h, m=mt: e.matmul(
                PS[5 + m][0:1, 0:MW], ONESC, POOLB[:, SQb + m * MW:SQb + (m + 1) * MW],
                start=(h == 0), stop=(h == NH - 1)),
               reads=["sqb", "cb"], writes=[f"PS{5 + mt}"])
    for mt in range(NMW):
        op("vector", lambda e, m=mt: e.tensor_scalar(
            rowm(0, m), PS[5 + m][0:1, 0:MW], 1.0 / (NH * HD), 1e-6, ALU.mult, ALU.add),
           reads=[f"PS{5 + mt}"], extra_deps=B.barrier("row_"),
           writes=["rms_v"])
    rsqrt_row(0, 3, 2, "rms_v", "rms_r", "rms_t")
    if cfg.debug:
        op("vector", lambda e: e.tensor_copy(xbuf(3)[0:1, 0:min(T, MW)], row(3)[0:1, 0:min(T, MW)]),
           reads=["rms_r"], writes=["xdbg"])
        op("gpsimd", lambda e: e.dma_start(out=dbg["d_rstd"][0:1, 0:min(T, MW)], in_=xbuf(3)[0:1, 0:min(T, MW)]),
           reads=["xdbg"], dma="dbg3")

    for mt in range(NMW):
        op("tensor", lambda e, m=mt: e.matmul(PS[4 + m][:, 0:MW], ONESR, rowm(3, m),
                                              start=True, stop=True),
           reads=["rms_r", "cf"], writes=[f"PS{4 + mt}"])

    if upto == "rms":
        return _finish()

    for gc in range(NH):
        gp = gc % 2
        gb = 2 * gp
        gtc = GTb
        gsl = 2 if (gp == 0 or cfg.debug) else 3
        wt = wstream(wg_in[gc, :, :], piece=2048)
        for fcc in range(FC):
            wap, wres = wt(fcc)
            for mt in range(NMW):
                op("tensor", lambda e, w=wap, f=fcc, m=mt, gb=gb: e.matmul(
                    PS[gb + m][:, 0:MW], w,
                    XN[:, f * T + m * MW:f * T + m * MW + MW],
                    start=(f == 0), stop=(f == FC - 1)),
                   reads=wres + [f"xn{fcc}"], writes=[f"PS{gb + mt}"])
        for mt in range(NMW):
            op("scalar", lambda e, m=mt, gb=gb, gtc=gtc: e.activation(
                POOLB[:, gtc + m * MW:gtc + (m + 1) * MW], PS[gb + m][:, 0:MW], AF.Sigmoid,
                bias=0.0, scale=1.0),
               reads=[f"PS{gb + mt}"], writes=["gt"],
               extra_deps=((B.barrier("sqb") + B.barrier("kvsb")) if gc < 2 and mt == 0 else []))
            op("vector", lambda e, m=mt, gc=gc, gsl=gsl: e.tensor_mul(
                xbuf(gsl), O[:, gc * T + m * MW:gc * T + (m + 1) * MW], PS[4 + m][:, 0:MW]),
               reads=[f"o{gc}", f"PS{4 + mt}"],
               extra_deps=(B.barrier("acc") + B.barrier("xdbg") if gc < 2 and mt == 0 else []),
               writes=[f"gsc{gp}"])
            op("vector", lambda e, m=mt, gc=gc, gsl=gsl: e.tensor_scalar_mul(
                xbuf(gsl), xbuf(gsl), NWC[:, gc:gc + 1]),
               reads=[f"gsc{gp}", "nwc"], writes=[f"gsc{gp}"])
            op("vector", lambda e, m=mt, gc=gc, gsl=gsl, gtc=gtc: e.tensor_mul(
                O[:, gc * T + m * MW:gc * T + (m + 1) * MW], xbuf(gsl),
                POOLB[:, gtc + m * MW:gtc + (m + 1) * MW]),
               reads=[f"gsc{gp}", "gt"], writes=[f"o{gc}"])

    if cfg.debug:
        dump_fm(dbg["d_og"], lambda ccc, w0: O[:, ccc * T + w0: ccc * T + w0 + MW],
                NH, T, lambda ccc: f"o{ccc}", chan="dbg4")

    if upto == "mid":
        return _finish()

    NKQ = 4
    KCQ = KC // NKQ
    sbar = B.barrier("row_") + B.barrier("rms_") + B.barrier("acc")
    for th in range(2):
        tw = th * FTS
        for fcc in range(FC):
            bank = fcc % 2
            wt = wstream(wo_in[fcc, :, :], piece=2048)
            for oc in range(NH):
                wap, wres = wt(oc)
                op("tensor", lambda e, w=wap, oc=oc, bank=bank, tw=tw: e.matmul(
                    PS[bank][:, 0:FTS], w,
                    O[:, oc * T + tw:oc * T + tw + FTS],
                    start=(oc == 0), stop=(oc == NH - 1)),
                   reads=wres + [f"o{oc}"], writes=[f"PS{bank}"])
            op("vector", lambda e, f=fcc, bank=bank: e.tensor_copy(
                S_ap(f * FTS, FTS), PS[bank][:, 0:FTS]),
               reads=[f"PS{bank}"], extra_deps=(sbar if th == 0 else []),
               writes=[f"s{fcc}"])
        if cfg.debug:
            for fcc in range(FC):
                op("scalar", lambda e, f=fcc: e.activation(
                    xbuf(3)[:, 0:FTS], S_ap(f * FTS, FTS), AF.Identity,
                    bias=0.0, scale=1.0),
                   reads=[f"s{fcc}"], writes=["xdbg"])
                op("gpsimd", lambda e, f=fcc, tw=tw: e.dma_start(
                    out=dbg["d_s"][f * 128:(f + 1) * 128, tw:tw + FTS], in_=xbuf(3)[:, 0:FTS]),
                   reads=["xdbg"], dma="dbg5")
        emit_f1(th, F1LATE if th == 0 else list(range(KC)))
        for dc in range(FC):
            bank = 4 + dc % 2
            for q in range(NKQ):
                wt = wstream(w2_in[dc * NKQ + q, :, :], quant=2, piece=512)
                for kq in range(KCQ):
                    kc = q * KCQ + kq
                    wap, wres = wt(kq)
                    op("tensor", lambda e, w=wap, kc=kc, bank=bank: e.matmul(
                        PS[bank][:, 0:FTS], w,
                        POOLB[:, Hb + kc * FTS:Hb + (kc + 1) * FTS],
                        start=(kc == 0), stop=(kc == KC - 1)),
                       reads=wres + [f"h{kc}"], writes=[f"PS{bank}"])
            op("scalar", lambda e, dc=dc, bank=bank: e.activation(
                xbuf(2)[:, 0:FTS], PS[bank][:, 0:FTS], AF.Silu,
                bias=B2C[:, dc:dc + 1], scale=1.0),
               reads=[f"PS{bank}", "b2c"], writes=["gsc"])
            op("gpsimd", lambda e, dc=dc, tw=tw: e.dma_start(
                out=xbuf(0)[:, 0:FTS], in_=x_in[dc * 128:(dc + 1) * 128, tw:tw + FTS]),
               writes=["xb0"],
               extra_deps=(B.barrier("acc") if (th == 0 and dc == 0) else []),
               dma="xf")
            op("vector", lambda e, dc=dc: e.tensor_mul(
                xbuf(2)[:, 0:FTS], xbuf(2)[:, 0:FTS], S_ap(dc * FTS, FTS)),
               reads=["gsc", f"s{dc}"], writes=["gsc"])
            op("vector", lambda e, dc=dc: e.tensor_add(
                xbuf(1)[:, 0:FTS], xbuf(2)[:, 0:FTS], xbuf(0)[:, 0:FTS]),
               reads=["gsc", "xb0", "outb"], writes=["outb"])
            op("gpsimd", lambda e, dc=dc, tw=tw: e.dma_start(
                out=out_t[dc * 128:(dc + 1) * 128, tw:tw + FTS], in_=xbuf(1)[:, 0:FTS]),
               reads=["outb"], dma="out")

    fin = B.dma_sem("out")
    op("gpsimd", lambda e: e.nop(), extra_deps=[(fin, B.dcount["out"], "gpsimd", True)])

    return _finish()


def _xbuf_doc():
    pass


def prep_in_maps(cfg, L0_attn_hidden_states, L0_attn_slope_rate, L0_attn_w_qkv,
                 L0_attn_w_gate, L0_attn_w_out, L0_attn_norm_w,
                 L0_ffn_w1, L0_ffn_b1, L0_ffn_w2, L0_ffn_b2):
    P, D, T, NH, HD, DFF = cfg.P, cfg.D, cfg.T, cfg.NH, cfg.HD, cfg.DFF
    FC, KC = cfg.FC, cfg.KC
    f32 = np.float32
    x = np.ascontiguousarray(np.asarray(L0_attn_hidden_states, dtype=f32)).reshape(-1, D)
    wq_t = np.ascontiguousarray(np.asarray(L0_attn_w_qkv, dtype=f32)
                                .reshape(NH, 3, HD, FC, P)[:, 0]
                                .transpose(0, 3, 2, 1)).reshape(NH, P, FC * 128)
    wk_t = np.ascontiguousarray(np.asarray(L0_attn_w_qkv, dtype=f32)
                                .reshape(NH, 3, HD, FC, P)[:, 1]
                                .transpose(0, 3, 2, 1)).reshape(NH, P, FC * 128)
    wv_t = np.ascontiguousarray(np.asarray(L0_attn_w_qkv, dtype=f32)
                                .reshape(NH, 3, HD, FC, P)[:, 2]
                                .transpose(0, 3, 2, 1)).reshape(NH, P, FC * 128)
    wg_t = np.ascontiguousarray(np.asarray(L0_attn_w_gate, dtype=f32)
                                .reshape(FC, 128, FC, P).transpose(0, 3, 2, 1)).reshape(FC, P, FC * 128)
    wo_t = np.ascontiguousarray(np.asarray(L0_attn_w_out, dtype=f32)
                                .reshape(FC, 128, FC, P).transpose(0, 3, 2, 1)).reshape(FC, P, FC * 128)
    w1_t = np.ascontiguousarray(np.asarray(L0_ffn_w1, dtype=f32)
                                .reshape(KC, 128, FC, P).transpose(0, 3, 2, 1)).reshape(KC, P, FC * 128)
    NKQ = 4
    KCQ = KC // NKQ
    w2r = np.asarray(L0_ffn_w2, dtype=f32).reshape(FC, 128, KC, P)
    w2_t = np.ascontiguousarray(
        w2r.transpose(0, 2, 3, 1)
        .reshape(FC, NKQ, KCQ, P, 128)
        .transpose(0, 1, 3, 2, 4)
    ).reshape(FC * NKQ, P, KCQ * 128)
    b1_t = np.ascontiguousarray(np.asarray(L0_ffn_b1, dtype=f32).reshape(KC, P).T)
    b2_t = np.ascontiguousarray(np.asarray(L0_ffn_b2, dtype=f32).reshape(FC, P).T)
    nw_t = np.ascontiguousarray(np.asarray(L0_attn_norm_w, dtype=f32).reshape(FC, P).T)
    slopes = np.ascontiguousarray(np.asarray(L0_attn_slope_rate, dtype=f32).reshape(1, NH))

    n1t = max(1, (DFF * D // 8) // (128 * 2048))
    c1 = (DFF * D // 8) // (128 * n1t)
    w1f = np.asarray(L0_ffn_w1, dtype=f32).reshape(-1)
    w2f = np.asarray(L0_ffn_w2, dtype=f32).reshape(-1)

    in_maps = []
    ncore = cfg.n_cores
    for c in range(ncore):
        xs = np.ascontiguousarray(x[c * T:(c + 1) * T].T)
        sel = np.zeros((1, 8), dtype=f32)
        if c % 2 == 1:
            sel[0, c - 1] = 1.0
        sz = DFF * D // 8
        m = {
            "x_fm": xs, "wq_t": wq_t, "wk_t": wk_t, "wv_t": wv_t,
            "wg_t": wg_t, "wo_t": wo_t, "w1_t": w1_t, "w2_t": w2_t,
            "w1sl": np.ascontiguousarray(w1f[c * sz:(c + 1) * sz].reshape(n1t, P, c1)),
            "w2sl": np.ascontiguousarray(w2f[c * sz:(c + 1) * sz].reshape(n1t, P, c1)),
            "b1_t": b1_t, "b2_t": b2_t, "nw_t": nw_t, "slopes": slopes,
            "kvsel": sel,
        }
        in_maps.append(m)
    return in_maps


def kernel(**inputs):
    cfg = CFG(D=2048, T=1024, NH=16, DFF=8192)
    in_maps = prep_in_maps(cfg, **inputs)
    nc, _ = build_nc(cfg)
    res = run_bass_kernel_spmd(nc, in_maps, core_ids=list(range(cfg.n_cores)))
    outs = [np.ascontiguousarray(r["out_fm"].T) for r in res.results]
    full = np.concatenate(outs, axis=0)
    return full.reshape(4, 2048, 2048)


# version 06
# speedup vs baseline: 2.7364x; 2.7364x over previous
import os as _os
import numpy as np
import ml_dtypes
from contextlib import ExitStack
from collections import defaultdict

import concourse.bass as bass
import concourse.mybir as mybir
from concourse.bass_utils import run_bass_kernel_spmd

F32 = mybir.dt.float32
BF16 = mybir.dt.bfloat16
AF = mybir.ActivationFunctionType
ALU = mybir.AluOpType
AX = mybir.AxisListType

MAGIC = float(np.float32(1.5 * 2 ** 23))


class PB:
    ENGS = ("sync", "vector", "scalar", "tensor", "gpsimd")

    def __init__(self, nc, ctx):
        self.nc = nc
        self.ctx = ctx
        self.items = {e: [] for e in self.ENGS}
        self.esem = {e: ctx.enter_context(nc.semaphore(f"S_{e}")) for e in self.ENGS}
        self.dsem = {}
        self.ecount = {e: 0 for e in self.ENGS}
        self.dcount = defaultdict(int)
        self.res_w = {}
        self.res_r = defaultdict(list)

    def dma_sem(self, name):
        if name not in self.dsem:
            self.dsem[name] = self.ctx.enter_context(self.nc.semaphore(f"D_{name}"))
        return self.dsem[name]

    def op(self, eng, fn, reads=(), writes=(), extra_deps=(), dma=None, tag=None):
        deps = set()
        for r in reads:
            t = self.res_w.get(r)
            if t is not None:
                deps.add(t)
        for r in writes:
            t = self.res_w.get(r)
            if t is not None:
                deps.add(t)
            for t2 in self.res_r.get(r, ()):
                deps.add(t2)
        for t in extra_deps:
            if t is not None:
                deps.add(t)
        if dma is None:
            sem = self.esem[eng]
            self.ecount[eng] += 1
            tok = (sem, self.ecount[eng], eng, False)
        else:
            step = 1 if dma.startswith("ccol") else 16
            sem = self.dma_sem(dma)
            self.dcount[dma] += step
            tok = (sem, self.dcount[dma], eng, True)
        deps.discard(tok)
        self.items[eng].append((fn, sorted(deps, key=lambda t: (t[0].name, t[1])),
                               sem, (1 if (dma and dma.startswith("ccol")) else (16 if dma else 1)), tag))
        for r in reads:
            self.res_r[r].append(tok)
        for r in writes:
            self.res_w[r] = tok
            self.res_r[r] = [tok]
        return tok

    def barrier(self, prefix):
        toks = set()
        for r in set(list(self.res_w) + list(self.res_r)):
            if r.startswith(prefix):
                t = self.res_w.get(r)
                if t is not None:
                    toks.add(t)
                for t2 in self.res_r.get(r, ()):
                    toks.add(t2)
        return list(toks)

    def emit(self):
        nc = self.nc
        stats = {e: {"ops": len(self.items[e]), "waits": 0} for e in self.ENGS}
        with nc.Block() as block:
            for eng in self.ENGS:
                items = self.items[eng]
                if not items:
                    continue

                def make(eng=eng, items=items):
                    def f(proxy):
                        waited = {}
                        for (fn, deps, sem, inc, tag) in items:
                            for (dsm, val, peng, isdma) in deps:
                                if peng == "tensor" and eng == "tensor" and not isdma:
                                    continue
                                if waited.get(dsm.name, 0) >= val:
                                    continue
                                proxy.wait_ge(dsm, val)
                                waited[dsm.name] = val
                                stats[eng]["waits"] += 1
                            fn(proxy).then_inc(sem, inc)
                    return f

                getattr(block, eng)(make())
        return stats


class CFG:
    def __init__(self, D=2048, T=1024, NH=16, DFF=8192, debug=False):
        self.P = 128
        self.D, self.T, self.NH, self.HD, self.DFF = D, T, NH, 128, DFF
        self.BLK = 256
        self.FTS = T // 2
        self.n_cores = 8
        self.debug = debug
        self.FC = D // 128
        self.KC = DFF // 128
        self.NB = T // self.BLK
        self.NTC = T // 128
        self.MW = min(512, T)
        self.NMW = T // self.MW
        assert T % self.BLK == 0


def build_nc(cfg, upto='all'):
    P, D, T, NH, HD, DFF = cfg.P, cfg.D, cfg.T, cfg.NH, cfg.HD, cfg.DFF
    FC, KC, NB, BLK, NTC = cfg.FC, cfg.KC, cfg.NB, cfg.BLK, cfg.NTC
    FTS, MW, NMW = cfg.FTS, cfg.MW, cfg.NMW
    CCW = NH * HD

    nc = bass.Bass("TRN2", target_bir_lowering=False)

    x_in = nc.dram_tensor("x_fm", [D, T], F32, kind="ExternalInput")
    wq_in = nc.dram_tensor("wq_t", [NH, P, FC * 128], F32, kind="ExternalInput")
    wk_in = nc.dram_tensor("wk_t", [NH, P, FC * 128], F32, kind="ExternalInput")
    wv_in = nc.dram_tensor("wv_t", [NH, P, FC * 128], F32, kind="ExternalInput")
    wg_in = nc.dram_tensor("wg_t", [FC, P, FC * 128], F32, kind="ExternalInput")
    wo_in = nc.dram_tensor("wo_t", [FC, P, FC * 128], F32, kind="ExternalInput")
    w1_in = nc.dram_tensor("w1_t", [KC, P, FC * 128], F32, kind="ExternalInput")
    w2_in = nc.dram_tensor("w2_t", [FC * 4, P, KC * 32], F32, kind="ExternalInput")
    n1t = max(1, (DFF * D // 8) // (128 * 2048))
    c1 = (DFF * D // 8) // (128 * n1t)
    w1sl_in = nc.dram_tensor("w1sl", [n1t, P, c1], F32, kind="ExternalInput")
    w2sl_in = nc.dram_tensor("w2sl", [n1t, P, c1], F32, kind="ExternalInput")
    b1_in = nc.dram_tensor("b1_t", [P, KC], F32, kind="ExternalInput")
    b2_in = nc.dram_tensor("b2_t", [P, FC], F32, kind="ExternalInput")
    nw_in = nc.dram_tensor("nw_t", [P, FC], F32, kind="ExternalInput")
    slo_in = nc.dram_tensor("slopes", [1, NH], F32, kind="ExternalInput")
    kvsel_in = nc.dram_tensor("kvsel", [1, 8], F32, kind="ExternalInput")
    out_t = nc.dram_tensor("out_fm", [D, T], F32, kind="ExternalOutput")

    w1q_s = nc.dram_tensor("w1q_s", [KC, P, FC * 128], BF16, kind="Internal")
    w2q_s = nc.dram_tensor("w2q_s", [FC * 4, P, (KC // 4) * 128], BF16, kind="Internal")
    cc_in = nc.dram_tensor("cc_in", [P, CCW], BF16, kind="Internal")
    cc_out = nc.dram_tensor("cc_out", [8 * P, CCW], BF16, kind="Internal")
    cc_in2 = nc.dram_tensor("cc_in2", [16, 128], F32, kind="Internal")
    cc_out2 = nc.dram_tensor("cc_out2", [128, 128], F32, kind="Internal")

    arrq_np = (np.arange(BLK) + 1).astype(np.float32)
    arrk_np = (BLK - 1 - np.arange(BLK)).astype(np.float32)
    dm = np.arange(BLK)[None, :] - np.arange(P)[:, None]
    didx_np = np.concatenate([np.maximum(dm - j * P, 0) for j in range(BLK // P)], 1)
    dmsk_np = np.concatenate([(dm - j * P >= 0) for j in range(BLK // P)], 1)
    cb_np = np.concatenate([
        np.broadcast_to(arrq_np, (P, BLK)),
        np.broadcast_to(arrk_np, (P, BLK)),
        didx_np.astype(np.float32),
        dmsk_np.astype(np.float32),
        np.eye(P, dtype=np.float32),
        np.ones((P, 1), dtype=np.float32),
    ], axis=1).astype(ml_dtypes.bfloat16)
    cf_np = np.concatenate([
        np.ones((P, P), dtype=np.float32),
        np.broadcast_to((np.arange(NB) * BLK).astype(np.float32), (P, NB)),
        np.full((P, 1), float(BLK), np.float32),
        np.full((P, 1), -MAGIC, np.float32),
    ], axis=1)
    cb_t = nc.inline_tensor(cb_np, name="c_bf")
    cf_t = nc.inline_tensor(np.ascontiguousarray(cf_np), name="c_f32")

    dbg = {}
    if cfg.debug:
        for nm, shp in [("d_xn", [D, T]), ("d_o1", [NH * HD, T]), ("d_og", [NH * HD, T]),
                        ("d_s", [D, T]), ("d_cc", [P, CCW + 2]), ("d_rstd", [1, T]),
                        ("d_invs", [1, 2]), ("d_kvin", [P, CCW])]:
            dbg[nm] = nc.dram_tensor(nm, shp, F32, kind="ExternalOutput")

    ctx = ExitStack()
    SB = lambda n, s, d=F32: ctx.enter_context(nc.sbuf_tensor(n, s, d))

    XN = SB("XN", [P, FC * T], BF16)
    O = SB("O", [P, NH * T], BF16)
    NA = 2048 + max(FC * FTS // 2, 6 * T)
    POOLA = SB("POOLA", [P, NA], F32)
    KVc = 0
    Sc = 2048

    def S_ap(c0, w):
        return POOLA[:, :].bitcast(BF16)[:, 2 * Sc + c0:2 * Sc + c0 + w]
    NBC = max(KC * FTS, NH * T + 16 * T + max(T, 2 * NB * HD, NMW * MW), 2048)
    POOLB = SB("POOLB", [P, NBC], BF16)
    QCb = 0
    TR0 = NH * T
    TAb = [TR0 + pb * 8 * T for pb in range(2)]
    TBb = [TR0 + pb * 8 * T + T for pb in range(2)]
    TCb = [TR0 + pb * 8 * T + 2 * T for pb in range(2)]
    TDb = [TR0 + pb * 8 * T + 3 * T for pb in range(2)]
    TEb = [TR0 + pb * 8 * T + 4 * T for pb in range(2)]
    TGb = [TR0 + pb * 8 * T + 5 * T for pb in range(2)]
    QKb = [TR0 + pb * 8 * T + 6 * T for pb in range(2)]
    SQb = TR0 + 16 * T
    GTb = SQb
    KVIb = QKb[0]
    XQb = 0
    Hb = 0
    ENDB = TR0 + 16 * T + max(T, 2 * NB * HD, NMW * MW)
    assert ENDB <= NBC, ENDB

    NWC = SB("NWC", [P, FC], F32)
    B1C = SB("B1C", [P, KC], F32)
    B2C = SB("B2C", [P, FC], F32)
    SLO = SB("SLO", [1, NH], F32)
    NEGR = SB("NEGR", [1, NH], F32)
    NEGSB = SB("NEGSB", [P, NH], F32)
    BDB = SB("BDB", [P, NH], F32)
    KVSEL = SB("KVSEL", [1, 8], F32)
    KVSELB = SB("KVSELB", [P, 8], F32)
    SCAL = SB("SCAL", [1, 4], F32)
    PART8 = SB("PART8", [8, 2], F32)
    INVSB = SB("INVSB", [P, 2], F32)
    WSUM = SB("WSUM", [P, 4], F32)
    KVBF = SB("KVBF", [P, HD], BF16)
    PSTG = SB("PSTG", [P, 128], F32)
    NSLOT = 10
    SLOTW = 512
    WLBUF = SB("WLBUF", [P, NSLOT * SLOTW], F32)
    WBBUF = SB("WBBUF", [P, NSLOT * SLOTW], BF16)
    WL = [WLBUF[:, i * SLOTW:(i + 1) * SLOTW] for i in range(NSLOT)]
    WB = [WBBUF[:, i * SLOTW:(i + 1) * SLOTW] for i in range(NSLOT)]
    CONSTB = SB("CONSTB", [P, 6 * BLK + P + 1], BF16)
    CONSTF = SB("CONSTF", [P, P + NB + 2], F32)
    ARRQ = CONSTB[:, 0:BLK]
    ARRK = CONSTB[:, BLK:2 * BLK]
    DIDX = CONSTB[:, 2 * BLK:4 * BLK]
    DMSK = CONSTB[:, 4 * BLK:6 * BLK]
    IDENT = CONSTB[:, 6 * BLK:6 * BLK + P]
    ONESC = CONSTB[:, 6 * BLK + P:6 * BLK + P + 1]
    ONESR = CONSTF[0:1, 0:P]
    ONESCF = CONSTF[:, 0:1]
    BLKI = CONSTF[:, P:P + NB]
    BLKCOL = CONSTF[:, P + NB:P + NB + 1]
    NEGM = CONSTF[:, P + NB + 1:P + NB + 2]
    DECQT = SB("DECQT", [P, 2 * BLK], BF16)
    DECKT = SB("DECKT", [P, 2 * BLK], BF16)
    DIAGT = SB("DIAGT", [P, 4 * BLK], BF16)
    BDPT = SB("BDPT", [P, 2 * NB], F32)
    DECQ = [DECQT[:, i * BLK:(i + 1) * BLK] for i in range(2)]
    DECK = [DECKT[:, i * BLK:(i + 1) * BLK] for i in range(2)]
    DIAG = [DIAGT[:, i * 2 * BLK:(i + 1) * 2 * BLK] for i in range(2)]
    BDP = [BDPT[:, i * NB:(i + 1) * NB] for i in range(2)]

    PS = [ctx.enter_context(nc.psum_tensor(f"PS{i}", [P, 512], F32)) for i in range(7)]
    PT = ctx.enter_context(nc.psum_tensor("PT", [P, 512], F32))

    B = PB(nc, ctx)
    op = B.op

    def row(k):
        return POOLA[0:1, Sc + k * T: Sc + (k + 1) * T]

    def rowm(k, mt):
        return POOLA[0:1, Sc + k * T + mt * MW: Sc + k * T + mt * MW + MW]

    def _finish():
        toks = []
        for ch, cnt in list(B.dcount.items()):
            if cnt:
                toks.append((B.dsem[ch], cnt, "sync", True))
        for eng in B.ENGS:
            if B.ecount[eng] and eng != "gpsimd":
                toks.append((B.esem[eng], B.ecount[eng], eng, False))
        op("gpsimd", lambda e: e.nop(), extra_deps=toks)
        stats = B.emit()
        ctx.close()
        return nc, stats

    op("gpsimd", lambda e: e.dma_start(out=CONSTB[:, :], in_=cb_t[:, :]),
       writes=["cb"], dma="ccb")
    op("gpsimd", lambda e: e.dma_start(out=CONSTF[:, :], in_=cf_t[:, :]),
       writes=["cf"], dma="ccf")
    for (dst, srct, res) in [(NWC, nw_in, "nwc"), (B1C, b1_in, "b1c"), (B2C, b2_in, "b2c"),
                             (SLO, slo_in, "slo"), (KVSEL, kvsel_in, "kvsel")]:
        op("gpsimd", lambda e, d=dst, s=srct: e.dma_start(out=d[:, :], in_=s[:, :]),
           writes=[res], dma=f"c_{res}")

    op("vector", lambda e: e.tensor_scalar_mul(NEGR[:, :], SLO[:, :], -1.0),
       reads=["slo"], writes=["negr"])
    for h in range(NH):
        op("tensor", lambda e, h=h: e.matmul(PS[5][:, 0:1], ONESR,
                                             NEGR[0:1, h:h + 1], start=True, stop=True),
           reads=["negr", "cf"], writes=["PS5"])
        op("vector", lambda e, h=h: e.tensor_copy(NEGSB[:, h:h + 1], PS[5][:, 0:1]),
           reads=["PS5"], writes=["negsb"])
    op("tensor", lambda e: e.matmul(PS[5][:, 0:8], ONESR, KVSEL[0:1, :],
                                    start=True, stop=True),
       reads=["kvsel", "cf"], writes=["PS5"])
    op("vector", lambda e: e.tensor_copy(KVSELB[:, :], PS[5][:, 0:8]),
       reads=["PS5"], writes=["kvselb"])
    for h in range(NH):
        op("scalar", lambda e, h=h: e.activation(BDB[:, h:h + 1], BLKCOL, AF.Exp,
                                                 bias=0.0, scale=NEGSB[:, h:h + 1]),
           reads=["negsb", "cf", "bdb"], writes=["bdb"])

    XQs = [XQb + i * 2 * T for i in range(2)]

    def xq_ap(bi, half):
        c = XQs[bi] + half * T
        return POOLB[:, c:c + T]

    def xbuf(bi):
        return POOLA[:, KVc + bi * MW: KVc + (bi + 1) * MW]

    def xrow(bi):
        return POOLA[:, KVc + bi * T: KVc + (bi + 1) * T]

    for fcc in range(FC):
        bi = fcc % 2
        op("gpsimd", lambda e, b=bi, f=fcc: e.dma_start(
            out=xrow(b), in_=x_in[f * 128:(f + 1) * 128, :]),
           writes=[f"xr{bi}"], dma=f"x{bi}")
        op("vector", lambda e, b=bi: e.tensor_copy(xq_ap(b, 0), xrow(b)),
           reads=[f"xr{bi}"], writes=[f"xc{bi}"])
        op("scalar", lambda e, b=bi: e.activation(xq_ap(b, 1), xrow(b), AF.Square,
                                                  bias=0.0, scale=1.0),
           reads=[f"xr{bi}"], writes=[f"xq{bi}"])
        for mt in range(NMW):
            op("tensor", lambda e, b=bi, f=fcc, m=mt: e.matmul(
                PS[0 + m][0:1, 0:MW], ONESC, xq_ap(b, 0)[:, m * MW:(m + 1) * MW],
                start=(f == 0), stop=(f == FC - 1)),
               reads=[f"xc{bi}", "cb"], writes=[f"PS{0 + mt}"])
            op("tensor", lambda e, b=bi, f=fcc, m=mt: e.matmul(
                PS[2 + m][0:1, 0:MW], ONESC, xq_ap(b, 1)[:, m * MW:(m + 1) * MW],
                start=(f == 0), stop=(f == FC - 1)),
               reads=[f"xq{bi}", "cb"], writes=[f"PS{2 + mt}"])
    for mt in range(NMW):
        op("vector", lambda e, m=mt: e.tensor_scalar_mul(rowm(0, m), PS[0 + m][0:1, 0:MW], 1.0 / D),
           reads=[f"PS{0 + mt}"], writes=["row_mean"])
        op("vector", lambda e, m=mt: e.tensor_scalar_mul(rowm(1, m), PS[2 + m][0:1, 0:MW], 1.0 / D),
           reads=[f"PS{2 + mt}"], writes=["row_msq"])
    op("vector", lambda e: e.tensor_mul(row(2), row(0), row(0)),
       reads=["row_mean"], writes=["row_tmp"])
    op("vector", lambda e: e.tensor_sub(row(1), row(1), row(2)),
       reads=["row_msq", "row_tmp"], writes=["row_msq"])
    op("vector", lambda e: e.tensor_scalar_add(row(1), row(1), 1e-5),
       reads=["row_msq"], writes=["row_msq"])

    def rsqrt_row(src_k, dst_k, tmp_k, src_res, dst_res, tmp_res):
        op("scalar", lambda e: e.activation(row(tmp_k), row(src_k), AF.Sqrt,
                                            bias=0.0, scale=1.0),
           reads=[src_res], writes=[tmp_res])
        op("vector", lambda e: e.reciprocal(row(dst_k), row(tmp_k)),
           reads=[tmp_res], writes=[dst_res])
        op("vector", lambda e: e.tensor_mul(row(tmp_k), row(dst_k), row(dst_k)),
           reads=[dst_res], writes=[tmp_res])
        op("vector", lambda e: e.tensor_mul(row(tmp_k), row(tmp_k), row(src_k)),
           reads=[tmp_res, src_res], writes=[tmp_res])
        op("vector", lambda e: e.tensor_scalar(row(tmp_k), row(tmp_k), -0.5, 1.5,
                                               ALU.mult, ALU.add),
           reads=[tmp_res], writes=[tmp_res])
        op("vector", lambda e: e.tensor_mul(row(dst_k), row(dst_k), row(tmp_k)),
           reads=[dst_res, tmp_res], writes=[dst_res])

    rsqrt_row(1, 3, 2, "row_msq", "row_arow", "row_tmp")
    op("vector", lambda e: e.scalar_tensor_tensor(row(4), row(0), -1.0, row(3),
                                                  ALU.mult, ALU.mult),
       reads=["row_mean", "row_arow"], writes=["row_brow"])
    for mt in range(NMW):
        op("tensor", lambda e, m=mt: e.matmul(PS[4 + m][:, 0:MW], ONESR, rowm(3, m),
                                              start=True, stop=True),
           reads=["row_arow", "cf"], writes=[f"PS{4 + mt}"])
        op("tensor", lambda e, m=mt: e.matmul(PS[0 + m][:, 0:MW], ONESR, rowm(4, m),
                                              start=True, stop=True),
           reads=["row_brow", "cf"], writes=[f"PS{0 + mt}"])
    for fcc in range(FC):
        bi = fcc % 2
        op("gpsimd", lambda e, b=bi, f=fcc: e.dma_start(
            out=xrow(b), in_=x_in[f * 128:(f + 1) * 128, :]),
           writes=[f"xr{bi}"], dma=f"x{bi}")
        for mt in range(NMW):
            op("vector", lambda e, b=bi, m=mt: e.tensor_mul(
                xrow(b)[:, m * MW:(m + 1) * MW], xrow(b)[:, m * MW:(m + 1) * MW],
                PS[4 + m][:, 0:MW]),
               reads=[f"xr{bi}", f"PS{4 + mt}"], writes=[f"xr{bi}"])
            op("vector", lambda e, b=bi, f=fcc, m=mt: e.tensor_add(
                XN[:, f * T + m * MW: f * T + (m + 1) * MW],
                xrow(b)[:, m * MW:(m + 1) * MW], PS[0 + m][:, 0:MW]),
               reads=[f"xr{bi}", f"PS{0 + mt}"], writes=[f"xn{fcc}"])

    if cfg.debug:
        def dump_fm(dramt, src_ap_fn, nchunk, width, res_of, chan="dbg"):
            for ccc in range(nchunk):
                for w0 in range(0, width, MW):
                    op("scalar", lambda e, ccc=ccc, w0=w0: e.activation(
                        xbuf(3), src_ap_fn(ccc, w0), AF.Identity, bias=0.0, scale=1.0),
                       reads=[res_of(ccc)], writes=["xdbg"])
                    op("gpsimd", lambda e, ccc=ccc, w0=w0, d=dramt: e.dma_start(
                        out=d[ccc * 128:(ccc + 1) * 128, w0:w0 + MW], in_=xbuf(3)),
                       reads=["xdbg"], dma=chan)

        dump_fm(dbg["d_xn"], lambda ccc, w0: XN[:, ccc * T + w0: ccc * T + w0 + MW],
                FC, T, lambda ccc: f"xn{ccc}")

    if upto == "p0":
        return _finish()

    wuc = [0]
    op("gpsimd", lambda e: e.memset(WSUM[:, :], 0.0), writes=["wsum", "wsumt"])
    for wi, wsl in enumerate([w1sl_in, w2sl_in]):
        for t in range(n1t):
            for c0 in range(0, c1, SLOTW):
                w = min(SLOTW, c1 - c0)
                s = wuc[0] % NSLOT
                wuc[0] += 1
                op("scalar", lambda e, s=s, t=t, wt=wsl, c0=c0, w=w: e.dma_start(
                    out=WL[s][:, 0:w], in_=wt[t, :, c0:c0 + w]),
                   writes=[f"wl{s}"], dma=f"wsl{s}")
                op("vector", lambda e, s=s, wi=wi, w=w: e.tensor_reduce(
                    WSUM[:, 2 + wi:3 + wi], WL[s][:, 0:w], AX.X, ALU.add,
                    apply_absolute_value=True),
                   reads=[f"wl{s}"], writes=["wsumt"])
                op("vector", lambda e, wi=wi: e.tensor_add(
                    WSUM[:, wi:wi + 1], WSUM[:, wi:wi + 1], WSUM[:, 2 + wi:3 + wi]),
                   reads=["wsum", "wsumt"], writes=["wsum"])
    op("tensor", lambda e: e.matmul(PS[6][0:1, 0:1], WSUM[:, 0:1], ONESCF,
                                    start=True, stop=True),
       reads=["wsum", "cf"], writes=["PS6"])
    op("tensor", lambda e: e.matmul(PS[6][0:1, 1:2], WSUM[:, 1:2], ONESCF,
                                    start=False, stop=True, skip_group_check=True),
       reads=["wsum", "cf"], writes=["PS6"])
    op("vector", lambda e: e.tensor_copy(SCAL[0:1, 0:2], PS[6][0:1, 0:2]),
       reads=["PS6"], writes=["scal"])
    op("gpsimd", lambda e: e.memset(PSTG[:, :], 0.0), writes=["pstg"])
    op("vector", lambda e: e.tensor_copy(PSTG[0:1, 0:2], SCAL[0:1, 0:2]),
       reads=["scal", "pstg"], writes=["pstg"])
    op("gpsimd", lambda e: e.dma_start(out=cc_in2[:, :], in_=PSTG[0:16, :]),
       reads=["pstg"], writes=["ccin2"], dma="cc2")
    if _os.environ.get("NOCC"):
        cc2tok = op("gpsimd", lambda e: e.dma_start(out=cc_out2[0:16, :], in_=cc_in2[:, :]),
                    reads=["ccin2"], writes=["ccout2"], dma="nocc2")
    else:
        cc2tok = op("gpsimd", lambda e: e.collective_compute(
            "AllGather", ALU.bypass, replica_groups=[list(range(8))],
            ins=[cc_in2.ap()], outs=[cc_out2.ap()]),
            reads=["ccin2"], writes=["ccout2"], dma="ccol2")
    op("gpsimd", lambda e: e.dma_start(
        out=PART8[:, :], in_=cc_out2.ap().rearrange("(r p) c -> r p c", p=16)[:, 0, 0:2]),
       reads=["ccout2"], extra_deps=[cc2tok], writes=["part8"], dma="p8")
    op("tensor", lambda e: e.matmul(PS[6][0:1, 0:1], PART8[:, 0:1], ONESCF[0:8, 0:1],
                                    start=True, stop=True),
       reads=["part8", "cf"], writes=["PS6"])
    op("tensor", lambda e: e.matmul(PS[6][0:1, 1:2], PART8[:, 1:2], ONESCF[0:8, 0:1],
                                    start=False, stop=True, skip_group_check=True),
       reads=["part8", "cf"], writes=["PS6"])
    op("vector", lambda e, n=float(DFF * D): e.tensor_scalar_mul(
        SCAL[0:1, 2:4], PS[6][0:1, 0:2], 1.0 / n),
       reads=["PS6"], writes=["scal2"])
    op("vector", lambda e: e.tensor_scalar_max(SCAL[0:1, 2:4], SCAL[0:1, 2:4], 1e-8),
       reads=["scal2"], writes=["scal2"])
    op("vector", lambda e: e.reciprocal(SCAL[0:1, 2:4], SCAL[0:1, 2:4]),
       reads=["scal2"], writes=["scal2"])
    op("tensor", lambda e: e.matmul(PS[6][:, 0:2], ONESR, SCAL[0:1, 2:4],
                                    start=True, stop=True),
       reads=["scal2", "cf"], writes=["PS6"])
    op("vector", lambda e: e.tensor_copy(INVSB[:, :], PS[6][:, 0:2]),
       reads=["PS6"], writes=["invsb"])

    xqbar = B.barrier("xc") + B.barrier("xq")

    def wstream(src_ap, quant=0, piece=1024, spill_ap=None, bf16_src=False):
        cols = src_ap.shape[-1]
        spp = piece // SLOTW
        parts = []
        for c0 in range(0, cols, piece):
            w = min(piece, cols - c0)
            if (wuc[0] % NSLOT) + spp > NSLOT:
                wuc[0] += NSLOT - (wuc[0] % NSLOT)
            s = wuc[0] % NSLOT
            wuc[0] += spp
            wres = [f"wl{s + k}" for k in range(spp)]
            bres = [f"wb{s + k}" for k in range(spp)]
            wap = WLBUF[:, s * SLOTW:s * SLOTW + w]
            bap = WBBUF[:, s * SLOTW:s * SLOTW + w]
            if bf16_src:
                op("sync", lambda e, a=src_ap, c0=c0, w=w, d=bap: e.dma_start(
                    out=d, in_=a[:, c0:c0 + w]),
                   writes=bres, dma=f"w{s}")
                parts.append((s, c0, w, bres))
                continue
            op("sync", lambda e, a=src_ap, c0=c0, w=w, d=wap: e.dma_start(
                out=d, in_=a[:, c0:c0 + w]),
               writes=wres, dma=f"w{s}")
            if not quant:
                op("scalar", lambda e, d=bap, sv=wap: e.activation(
                    d, sv, AF.Identity, bias=0.0, scale=1.0),
                   reads=wres, writes=bres)
            else:
                op("scalar", lambda e, d=wap, q=quant: e.activation(
                    d, d, AF.Identity, bias=0.0, scale=INVSB[:, q - 1:q]),
                   reads=wres + ["invsb"], writes=wres)
                op("vector", lambda e, d=wap: e.tensor_scalar(
                    d, d, -1.25, 1.25, ALU.max, ALU.min),
                   reads=wres, writes=wres)
                op("vector", lambda e, d=bap, sv=wap: e.tensor_scalar(
                    d, sv, MAGIC, MAGIC, ALU.add, ALU.subtract),
                   reads=wres, writes=bres)
                if spill_ap is not None:
                    op("gpsimd", lambda e, sp=spill_ap, c0=c0, w=w, sv=bap: e.dma_start(
                        out=sp[:, c0:c0 + w], in_=sv),
                       reads=bres, dma=f"sp{s}")
            parts.append((s, c0, w, bres))

        def tile(j, parts=parts):
            c = j * 128
            for (s, c0, w, bres) in parts:
                if c0 <= c < c0 + w:
                    return WBBUF[:, s * SLOTW + (c - c0):s * SLOTW + (c - c0) + 128], bres
            raise IndexError(c)
        return tile

    def proj(col_dst, res_dst, w_src, psb, act=AF.Silu):
        wt = wstream(w_src, piece=2048)
        for fcc in range(FC):
            wap, wres = wt(fcc)
            for mt in range(NMW):
                op("tensor", lambda e, w=wap, f=fcc, m=mt: e.matmul(
                    PS[psb + m][:, 0:MW], w,
                    XN[:, f * T + m * MW:f * T + m * MW + MW],
                    start=(f == 0), stop=(f == FC - 1)),
                   reads=wres + [f"xn{fcc}"], writes=[f"PS{psb + mt}"])
        for mt in range(NMW):
            op("scalar", lambda e, m=mt, d=col_dst: e.activation(
                POOLB[:, d + m * MW:d + (m + 1) * MW], PS[psb + m][:, 0:MW], act,
                bias=0.0, scale=1.0),
               reads=[f"PS{psb + mt}"], writes=[res_dst])

    KVSB = [SQb + pb * NB * HD for pb in range(2)]
    _skip = set(_os.environ.get("ATTN_SKIP", "").split(","))

    def stageA(h):
        pb = h % 2
        TAp, TBp, TCp, TEp, TGp, QKp = (TAb[pb], TBb[pb], TCb[pb],
                                        TEb[pb], TGb[pb], QKb[pb])
        TDp = TDb[pb]
        op("scalar", lambda e, h=h, pb=pb: e.activation(DECQ[pb], ARRQ, AF.Exp,
                                                        bias=0.0, scale=NEGSB[:, h:h + 1]),
           reads=["cb", "negsb", f"decq{pb}"], writes=[f"decq{pb}"])
        op("scalar", lambda e, h=h, pb=pb: e.activation(DECK[pb], ARRK, AF.Exp,
                                                        bias=0.0, scale=NEGSB[:, h:h + 1]),
           reads=["cb", "negsb", f"deck{pb}"], writes=[f"deck{pb}"])
        op("scalar", lambda e, h=h, pb=pb: e.activation(DIAG[pb], DIDX, AF.Exp,
                                                        bias=0.0, scale=NEGSB[:, h:h + 1]),
           reads=["cb", "negsb", f"diag{pb}"], writes=[f"diag{pb}"])
        op("vector", lambda e, pb=pb: e.tensor_mul(DIAG[pb], DIAG[pb], DMSK),
           reads=[f"diag{pb}", "cb"], writes=[f"diag{pb}"])
        op("scalar", lambda e, h=h, pb=pb: e.activation(BDP[pb], BLKI, AF.Exp,
                                                        bias=0.0, scale=NEGSB[:, h:h + 1]),
           reads=["cf", "negsb", f"bdp{pb}"], writes=[f"bdp{pb}"])
        proj(TAp, f"ta{pb}", wq_in[h, :, :], 0)
        for i in range(NB):
            op("vector", lambda e, i=i, pb=pb, TEp=TEp, TAp=TAp: e.tensor_mul(
                POOLB[:, TEp + i * BLK:TEp + (i + 1) * BLK],
                POOLB[:, TAp + i * BLK:TAp + (i + 1) * BLK], DECQ[pb]),
               reads=[f"ta{pb}", f"decq{pb}"], writes=[f"te{pb}"])
            op("vector", lambda e, i=i, h=h, pb=pb, TEp=TEp: e.tensor_scalar_mul(
                POOLB[:, QCb + h * T + i * BLK:QCb + h * T + (i + 1) * BLK],
                POOLB[:, TEp + i * BLK:TEp + (i + 1) * BLK], BDP[pb][:, i:i + 1]),
               reads=[f"te{pb}", f"bdp{pb}"],
               extra_deps=(xqbar if h < 4 else []), writes=[f"qc{h}"])
        proj(TBp, f"tb{pb}", wk_in[h, :, :], 2)
        for i in range(NB):
            for j in range(2):
                bank = 5 + j
                op("tensor", lambda e, i=i, j=j, bank=bank, TBp=TBp, TAp=TAp: e.matmul(
                    PS[bank][:, 0:BLK],
                    POOLB[:, TBp + (2 * i + j) * 128:TBp + (2 * i + j + 1) * 128],
                    POOLB[:, TAp + i * BLK:TAp + (i + 1) * BLK],
                    start=True, stop=True),
                   reads=[f"tb{pb}", f"ta{pb}"], writes=[f"PS{bank}"])
                op("vector", lambda e, i=i, j=j, bank=bank, pb=pb, QKp=QKp: e.tensor_mul(
                    POOLB[:, QKp + (2 * i + j) * BLK:QKp + (2 * i + j + 1) * BLK],
                    PS[bank][:, 0:BLK], DIAG[pb][:, j * BLK:(j + 1) * BLK]),
                   reads=[f"PS{bank}", f"diag{pb}"], writes=[f"qkd{pb}"])
        for i in range(NB):
            op("vector", lambda e, i=i, pb=pb, TCp=TCp, TBp=TBp: e.tensor_mul(
                POOLB[:, TCp + i * BLK:TCp + (i + 1) * BLK],
                POOLB[:, TBp + i * BLK:TBp + (i + 1) * BLK], DECK[pb]),
               reads=[f"tb{pb}", f"deck{pb}"], writes=[f"tc{pb}"])
        proj(TAp, f"ta{pb}", wv_in[h, :, :], 0)
        if "trans" in _skip:
            return
        for tt in range(NTC):
            bnk = PT if (tt // 2) % 2 == 0 else PS[4]
            bres = "PTb" if (tt // 2) % 2 == 0 else "PS4"
            op("tensor", lambda e, tt=tt, b=bnk, TCp=TCp: e.matmul(
                b[:, (tt % 2) * 128:(tt % 2 + 1) * 128],
                POOLB[:, TCp + tt * 128:TCp + (tt + 1) * 128], IDENT,
                start=True, stop=True, skip_group_check=True),
               reads=[f"tc{pb}", "cb"], writes=[bres], tag="ktr")
            if tt % 2 == 1:
                op("vector", lambda e, tt=tt, b=bnk, TGp=TGp: e.tensor_copy(
                    POOLB[:, TGp + (tt - 1) * 128:TGp + (tt + 1) * 128],
                    b[:, 0:256]),
                   reads=[bres], writes=[f"tg{pb}"])
        for tt in range(NTC):
            bnk = PT if (tt // 2) % 2 == 0 else PS[4]
            bres = "PTb" if (tt // 2) % 2 == 0 else "PS4"
            op("tensor", lambda e, tt=tt, b=bnk, TAp=TAp: e.matmul(
                b[:, (tt % 2) * 128:(tt % 2 + 1) * 128],
                POOLB[:, TAp + tt * 128:TAp + (tt + 1) * 128], IDENT,
                start=True, stop=True, skip_group_check=True),
               reads=[f"ta{pb}", "cb"], writes=[bres], tag="vtr")
            if tt % 2 == 1:
                op("vector", lambda e, tt=tt, b=bnk, TDp=TDp: e.tensor_copy(
                    POOLB[:, TDp + (tt - 1) * 128:TDp + (tt + 1) * 128],
                    b[:, 0:256]),
                   reads=[bres], writes=[f"td{pb}"])

    def stageC(h):
        pb = h % 2
        TDp, TGp = TDb[pb], TGb[pb]
        kb = KVSB[pb]
        for i in range(NB):
            dbk = 2 + i % 2
            for j in range(2):
                op("tensor", lambda e, i=i, j=j, TGp=TGp, TDp=TDp, dbk=dbk: e.matmul(
                    PS[dbk][:, 0:HD],
                    POOLB[:, TGp + (2 * i + j) * 128:TGp + (2 * i + j + 1) * 128],
                    POOLB[:, TDp + (2 * i + j) * 128:TDp + (2 * i + j + 1) * 128],
                    start=(j == 0), stop=(j == 1)),
                   reads=[f"tg{pb}", f"td{pb}"], writes=[f"PS{dbk}"])
            if i == 0:
                op("vector", lambda e, h=h, dbk=dbk: e.tensor_copy(
                    POOLA[:, KVc + h * HD:KVc + (h + 1) * HD], PS[dbk][:, 0:HD]),
                   reads=[f"PS{dbk}"], writes=[f"kv{h}"])
            else:
                op("vector", lambda e, h=h, dbk=dbk: e.scalar_tensor_tensor(
                    POOLA[:, KVc + h * HD:KVc + (h + 1) * HD],
                    POOLA[:, KVc + h * HD:KVc + (h + 1) * HD], BDB[:, h:h + 1],
                    PS[dbk][:, 0:HD], ALU.mult, ALU.add),
                   reads=[f"kv{h}", "bdb", f"PS{dbk}"], writes=[f"kv{h}"])
            if i < NB - 1:
                op("vector", lambda e, h=h, i=i, kb=kb: e.tensor_copy(
                    POOLB[:, kb + i * HD:kb + (i + 1) * HD],
                    POOLA[:, KVc + h * HD:KVc + (h + 1) * HD]),
                   reads=[f"kv{h}"], writes=[f"kvsb{pb}"],
                   extra_deps=(sqgbar if h < 2 else []))

    def stageD(h):
        pb = h % 2
        TDp, TEp, QKp = TDb[pb], TEb[pb], QKb[pb]
        kb = KVSB[pb]
        for i in range(NB):
            bank = 5 if i % 2 == 0 else 6
            if i > 0:
                op("tensor", lambda e, i=i, TEp=TEp, kb=kb, bank=bank: e.matmul(
                    PS[bank][:, 0:BLK], POOLB[:, kb + (i - 1) * HD:kb + i * HD],
                    POOLB[:, TEp + i * BLK:TEp + (i + 1) * BLK],
                    start=True, stop=False),
                   reads=[f"kvsb{pb}", f"te{pb}"], writes=[f"PS{bank}"])
            for j in range(2):
                op("tensor", lambda e, i=i, j=j, TDp=TDp, QKp=QKp, bank=bank: e.matmul(
                    PS[bank][:, 0:BLK],
                    POOLB[:, TDp + (2 * i + j) * 128:TDp + (2 * i + j + 1) * 128],
                    POOLB[:, QKp + (2 * i + j) * BLK:QKp + (2 * i + j + 1) * BLK],
                    start=(i == 0 and j == 0), stop=(j == 1)),
                   reads=[f"td{pb}", f"qkd{pb}"], writes=[f"PS{bank}"])
            op("vector", lambda e, h=h, i=i, bank=bank: e.tensor_copy(
                O[:, h * T + i * BLK:h * T + (i + 1) * BLK], PS[bank][:, 0:BLK]),
               reads=[f"PS{bank}"], writes=[f"o{h}"])

    sqgbar = B.barrier("xdbg")
    for h in range(NH):
        stageA(h)
        if h >= 1 and "cd" not in _skip:
            stageC(h - 1)
            stageD(h - 1)
    if "cd" not in _skip:
        stageC(NH - 1)
        stageD(NH - 1)

    if cfg.debug:
        dump_fm(dbg["d_o1"], lambda ccc, w0: O[:, ccc * T + w0: ccc * T + w0 + MW],
                NH, T, lambda ccc: f"o{ccc}", chan="dbg2")

    if upto == "attn":
        return _finish()

    def kcbar(kc):
        c0, c1_ = Hb + kc * FTS, Hb + (kc + 1) * FTS
        bars = []
        if c0 < QCb + NH * T and c1_ > QCb:
            for hh in range(max(0, (c0 - QCb) // T), min(NH, (c1_ - QCb + T - 1) // T)):
                bars += B.barrier(f"qc{hh}")
        if c0 < TR0 + 16 * T and c1_ > TR0:
            for rn in ("ta", "tb", "tc", "td", "te", "tg", "qkd"):
                bars += B.barrier(rn)
            bars += B.barrier("xdbg")
        if c0 < KVIb + CCW and c1_ > KVIb:
            bars += B.barrier("kvin")
        return bars

    def emit_f1(th, kcs):
        tw = th * FTS
        for kc in kcs:
            bank = 2 + kc % 2
            wt = wstream(w1_in[kc, :, :], quant=1, piece=512)
            for fcc in range(FC):
                wap, wres = wt(fcc)
                op("tensor", lambda e, w=wap, f=fcc, bank=bank, tw=tw: e.matmul(
                    PS[bank][:, 0:FTS], w,
                    XN[:, f * T + tw:f * T + tw + FTS],
                    start=(f == 0), stop=(f == FC - 1)),
                   reads=wres + [f"xn{fcc}"], writes=[f"PS{bank}"])
            op("scalar", lambda e, kc=kc, bank=bank: e.activation(
                POOLB[:, Hb + kc * FTS:Hb + (kc + 1) * FTS], PS[bank][:, 0:FTS], AF.Gelu,
                bias=B1C[:, kc:kc + 1], scale=1.0),
               reads=[f"PS{bank}", "b1c"], extra_deps=(kcbar(kc) if th == 0 else []),
               writes=[f"h{kc}"])

    F1SAFE = [kc for kc in range(KC)
              if (Hb + (kc + 1) * FTS <= QCb or Hb + kc * FTS >= QCb + NH * T)
              and (Hb + (kc + 1) * FTS <= KVIb or Hb + kc * FTS >= KVIb + CCW)]
    F1LATE = [kc for kc in range(KC) if kc not in F1SAFE]

    op("vector", lambda e: e.tensor_copy(POOLB[:, KVIb:KVIb + CCW], POOLA[:, KVc:KVc + CCW]),
       reads=[f"kv{h}" for h in range(NH)],
       extra_deps=B.barrier("qkd") + B.barrier("tg") + B.barrier("td"),
       writes=["kvin"])
    op("gpsimd", lambda e: e.dma_start(out=cc_in[:, :], in_=POOLB[:, KVIb:KVIb + CCW]),
       reads=["kvin"], writes=["ccin"], dma="cc")
    if _os.environ.get("NOCC"):
        cctok = op("gpsimd", lambda e: e.dma_start(out=cc_out[0:P, :], in_=cc_in[:, :]),
                   reads=["ccin"], writes=["ccout"], dma="nocc1")
    else:
        cctok = op("gpsimd", lambda e: e.collective_compute(
            "AllGather", ALU.bypass, replica_groups=[list(range(8))],
            ins=[cc_in.ap()], outs=[cc_out.ap()]),
            reads=["ccin"], writes=["ccout"], dma="ccol1")
    emit_f1(0, F1SAFE[:8])
    if upto == "cc":
        return _finish()
    if cfg.debug:
        op("vector", lambda e: e.tensor_copy(xbuf(3)[0:1, 0:2], SCAL[0:1, 2:4]),
           reads=["scal2"], writes=["xdbg"])
        op("gpsimd", lambda e: e.dma_start(out=dbg["d_invs"][:, :], in_=xbuf(3)[0:1, 0:2]),
           reads=["xdbg"], dma="dbg3")

    kvbar = B.barrier("kv")
    KPW = SLOTW * 2
    NKH = (CCW + KPW - 1) // KPW
    for r in range(8):
        for hf in range(NKH):
            w = min(KPW, CCW - hf * KPW)
            s = (r * NKH + hf) % NSLOT
            op("gpsimd", lambda e, r=r, s=s, hf=hf, w=w: e.dma_start(
                out=WL[s].bitcast(BF16)[:, 0:w],
                in_=cc_out[r * P:(r + 1) * P, hf * KPW:hf * KPW + w]),
               writes=[f"wl{s}"], extra_deps=[cctok], reads=["ccout"], dma=f"kv{s}")
            acc_dst = POOLA[:, KVc + (r % 2) * CCW + hf * KPW: KVc + (r % 2) * CCW + hf * KPW + w]
            acc_src = POOLA[:, KVc + ((r + 1) % 2) * CCW + hf * KPW: KVc + ((r + 1) % 2) * CCW + hf * KPW + w]
            if r == 0:
                op("vector", lambda e, s=s, d=acc_dst, w=w: e.tensor_scalar_mul(
                    d, WL[s].bitcast(BF16)[:, 0:w], KVSELB[:, 0:1]),
                   reads=[f"wl{s}", "kvselb"], extra_deps=kvbar,
                   writes=["acc0"])
            else:
                op("vector", lambda e, s=s, r=r, d=acc_dst, a=acc_src, w=w: e.scalar_tensor_tensor(
                    d, WL[s].bitcast(BF16)[:, 0:w], KVSELB[:, r:r + 1], a,
                    ALU.mult, ALU.add),
                   reads=[f"wl{s}", "kvselb", f"acc{(r + 1) % 2}"], writes=[f"acc{r % 2}"])
    op("vector", lambda e: e.tensor_copy(POOLB[:, KVIb:KVIb + CCW],
                                         POOLA[:, KVc + CCW:KVc + 2 * CCW]),
       reads=["acc1", "kvin"], writes=["kvin"])
    emit_f1(0, F1SAFE[8:])
    if upto == "kvin":
        return _finish()
    if cfg.debug:
        pass

    for h in range(NH):
        for mt in range(NMW):
            bank = mt
            op("tensor", lambda e, h=h, m=mt, bank=bank: e.matmul(
                PS[bank][:, 0:MW], POOLB[:, KVIb + h * HD:KVIb + (h + 1) * HD],
                POOLB[:, QCb + h * T + m * MW:QCb + h * T + (m + 1) * MW],
                start=True, stop=True),
               reads=["kvin", f"qc{h}"], writes=[f"PS{bank}"])
            op("vector", lambda e, h=h, m=mt, bank=bank: e.tensor_add(
                O[:, h * T + m * MW:h * T + (m + 1) * MW],
                O[:, h * T + m * MW:h * T + (m + 1) * MW], PS[bank][:, 0:MW]),
               reads=[f"PS{bank}", f"o{h}"], writes=[f"o{h}"])

    if upto == "corr":
        return _finish()

    for h in range(NH):
        op("scalar", lambda e, h=h: e.activation(POOLB[:, SQb:SQb + T], O[:, h * T:(h + 1) * T],
                                                 AF.Square, bias=0.0, scale=1.0),
           reads=[f"o{h}", "sqb"], writes=["sqb"],
           extra_deps=(B.barrier("kvsb") if h == 0 else []))
        for mt in range(NMW):
            op("tensor", lambda e, h=h, m=mt: e.matmul(
                PS[5 + m][0:1, 0:MW], ONESC, POOLB[:, SQb + m * MW:SQb + (m + 1) * MW],
                start=(h == 0), stop=(h == NH - 1)),
               reads=["sqb", "cb"], writes=[f"PS{5 + mt}"])
    for mt in range(NMW):
        op("vector", lambda e, m=mt: e.tensor_scalar(
            rowm(0, m), PS[5 + m][0:1, 0:MW], 1.0 / (NH * HD), 1e-6, ALU.mult, ALU.add),
           reads=[f"PS{5 + mt}"], extra_deps=B.barrier("row_"),
           writes=["rms_v"])
    rsqrt_row(0, 3, 2, "rms_v", "rms_r", "rms_t")
    if cfg.debug:
        op("vector", lambda e: e.tensor_copy(xbuf(3)[0:1, 0:min(T, MW)], row(3)[0:1, 0:min(T, MW)]),
           reads=["rms_r"], writes=["xdbg"])
        op("gpsimd", lambda e: e.dma_start(out=dbg["d_rstd"][0:1, 0:min(T, MW)], in_=xbuf(3)[0:1, 0:min(T, MW)]),
           reads=["xdbg"], dma="dbg3")

    for mt in range(NMW):
        op("tensor", lambda e, m=mt: e.matmul(PS[4 + m][:, 0:MW], ONESR, rowm(3, m),
                                              start=True, stop=True),
           reads=["rms_r", "cf"], writes=[f"PS{4 + mt}"])

    if upto == "rms":
        return _finish()

    for gc in range(NH):
        gp = gc % 2
        gb = 2 * gp
        gtc = GTb
        gsl = 2 if (gp == 0 or cfg.debug) else 3
        wt = wstream(wg_in[gc, :, :], piece=2048)
        for fcc in range(FC):
            wap, wres = wt(fcc)
            for mt in range(NMW):
                op("tensor", lambda e, w=wap, f=fcc, m=mt, gb=gb: e.matmul(
                    PS[gb + m][:, 0:MW], w,
                    XN[:, f * T + m * MW:f * T + m * MW + MW],
                    start=(f == 0), stop=(f == FC - 1)),
                   reads=wres + [f"xn{fcc}"], writes=[f"PS{gb + mt}"])
        for mt in range(NMW):
            op("scalar", lambda e, m=mt, gb=gb, gtc=gtc: e.activation(
                POOLB[:, gtc + m * MW:gtc + (m + 1) * MW], PS[gb + m][:, 0:MW], AF.Sigmoid,
                bias=0.0, scale=1.0),
               reads=[f"PS{gb + mt}"], writes=["gt"],
               extra_deps=((B.barrier("sqb") + B.barrier("kvsb")) if gc < 2 and mt == 0 else []))
            op("vector", lambda e, m=mt, gc=gc, gsl=gsl: e.tensor_mul(
                xbuf(gsl), O[:, gc * T + m * MW:gc * T + (m + 1) * MW], PS[4 + m][:, 0:MW]),
               reads=[f"o{gc}", f"PS{4 + mt}"],
               extra_deps=(B.barrier("acc") + B.barrier("xdbg") if gc < 2 and mt == 0 else []),
               writes=[f"gsc{gp}"])
            op("vector", lambda e, m=mt, gc=gc, gsl=gsl: e.tensor_scalar_mul(
                xbuf(gsl), xbuf(gsl), NWC[:, gc:gc + 1]),
               reads=[f"gsc{gp}", "nwc"], writes=[f"gsc{gp}"])
            op("vector", lambda e, m=mt, gc=gc, gsl=gsl, gtc=gtc: e.tensor_mul(
                O[:, gc * T + m * MW:gc * T + (m + 1) * MW], xbuf(gsl),
                POOLB[:, gtc + m * MW:gtc + (m + 1) * MW]),
               reads=[f"gsc{gp}", "gt"], writes=[f"o{gc}"])

    if cfg.debug:
        dump_fm(dbg["d_og"], lambda ccc, w0: O[:, ccc * T + w0: ccc * T + w0 + MW],
                NH, T, lambda ccc: f"o{ccc}", chan="dbg4")

    if upto == "mid":
        return _finish()

    NKQ = 4
    KCQ = KC // NKQ
    sbar = B.barrier("row_") + B.barrier("rms_") + B.barrier("acc")
    for th in range(2):
        tw = th * FTS
        for fcc in range(FC):
            bank = fcc % 2
            wt = wstream(wo_in[fcc, :, :], piece=2048)
            for oc in range(NH):
                wap, wres = wt(oc)
                op("tensor", lambda e, w=wap, oc=oc, bank=bank, tw=tw: e.matmul(
                    PS[bank][:, 0:FTS], w,
                    O[:, oc * T + tw:oc * T + tw + FTS],
                    start=(oc == 0), stop=(oc == NH - 1)),
                   reads=wres + [f"o{oc}"], writes=[f"PS{bank}"])
            op("vector", lambda e, f=fcc, bank=bank: e.tensor_copy(
                S_ap(f * FTS, FTS), PS[bank][:, 0:FTS]),
               reads=[f"PS{bank}"], extra_deps=(sbar if th == 0 else []),
               writes=[f"s{fcc}"])
        if cfg.debug:
            for fcc in range(FC):
                op("scalar", lambda e, f=fcc: e.activation(
                    xbuf(3)[:, 0:FTS], S_ap(f * FTS, FTS), AF.Identity,
                    bias=0.0, scale=1.0),
                   reads=[f"s{fcc}"], writes=["xdbg"])
                op("gpsimd", lambda e, f=fcc, tw=tw: e.dma_start(
                    out=dbg["d_s"][f * 128:(f + 1) * 128, tw:tw + FTS], in_=xbuf(3)[:, 0:FTS]),
                   reads=["xdbg"], dma="dbg5")
        emit_f1(th, F1LATE if th == 0 else list(range(KC)))
        for dc in range(FC):
            bank = 4 + dc % 2
            for q in range(NKQ):
                wt = wstream(w2_in[dc * NKQ + q, :, :], quant=2, piece=512)
                for kq in range(KCQ):
                    kc = q * KCQ + kq
                    wap, wres = wt(kq)
                    op("tensor", lambda e, w=wap, kc=kc, bank=bank: e.matmul(
                        PS[bank][:, 0:FTS], w,
                        POOLB[:, Hb + kc * FTS:Hb + (kc + 1) * FTS],
                        start=(kc == 0), stop=(kc == KC - 1)),
                       reads=wres + [f"h{kc}"], writes=[f"PS{bank}"])
            op("scalar", lambda e, dc=dc, bank=bank: e.activation(
                xbuf(2)[:, 0:FTS], PS[bank][:, 0:FTS], AF.Silu,
                bias=B2C[:, dc:dc + 1], scale=1.0),
               reads=[f"PS{bank}", "b2c"], writes=["gsc"])
            op("gpsimd", lambda e, dc=dc, tw=tw: e.dma_start(
                out=xbuf(0)[:, 0:FTS], in_=x_in[dc * 128:(dc + 1) * 128, tw:tw + FTS]),
               writes=["xb0"],
               extra_deps=(B.barrier("acc") if (th == 0 and dc == 0) else []),
               dma="xf")
            op("vector", lambda e, dc=dc: e.tensor_mul(
                xbuf(2)[:, 0:FTS], xbuf(2)[:, 0:FTS], S_ap(dc * FTS, FTS)),
               reads=["gsc", f"s{dc}"], writes=["gsc"])
            op("vector", lambda e, dc=dc: e.tensor_add(
                xbuf(1)[:, 0:FTS], xbuf(2)[:, 0:FTS], xbuf(0)[:, 0:FTS]),
               reads=["gsc", "xb0", "outb"], writes=["outb"])
            op("gpsimd", lambda e, dc=dc, tw=tw: e.dma_start(
                out=out_t[dc * 128:(dc + 1) * 128, tw:tw + FTS], in_=xbuf(1)[:, 0:FTS]),
               reads=["outb"], dma="out")

    fin = B.dma_sem("out")
    op("gpsimd", lambda e: e.nop(), extra_deps=[(fin, B.dcount["out"], "gpsimd", True)])

    return _finish()


def _xbuf_doc():
    pass


def prep_in_maps(cfg, L0_attn_hidden_states, L0_attn_slope_rate, L0_attn_w_qkv,
                 L0_attn_w_gate, L0_attn_w_out, L0_attn_norm_w,
                 L0_ffn_w1, L0_ffn_b1, L0_ffn_w2, L0_ffn_b2):
    P, D, T, NH, HD, DFF = cfg.P, cfg.D, cfg.T, cfg.NH, cfg.HD, cfg.DFF
    FC, KC = cfg.FC, cfg.KC
    f32 = np.float32
    x = np.ascontiguousarray(np.asarray(L0_attn_hidden_states, dtype=f32)).reshape(-1, D)
    wq_t = np.ascontiguousarray(np.asarray(L0_attn_w_qkv, dtype=f32)
                                .reshape(NH, 3, HD, FC, P)[:, 0]
                                .transpose(0, 3, 2, 1)).reshape(NH, P, FC * 128)
    wk_t = np.ascontiguousarray(np.asarray(L0_attn_w_qkv, dtype=f32)
                                .reshape(NH, 3, HD, FC, P)[:, 1]
                                .transpose(0, 3, 2, 1)).reshape(NH, P, FC * 128)
    wv_t = np.ascontiguousarray(np.asarray(L0_attn_w_qkv, dtype=f32)
                                .reshape(NH, 3, HD, FC, P)[:, 2]
                                .transpose(0, 3, 2, 1)).reshape(NH, P, FC * 128)
    wg_t = np.ascontiguousarray(np.asarray(L0_attn_w_gate, dtype=f32)
                                .reshape(FC, 128, FC, P).transpose(0, 3, 2, 1)).reshape(FC, P, FC * 128)
    wo_t = np.ascontiguousarray(np.asarray(L0_attn_w_out, dtype=f32)
                                .reshape(FC, 128, FC, P).transpose(0, 3, 2, 1)).reshape(FC, P, FC * 128)
    w1_t = np.ascontiguousarray(np.asarray(L0_ffn_w1, dtype=f32)
                                .reshape(KC, 128, FC, P).transpose(0, 3, 2, 1)).reshape(KC, P, FC * 128)
    NKQ = 4
    KCQ = KC // NKQ
    w2r = np.asarray(L0_ffn_w2, dtype=f32).reshape(FC, 128, KC, P)
    w2_t = np.ascontiguousarray(
        w2r.transpose(0, 2, 3, 1)
        .reshape(FC, NKQ, KCQ, P, 128)
        .transpose(0, 1, 3, 2, 4)
    ).reshape(FC * NKQ, P, KCQ * 128)
    b1_t = np.ascontiguousarray(np.asarray(L0_ffn_b1, dtype=f32).reshape(KC, P).T)
    b2_t = np.ascontiguousarray(np.asarray(L0_ffn_b2, dtype=f32).reshape(FC, P).T)
    nw_t = np.ascontiguousarray(np.asarray(L0_attn_norm_w, dtype=f32).reshape(FC, P).T)
    slopes = np.ascontiguousarray(np.asarray(L0_attn_slope_rate, dtype=f32).reshape(1, NH))

    n1t = max(1, (DFF * D // 8) // (128 * 2048))
    c1 = (DFF * D // 8) // (128 * n1t)
    w1f = np.asarray(L0_ffn_w1, dtype=f32).reshape(-1)
    w2f = np.asarray(L0_ffn_w2, dtype=f32).reshape(-1)

    in_maps = []
    ncore = cfg.n_cores
    for c in range(ncore):
        xs = np.ascontiguousarray(x[c * T:(c + 1) * T].T)
        sel = np.zeros((1, 8), dtype=f32)
        if c % 2 == 1:
            sel[0, c - 1] = 1.0
        sz = DFF * D // 8
        m = {
            "x_fm": xs, "wq_t": wq_t, "wk_t": wk_t, "wv_t": wv_t,
            "wg_t": wg_t, "wo_t": wo_t, "w1_t": w1_t, "w2_t": w2_t,
            "w1sl": np.ascontiguousarray(w1f[c * sz:(c + 1) * sz].reshape(n1t, P, c1)),
            "w2sl": np.ascontiguousarray(w2f[c * sz:(c + 1) * sz].reshape(n1t, P, c1)),
            "b1_t": b1_t, "b2_t": b2_t, "nw_t": nw_t, "slopes": slopes,
            "kvsel": sel,
        }
        in_maps.append(m)
    return in_maps


def kernel(**inputs):
    cfg = CFG(D=2048, T=1024, NH=16, DFF=8192)
    in_maps = prep_in_maps(cfg, **inputs)
    nc, _ = build_nc(cfg)
    res = run_bass_kernel_spmd(nc, in_maps, core_ids=list(range(cfg.n_cores)))
    outs = [np.ascontiguousarray(r["out_fm"].T) for r in res.results]
    full = np.concatenate(outs, axis=0)
    return full.reshape(4, 2048, 2048)


# version 07
# speedup vs baseline: 2.9314x; 1.0712x over previous
import os as _os
import numpy as np
import ml_dtypes
from contextlib import ExitStack
from collections import defaultdict

import concourse.bass as bass
import concourse.mybir as mybir
from concourse.bass_utils import run_bass_kernel_spmd

F32 = mybir.dt.float32
BF16 = mybir.dt.bfloat16
AF = mybir.ActivationFunctionType
ALU = mybir.AluOpType
AX = mybir.AxisListType

MAGIC = float(np.float32(1.5 * 2 ** 23))


class PB:
    ENGS = ("sync", "vector", "scalar", "tensor", "gpsimd")

    def __init__(self, nc, ctx):
        self.nc = nc
        self.ctx = ctx
        self.items = {e: [] for e in self.ENGS}
        self.esem = {e: ctx.enter_context(nc.semaphore(f"S_{e}")) for e in self.ENGS}
        self.dsem = {}
        self.ecount = {e: 0 for e in self.ENGS}
        self.dcount = defaultdict(int)
        self.res_w = {}
        self.res_r = defaultdict(list)

    def dma_sem(self, name):
        if name not in self.dsem:
            self.dsem[name] = self.ctx.enter_context(self.nc.semaphore(f"D_{name}"))
        return self.dsem[name]

    def op(self, eng, fn, reads=(), writes=(), extra_deps=(), dma=None, tag=None):
        deps = set()
        for r in reads:
            t = self.res_w.get(r)
            if t is not None:
                deps.add(t)
        for r in writes:
            t = self.res_w.get(r)
            if t is not None:
                deps.add(t)
            for t2 in self.res_r.get(r, ()):
                deps.add(t2)
        for t in extra_deps:
            if t is not None:
                deps.add(t)
        if dma is None:
            sem = self.esem[eng]
            self.ecount[eng] += 1
            tok = (sem, self.ecount[eng], eng, False)
        else:
            step = 1 if dma.startswith("ccol") else 16
            sem = self.dma_sem(dma)
            self.dcount[dma] += step
            tok = (sem, self.dcount[dma], eng, True)
        deps.discard(tok)
        self.items[eng].append((fn, sorted(deps, key=lambda t: (t[0].name, t[1])),
                               sem, (1 if (dma and dma.startswith("ccol")) else (16 if dma else 1)), tag))
        for r in reads:
            self.res_r[r].append(tok)
        for r in writes:
            self.res_w[r] = tok
            self.res_r[r] = [tok]
        return tok

    def barrier(self, prefix):
        toks = set()
        for r in set(list(self.res_w) + list(self.res_r)):
            if r.startswith(prefix):
                t = self.res_w.get(r)
                if t is not None:
                    toks.add(t)
                for t2 in self.res_r.get(r, ()):
                    toks.add(t2)
        return list(toks)

    def emit(self):
        nc = self.nc
        stats = {e: {"ops": len(self.items[e]), "waits": 0} for e in self.ENGS}
        with nc.Block() as block:
            for eng in self.ENGS:
                items = self.items[eng]
                if not items:
                    continue

                def make(eng=eng, items=items):
                    def f(proxy):
                        waited = {}
                        for (fn, deps, sem, inc, tag) in items:
                            for (dsm, val, peng, isdma) in deps:
                                if peng == "tensor" and eng == "tensor" and not isdma:
                                    continue
                                if waited.get(dsm.name, 0) >= val:
                                    continue
                                proxy.wait_ge(dsm, val)
                                waited[dsm.name] = val
                                stats[eng]["waits"] += 1
                            fn(proxy).then_inc(sem, inc)
                    return f

                getattr(block, eng)(make())
        return stats


class CFG:
    def __init__(self, D=2048, T=1024, NH=16, DFF=8192, debug=False):
        self.P = 128
        self.D, self.T, self.NH, self.HD, self.DFF = D, T, NH, 128, DFF
        self.BLK = 256
        self.FTS = T // 2
        self.n_cores = 8
        self.debug = debug
        self.FC = D // 128
        self.KC = DFF // 128
        self.NB = T // self.BLK
        self.NTC = T // 128
        self.MW = min(512, T)
        self.NMW = T // self.MW
        assert T % self.BLK == 0


def build_nc(cfg, upto='all'):
    P, D, T, NH, HD, DFF = cfg.P, cfg.D, cfg.T, cfg.NH, cfg.HD, cfg.DFF
    FC, KC, NB, BLK, NTC = cfg.FC, cfg.KC, cfg.NB, cfg.BLK, cfg.NTC
    FTS, MW, NMW = cfg.FTS, cfg.MW, cfg.NMW
    CCW = NH * HD

    nc = bass.Bass("TRN2", target_bir_lowering=False)

    x_in = nc.dram_tensor("x_fm", [D, T], F32, kind="ExternalInput")
    wq_in = nc.dram_tensor("wq_t", [NH, P, FC * 128], F32, kind="ExternalInput")
    wk_in = nc.dram_tensor("wk_t", [NH, P, FC * 128], F32, kind="ExternalInput")
    wv_in = nc.dram_tensor("wv_t", [NH, P, FC * 128], F32, kind="ExternalInput")
    wg_in = nc.dram_tensor("wg_t", [FC, P, FC * 128], F32, kind="ExternalInput")
    wo_in = nc.dram_tensor("wo_t", [FC, P, FC * 128], F32, kind="ExternalInput")
    w1_in = nc.dram_tensor("w1_t", [KC, P, FC * 128], F32, kind="ExternalInput")
    w2_in = nc.dram_tensor("w2_t", [FC * 4, P, KC * 32], F32, kind="ExternalInput")
    n1t = max(1, (DFF * D // 8) // (128 * 2048))
    c1 = (DFF * D // 8) // (128 * n1t)
    w1sl_in = nc.dram_tensor("w1sl", [n1t, P, c1], F32, kind="ExternalInput")
    w2sl_in = nc.dram_tensor("w2sl", [n1t, P, c1], F32, kind="ExternalInput")
    b1_in = nc.dram_tensor("b1_t", [P, KC], F32, kind="ExternalInput")
    b2_in = nc.dram_tensor("b2_t", [P, FC], F32, kind="ExternalInput")
    nw_in = nc.dram_tensor("nw_t", [P, FC], F32, kind="ExternalInput")
    slo_in = nc.dram_tensor("slopes", [1, NH], F32, kind="ExternalInput")
    kvsel_in = nc.dram_tensor("kvsel", [1, 8], F32, kind="ExternalInput")
    out_t = nc.dram_tensor("out_fm", [D, T], F32, kind="ExternalOutput")

    w1q_s = nc.dram_tensor("w1q_s", [KC, P, FC * 128], BF16, kind="Internal")
    w2q_s = nc.dram_tensor("w2q_s", [FC * 4, P, (KC // 4) * 128], BF16, kind="Internal")
    cc_in = nc.dram_tensor("cc_in", [P, CCW], BF16, kind="Internal")
    cc_out = nc.dram_tensor("cc_out", [8 * P, CCW], BF16, kind="Internal")
    cc_in2 = nc.dram_tensor("cc_in2", [16, 128], F32, kind="Internal")
    cc_out2 = nc.dram_tensor("cc_out2", [128, 128], F32, kind="Internal")

    arrq_np = (np.arange(BLK) + 1).astype(np.float32)
    arrk_np = (BLK - 1 - np.arange(BLK)).astype(np.float32)
    dm = np.arange(BLK)[None, :] - np.arange(P)[:, None]
    didx_np = np.concatenate([np.maximum(dm - j * P, 0) for j in range(BLK // P)], 1)
    dmsk_np = np.concatenate([(dm - j * P >= 0) for j in range(BLK // P)], 1)
    cb_np = np.concatenate([
        np.broadcast_to(arrq_np, (P, BLK)),
        np.broadcast_to(arrk_np, (P, BLK)),
        didx_np.astype(np.float32),
        dmsk_np.astype(np.float32),
        np.eye(P, dtype=np.float32),
        np.ones((P, 1), dtype=np.float32),
    ], axis=1).astype(ml_dtypes.bfloat16)
    cf_np = np.concatenate([
        np.ones((P, P), dtype=np.float32),
        np.broadcast_to((np.arange(NB) * BLK).astype(np.float32), (P, NB)),
        np.full((P, 1), float(BLK), np.float32),
        np.full((P, 1), -MAGIC, np.float32),
    ], axis=1)
    cb_t = nc.inline_tensor(cb_np, name="c_bf")
    cf_t = nc.inline_tensor(np.ascontiguousarray(cf_np), name="c_f32")

    dbg = {}
    if cfg.debug:
        for nm, shp in [("d_xn", [D, T]), ("d_o1", [NH * HD, T]), ("d_og", [NH * HD, T]),
                        ("d_s", [D, T]), ("d_cc", [P, CCW + 2]), ("d_rstd", [1, T]),
                        ("d_invs", [1, 2]), ("d_kvin", [P, CCW])]:
            dbg[nm] = nc.dram_tensor(nm, shp, F32, kind="ExternalOutput")

    ctx = ExitStack()
    SB = lambda n, s, d=F32: ctx.enter_context(nc.sbuf_tensor(n, s, d))

    XN = SB("XN", [P, FC * T], BF16)
    O = SB("O", [P, NH * T], BF16)
    NA = 2048 + max(FC * FTS // 2, 6 * T)
    POOLA = SB("POOLA", [P, NA], F32)
    KVc = 0
    Sc = 2048

    def S_ap(c0, w):
        return POOLA[:, :].bitcast(BF16)[:, 2 * Sc + c0:2 * Sc + c0 + w]
    NBC = max(KC * FTS, NH * T + 16 * T + max(T, 2 * NB * HD, NMW * MW), 2048)
    POOLB = SB("POOLB", [P, NBC], BF16)
    QCb = 0
    TR0 = NH * T
    TAb = [TR0 + pb * 8 * T for pb in range(2)]
    TBb = [TR0 + pb * 8 * T + T for pb in range(2)]
    TCb = [TR0 + pb * 8 * T + 2 * T for pb in range(2)]
    TDb = [TR0 + pb * 8 * T + 3 * T for pb in range(2)]
    TEb = [TR0 + pb * 8 * T + 4 * T for pb in range(2)]
    TGb = [TR0 + pb * 8 * T + 5 * T for pb in range(2)]
    QKb = [TR0 + pb * 8 * T + 6 * T for pb in range(2)]
    SQb = TR0 + 16 * T
    GTb = SQb
    KVIb = QKb[0]
    XQb = 0
    Hb = 0
    ENDB = TR0 + 16 * T + max(T, 2 * NB * HD, NMW * MW)
    assert ENDB <= NBC, ENDB

    NWC = SB("NWC", [P, FC], F32)
    B1C = SB("B1C", [P, KC], F32)
    B2C = SB("B2C", [P, FC], F32)
    SLO = SB("SLO", [1, NH], F32)
    NEGR = SB("NEGR", [1, NH], F32)
    NEGSB = SB("NEGSB", [P, NH], F32)
    BDB = SB("BDB", [P, NH], F32)
    KVSEL = SB("KVSEL", [1, 8], F32)
    KVSELB = SB("KVSELB", [P, 8], F32)
    SCAL = SB("SCAL", [1, 4], F32)
    PART8 = SB("PART8", [8, 2], F32)
    INVSB = SB("INVSB", [P, 2], F32)
    WSUM = SB("WSUM", [P, 4], F32)
    KVBF = SB("KVBF", [P, HD], BF16)
    PSTG = SB("PSTG", [P, 128], F32)
    NSLOT = 10
    SLOTW = 512
    WLBUF = SB("WLBUF", [P, NSLOT * SLOTW], F32)
    WBBUF = SB("WBBUF", [P, NSLOT * SLOTW], BF16)
    WL = [WLBUF[:, i * SLOTW:(i + 1) * SLOTW] for i in range(NSLOT)]
    WB = [WBBUF[:, i * SLOTW:(i + 1) * SLOTW] for i in range(NSLOT)]
    CONSTB = SB("CONSTB", [P, 6 * BLK + P + 1], BF16)
    CONSTF = SB("CONSTF", [P, P + NB + 2], F32)
    ARRQ = CONSTB[:, 0:BLK]
    ARRK = CONSTB[:, BLK:2 * BLK]
    DIDX = CONSTB[:, 2 * BLK:4 * BLK]
    DMSK = CONSTB[:, 4 * BLK:6 * BLK]
    IDENT = CONSTB[:, 6 * BLK:6 * BLK + P]
    ONESC = CONSTB[:, 6 * BLK + P:6 * BLK + P + 1]
    ONESR = CONSTF[0:1, 0:P]
    ONESCF = CONSTF[:, 0:1]
    BLKI = CONSTF[:, P:P + NB]
    BLKCOL = CONSTF[:, P + NB:P + NB + 1]
    NEGM = CONSTF[:, P + NB + 1:P + NB + 2]
    DECQT = SB("DECQT", [P, 2 * BLK], BF16)
    DECKT = SB("DECKT", [P, 2 * BLK], BF16)
    DIAGT = SB("DIAGT", [P, 4 * BLK], BF16)
    BDPT = SB("BDPT", [P, 2 * NB], F32)
    DECQ = [DECQT[:, i * BLK:(i + 1) * BLK] for i in range(2)]
    DECK = [DECKT[:, i * BLK:(i + 1) * BLK] for i in range(2)]
    DIAG = [DIAGT[:, i * 2 * BLK:(i + 1) * 2 * BLK] for i in range(2)]
    BDP = [BDPT[:, i * NB:(i + 1) * NB] for i in range(2)]

    PS = [ctx.enter_context(nc.psum_tensor(f"PS{i}", [P, 512], F32)) for i in range(7)]
    PT = ctx.enter_context(nc.psum_tensor("PT", [P, 512], F32))

    B = PB(nc, ctx)
    op = B.op

    def row(k):
        return POOLA[0:1, Sc + k * T: Sc + (k + 1) * T]

    def rowm(k, mt):
        return POOLA[0:1, Sc + k * T + mt * MW: Sc + k * T + mt * MW + MW]

    def _finish():
        toks = []
        for ch, cnt in list(B.dcount.items()):
            if cnt:
                toks.append((B.dsem[ch], cnt, "sync", True))
        for eng in B.ENGS:
            if B.ecount[eng] and eng != "gpsimd":
                toks.append((B.esem[eng], B.ecount[eng], eng, False))
        op("gpsimd", lambda e: e.nop(), extra_deps=toks)
        stats = B.emit()
        ctx.close()
        return nc, stats

    op("gpsimd", lambda e: e.dma_start(out=CONSTB[:, :], in_=cb_t[:, :]),
       writes=["cb"], dma="ccb")
    op("gpsimd", lambda e: e.dma_start(out=CONSTF[:, :], in_=cf_t[:, :]),
       writes=["cf"], dma="ccf")
    for (dst, srct, res) in [(NWC, nw_in, "nwc"), (B1C, b1_in, "b1c"), (B2C, b2_in, "b2c"),
                             (SLO, slo_in, "slo"), (KVSEL, kvsel_in, "kvsel")]:
        op("gpsimd", lambda e, d=dst, s=srct: e.dma_start(out=d[:, :], in_=s[:, :]),
           writes=[res], dma=f"c_{res}")

    op("vector", lambda e: e.tensor_scalar_mul(NEGR[:, :], SLO[:, :], -1.0),
       reads=["slo"], writes=["negr"])
    for h in range(NH):
        op("tensor", lambda e, h=h: e.matmul(PS[5][:, 0:1], ONESR,
                                             NEGR[0:1, h:h + 1], start=True, stop=True),
           reads=["negr", "cf"], writes=["PS5"])
        op("vector", lambda e, h=h: e.tensor_copy(NEGSB[:, h:h + 1], PS[5][:, 0:1]),
           reads=["PS5"], writes=["negsb"])
    op("tensor", lambda e: e.matmul(PS[5][:, 0:8], ONESR, KVSEL[0:1, :],
                                    start=True, stop=True),
       reads=["kvsel", "cf"], writes=["PS5"])
    op("vector", lambda e: e.tensor_copy(KVSELB[:, :], PS[5][:, 0:8]),
       reads=["PS5"], writes=["kvselb"])
    for h in range(NH):
        op("scalar", lambda e, h=h: e.activation(BDB[:, h:h + 1], BLKCOL, AF.Exp,
                                                 bias=0.0, scale=NEGSB[:, h:h + 1]),
           reads=["negsb", "cf", "bdb"], writes=["bdb"])

    XQs = [XQb + i * 2 * T for i in range(2)]

    def xq_ap(bi, half):
        c = XQs[bi] + half * T
        return POOLB[:, c:c + T]

    def xbuf(bi):
        return POOLA[:, KVc + bi * MW: KVc + (bi + 1) * MW]

    def xrow(bi):
        return POOLA[:, KVc + bi * T: KVc + (bi + 1) * T]

    for fcc in range(FC):
        bi = fcc % 2
        op("gpsimd", lambda e, b=bi, f=fcc: e.dma_start(
            out=xrow(b), in_=x_in[f * 128:(f + 1) * 128, :]),
           writes=[f"xr{bi}"], dma=f"x{bi}")
        op("vector", lambda e, b=bi: e.tensor_copy(xq_ap(b, 0), xrow(b)),
           reads=[f"xr{bi}"], writes=[f"xc{bi}"])
        op("scalar", lambda e, b=bi: e.activation(xq_ap(b, 1), xrow(b), AF.Square,
                                                  bias=0.0, scale=1.0),
           reads=[f"xr{bi}"], writes=[f"xq{bi}"])
        for mt in range(NMW):
            op("tensor", lambda e, b=bi, f=fcc, m=mt: e.matmul(
                PS[0 + m][0:1, 0:MW], ONESC, xq_ap(b, 0)[:, m * MW:(m + 1) * MW],
                start=(f == 0), stop=(f == FC - 1)),
               reads=[f"xc{bi}", "cb"], writes=[f"PS{0 + mt}"])
            op("tensor", lambda e, b=bi, f=fcc, m=mt: e.matmul(
                PS[2 + m][0:1, 0:MW], ONESC, xq_ap(b, 1)[:, m * MW:(m + 1) * MW],
                start=(f == 0), stop=(f == FC - 1)),
               reads=[f"xq{bi}", "cb"], writes=[f"PS{2 + mt}"])
    for mt in range(NMW):
        op("vector", lambda e, m=mt: e.tensor_scalar_mul(rowm(0, m), PS[0 + m][0:1, 0:MW], 1.0 / D),
           reads=[f"PS{0 + mt}"], writes=["row_mean"])
        op("vector", lambda e, m=mt: e.tensor_scalar_mul(rowm(1, m), PS[2 + m][0:1, 0:MW], 1.0 / D),
           reads=[f"PS{2 + mt}"], writes=["row_msq"])
    op("vector", lambda e: e.tensor_mul(row(2), row(0), row(0)),
       reads=["row_mean"], writes=["row_tmp"])
    op("vector", lambda e: e.tensor_sub(row(1), row(1), row(2)),
       reads=["row_msq", "row_tmp"], writes=["row_msq"])
    op("vector", lambda e: e.tensor_scalar_add(row(1), row(1), 1e-5),
       reads=["row_msq"], writes=["row_msq"])

    def rsqrt_row(src_k, dst_k, tmp_k, src_res, dst_res, tmp_res):
        op("scalar", lambda e: e.activation(row(tmp_k), row(src_k), AF.Sqrt,
                                            bias=0.0, scale=1.0),
           reads=[src_res], writes=[tmp_res])
        op("vector", lambda e: e.reciprocal(row(dst_k), row(tmp_k)),
           reads=[tmp_res], writes=[dst_res])
        op("vector", lambda e: e.tensor_mul(row(tmp_k), row(dst_k), row(dst_k)),
           reads=[dst_res], writes=[tmp_res])
        op("vector", lambda e: e.tensor_mul(row(tmp_k), row(tmp_k), row(src_k)),
           reads=[tmp_res, src_res], writes=[tmp_res])
        op("vector", lambda e: e.tensor_scalar(row(tmp_k), row(tmp_k), -0.5, 1.5,
                                               ALU.mult, ALU.add),
           reads=[tmp_res], writes=[tmp_res])
        op("vector", lambda e: e.tensor_mul(row(dst_k), row(dst_k), row(tmp_k)),
           reads=[dst_res, tmp_res], writes=[dst_res])

    rsqrt_row(1, 3, 2, "row_msq", "row_arow", "row_tmp")
    op("vector", lambda e: e.scalar_tensor_tensor(row(4), row(0), -1.0, row(3),
                                                  ALU.mult, ALU.mult),
       reads=["row_mean", "row_arow"], writes=["row_brow"])
    BB = [PS[6], PT]
    BBres = ["PS6", "PTb"]
    for mt in range(NMW):
        op("tensor", lambda e, m=mt: e.matmul(PS[4 + m][:, 0:MW], ONESR, rowm(3, m),
                                              start=True, stop=True),
           reads=["row_arow", "cf"], writes=[f"PS{4 + mt}"])
        op("tensor", lambda e, m=mt: e.matmul(BB[m][:, 0:MW], ONESR, rowm(4, m),
                                              start=True, stop=True),
           reads=["row_brow", "cf"], writes=[BBres[mt]])
    for fcc in range(FC):
        bi = fcc % 2
        op("gpsimd", lambda e, b=bi, f=fcc: e.dma_start(
            out=xrow(b), in_=x_in[f * 128:(f + 1) * 128, :]),
           writes=[f"xr{bi}"], dma=f"x{bi}")
        for mt in range(NMW):
            op("vector", lambda e, b=bi, m=mt: e.tensor_mul(
                xrow(b)[:, m * MW:(m + 1) * MW], xrow(b)[:, m * MW:(m + 1) * MW],
                PS[4 + m][:, 0:MW]),
               reads=[f"xr{bi}", f"PS{4 + mt}"], writes=[f"xr{bi}"])
            op("vector", lambda e, b=bi, f=fcc, m=mt: e.tensor_add(
                XN[:, f * T + m * MW: f * T + (m + 1) * MW],
                xrow(b)[:, m * MW:(m + 1) * MW], BB[m][:, 0:MW]),
               reads=[f"xr{bi}", BBres[mt]], writes=[f"xn{fcc}"])

    if cfg.debug:
        def dump_fm(dramt, src_ap_fn, nchunk, width, res_of, chan="dbg"):
            for ccc in range(nchunk):
                for w0 in range(0, width, MW):
                    op("scalar", lambda e, ccc=ccc, w0=w0: e.activation(
                        xbuf(3), src_ap_fn(ccc, w0), AF.Identity, bias=0.0, scale=1.0),
                       reads=[res_of(ccc)], writes=["xdbg"])
                    op("gpsimd", lambda e, ccc=ccc, w0=w0, d=dramt: e.dma_start(
                        out=d[ccc * 128:(ccc + 1) * 128, w0:w0 + MW], in_=xbuf(3)),
                       reads=["xdbg"], dma=chan)

        dump_fm(dbg["d_xn"], lambda ccc, w0: XN[:, ccc * T + w0: ccc * T + w0 + MW],
                FC, T, lambda ccc: f"xn{ccc}")

    if upto == "p0":
        return _finish()

    wuc = [0]
    op("gpsimd", lambda e: e.memset(WSUM[:, :], 0.0), writes=["wsum", "wsumt"])
    for wi, wsl in enumerate([w1sl_in, w2sl_in]):
        for t in range(n1t):
            for c0 in range(0, c1, SLOTW):
                w = min(SLOTW, c1 - c0)
                s = wuc[0] % NSLOT
                wuc[0] += 1
                op("scalar", lambda e, s=s, t=t, wt=wsl, c0=c0, w=w: e.dma_start(
                    out=WL[s][:, 0:w], in_=wt[t, :, c0:c0 + w]),
                   writes=[f"wl{s}"], dma=f"wsl{s}")
                op("vector", lambda e, s=s, wi=wi, w=w: e.tensor_reduce(
                    WSUM[:, 2 + wi:3 + wi], WL[s][:, 0:w], AX.X, ALU.add,
                    apply_absolute_value=True),
                   reads=[f"wl{s}"], writes=["wsumt"])
                op("vector", lambda e, wi=wi: e.tensor_add(
                    WSUM[:, wi:wi + 1], WSUM[:, wi:wi + 1], WSUM[:, 2 + wi:3 + wi]),
                   reads=["wsum", "wsumt"], writes=["wsum"])
    op("tensor", lambda e: e.matmul(PS[6][0:1, 0:1], WSUM[:, 0:1], ONESCF,
                                    start=True, stop=True),
       reads=["wsum", "cf"], writes=["PS6"])
    op("tensor", lambda e: e.matmul(PS[6][0:1, 1:2], WSUM[:, 1:2], ONESCF,
                                    start=False, stop=True, skip_group_check=True),
       reads=["wsum", "cf"], writes=["PS6"])
    op("vector", lambda e: e.tensor_copy(SCAL[0:1, 0:2], PS[6][0:1, 0:2]),
       reads=["PS6"], writes=["scal"])
    op("gpsimd", lambda e: e.memset(PSTG[:, :], 0.0), writes=["pstg"])
    op("vector", lambda e: e.tensor_copy(PSTG[0:1, 0:2], SCAL[0:1, 0:2]),
       reads=["scal", "pstg"], writes=["pstg"])
    op("gpsimd", lambda e: e.dma_start(out=cc_in2[:, :], in_=PSTG[0:16, :]),
       reads=["pstg"], writes=["ccin2"], dma="cc2")
    if _os.environ.get("NOCC"):
        cc2tok = op("gpsimd", lambda e: e.dma_start(out=cc_out2[0:16, :], in_=cc_in2[:, :]),
                    reads=["ccin2"], writes=["ccout2"], dma="nocc2")
    else:
        cc2tok = op("gpsimd", lambda e: e.collective_compute(
            "AllGather", ALU.bypass, replica_groups=[list(range(8))],
            ins=[cc_in2.ap()], outs=[cc_out2.ap()]),
            reads=["ccin2"], writes=["ccout2"], dma="ccol2")
    op("gpsimd", lambda e: e.dma_start(
        out=PART8[:, :], in_=cc_out2.ap().rearrange("(r p) c -> r p c", p=16)[:, 0, 0:2]),
       reads=["ccout2"], extra_deps=[cc2tok], writes=["part8"], dma="p8")
    op("tensor", lambda e: e.matmul(PS[6][0:1, 0:1], PART8[:, 0:1], ONESCF[0:8, 0:1],
                                    start=True, stop=True),
       reads=["part8", "cf"], writes=["PS6"])
    op("tensor", lambda e: e.matmul(PS[6][0:1, 1:2], PART8[:, 1:2], ONESCF[0:8, 0:1],
                                    start=False, stop=True, skip_group_check=True),
       reads=["part8", "cf"], writes=["PS6"])
    op("vector", lambda e, n=float(DFF * D): e.tensor_scalar_mul(
        SCAL[0:1, 2:4], PS[6][0:1, 0:2], 1.0 / n),
       reads=["PS6"], writes=["scal2"])
    op("vector", lambda e: e.tensor_scalar_max(SCAL[0:1, 2:4], SCAL[0:1, 2:4], 1e-8),
       reads=["scal2"], writes=["scal2"])
    op("vector", lambda e: e.reciprocal(SCAL[0:1, 2:4], SCAL[0:1, 2:4]),
       reads=["scal2"], writes=["scal2"])
    op("tensor", lambda e: e.matmul(PS[6][:, 0:2], ONESR, SCAL[0:1, 2:4],
                                    start=True, stop=True),
       reads=["scal2", "cf"], writes=["PS6"])
    op("vector", lambda e: e.tensor_copy(INVSB[:, :], PS[6][:, 0:2]),
       reads=["PS6"], writes=["invsb"])

    xqbar = B.barrier("xc") + B.barrier("xq")

    def wstream(src_ap, quant=0, piece=1024, spill_ap=None, bf16_src=False):
        cols = src_ap.shape[-1]
        spp = piece // SLOTW
        parts = []
        for c0 in range(0, cols, piece):
            w = min(piece, cols - c0)
            if (wuc[0] % NSLOT) + spp > NSLOT:
                wuc[0] += NSLOT - (wuc[0] % NSLOT)
            s = wuc[0] % NSLOT
            wuc[0] += spp
            wres = [f"wl{s + k}" for k in range(spp)]
            bres = [f"wb{s + k}" for k in range(spp)]
            wap = WLBUF[:, s * SLOTW:s * SLOTW + w]
            bap = WBBUF[:, s * SLOTW:s * SLOTW + w]
            if bf16_src:
                op("sync", lambda e, a=src_ap, c0=c0, w=w, d=bap: e.dma_start(
                    out=d, in_=a[:, c0:c0 + w]),
                   writes=bres, dma=f"w{s}")
                parts.append((s, c0, w, bres))
                continue
            op("sync", lambda e, a=src_ap, c0=c0, w=w, d=wap: e.dma_start(
                out=d, in_=a[:, c0:c0 + w]),
               writes=wres, dma=f"w{s}")
            if not quant:
                op("scalar", lambda e, d=bap, sv=wap: e.activation(
                    d, sv, AF.Identity, bias=0.0, scale=1.0),
                   reads=wres, writes=bres)
            else:
                op("scalar", lambda e, d=wap, q=quant: e.activation(
                    d, d, AF.Identity, bias=0.0, scale=INVSB[:, q - 1:q]),
                   reads=wres + ["invsb"], writes=wres)
                op("vector", lambda e, d=wap: e.tensor_scalar(
                    d, d, -1.25, 1.25, ALU.max, ALU.min),
                   reads=wres, writes=wres)
                op("vector", lambda e, d=bap, sv=wap: e.tensor_scalar(
                    d, sv, MAGIC, MAGIC, ALU.add, ALU.subtract),
                   reads=wres, writes=bres)
                if spill_ap is not None:
                    op("gpsimd", lambda e, sp=spill_ap, c0=c0, w=w, sv=bap: e.dma_start(
                        out=sp[:, c0:c0 + w], in_=sv),
                       reads=bres, dma=f"sp{s}")
            parts.append((s, c0, w, bres))

        def tile(j, parts=parts):
            c = j * 128
            for (s, c0, w, bres) in parts:
                if c0 <= c < c0 + w:
                    return WBBUF[:, s * SLOTW + (c - c0):s * SLOTW + (c - c0) + 128], bres
            raise IndexError(c)
        return tile

    def proj(col_dst, res_dst, w_src, psb, act=AF.Silu):
        wt = wstream(w_src, piece=2048)
        for fcc in range(FC):
            wap, wres = wt(fcc)
            for mt in range(NMW):
                op("tensor", lambda e, w=wap, f=fcc, m=mt: e.matmul(
                    PS[psb + m][:, 0:MW], w,
                    XN[:, f * T + m * MW:f * T + m * MW + MW],
                    start=(f == 0), stop=(f == FC - 1)),
                   reads=wres + [f"xn{fcc}"], writes=[f"PS{psb + mt}"])
        for mt in range(NMW):
            op("scalar", lambda e, m=mt, d=col_dst: e.activation(
                POOLB[:, d + m * MW:d + (m + 1) * MW], PS[psb + m][:, 0:MW], act,
                bias=0.0, scale=1.0),
               reads=[f"PS{psb + mt}"], writes=[res_dst])

    KVSB = [SQb + pb * NB * HD for pb in range(2)]
    _skip = set(_os.environ.get("ATTN_SKIP", "").split(","))

    def stageA1(h):
        pb = h % 2
        TAp, TBp, TCp, TEp, TGp, QKp = (TAb[pb], TBb[pb], TCb[pb],
                                        TEb[pb], TGb[pb], QKb[pb])
        TDp = TDb[pb]
        op("scalar", lambda e, h=h, pb=pb: e.activation(DECQ[pb], ARRQ, AF.Exp,
                                                        bias=0.0, scale=NEGSB[:, h:h + 1]),
           reads=["cb", "negsb", f"decq{pb}"], writes=[f"decq{pb}"])
        op("scalar", lambda e, h=h, pb=pb: e.activation(DECK[pb], ARRK, AF.Exp,
                                                        bias=0.0, scale=NEGSB[:, h:h + 1]),
           reads=["cb", "negsb", f"deck{pb}"], writes=[f"deck{pb}"])
        op("scalar", lambda e, h=h, pb=pb: e.activation(DIAG[pb], DIDX, AF.Exp,
                                                        bias=0.0, scale=NEGSB[:, h:h + 1]),
           reads=["cb", "negsb", f"diag{pb}"], writes=[f"diag{pb}"])
        op("vector", lambda e, pb=pb: e.tensor_mul(DIAG[pb], DIAG[pb], DMSK),
           reads=[f"diag{pb}", "cb"], writes=[f"diag{pb}"])
        op("scalar", lambda e, h=h, pb=pb: e.activation(BDP[pb], BLKI, AF.Exp,
                                                        bias=0.0, scale=NEGSB[:, h:h + 1]),
           reads=["cf", "negsb", f"bdp{pb}"], writes=[f"bdp{pb}"])
        proj(TAp, f"ta{pb}", wq_in[h, :, :], 0)
        for i in range(NB):
            op("vector", lambda e, i=i, pb=pb, TEp=TEp, TAp=TAp: e.tensor_mul(
                POOLB[:, TEp + i * BLK:TEp + (i + 1) * BLK],
                POOLB[:, TAp + i * BLK:TAp + (i + 1) * BLK], DECQ[pb]),
               reads=[f"ta{pb}", f"decq{pb}"], writes=[f"te{pb}"])
            op("vector", lambda e, i=i, h=h, pb=pb, TEp=TEp: e.tensor_scalar_mul(
                POOLB[:, QCb + h * T + i * BLK:QCb + h * T + (i + 1) * BLK],
                POOLB[:, TEp + i * BLK:TEp + (i + 1) * BLK], BDP[pb][:, i:i + 1]),
               reads=[f"te{pb}", f"bdp{pb}"],
               extra_deps=(xqbar if h < 4 else []), writes=[f"qc{h}"])
    def stageA2(h):
        pb = h % 2
        TAp, TBp, TCp, TEp, TGp, QKp = (TAb[pb], TBb[pb], TCb[pb],
                                        TEb[pb], TGb[pb], QKb[pb])
        TDp = TDb[pb]
        proj(TBp, f"tb{pb}", wk_in[h, :, :], 2)
        for i in range(NB):
            for j in range(2):
                bank = 5 + j
                op("tensor", lambda e, i=i, j=j, bank=bank, TBp=TBp, TAp=TAp: e.matmul(
                    PS[bank][:, 0:BLK],
                    POOLB[:, TBp + (2 * i + j) * 128:TBp + (2 * i + j + 1) * 128],
                    POOLB[:, TAp + i * BLK:TAp + (i + 1) * BLK],
                    start=True, stop=True),
                   reads=[f"tb{pb}", f"ta{pb}"], writes=[f"PS{bank}"])
                op("vector", lambda e, i=i, j=j, bank=bank, pb=pb, QKp=QKp: e.tensor_mul(
                    POOLB[:, QKp + (2 * i + j) * BLK:QKp + (2 * i + j + 1) * BLK],
                    PS[bank][:, 0:BLK], DIAG[pb][:, j * BLK:(j + 1) * BLK]),
                   reads=[f"PS{bank}", f"diag{pb}"], writes=[f"qkd{pb}"])
        for i in range(NB):
            op("vector", lambda e, i=i, pb=pb, TCp=TCp, TBp=TBp: e.tensor_mul(
                POOLB[:, TCp + i * BLK:TCp + (i + 1) * BLK],
                POOLB[:, TBp + i * BLK:TBp + (i + 1) * BLK], DECK[pb]),
               reads=[f"tb{pb}", f"deck{pb}"], writes=[f"tc{pb}"])
        proj(TAp, f"ta{pb}", wv_in[h, :, :], 0)
        if "trans" in _skip:
            return
        for tt in range(NTC):
            bnk = PT if (tt // 2) % 2 == 0 else PS[4]
            bres = "PTb" if (tt // 2) % 2 == 0 else "PS4"
            op("tensor", lambda e, tt=tt, b=bnk, TCp=TCp: e.matmul(
                b[:, (tt % 2) * 128:(tt % 2 + 1) * 128],
                POOLB[:, TCp + tt * 128:TCp + (tt + 1) * 128], IDENT,
                start=True, stop=True, skip_group_check=True),
               reads=[f"tc{pb}", "cb"], writes=[bres], tag="ktr")
            if tt % 2 == 1:
                op("vector", lambda e, tt=tt, b=bnk, TGp=TGp: e.tensor_copy(
                    POOLB[:, TGp + (tt - 1) * 128:TGp + (tt + 1) * 128],
                    b[:, 0:256]),
                   reads=[bres], writes=[f"tg{pb}"])
        for tt in range(NTC):
            bnk = PT if (tt // 2) % 2 == 0 else PS[4]
            bres = "PTb" if (tt // 2) % 2 == 0 else "PS4"
            op("tensor", lambda e, tt=tt, b=bnk, TAp=TAp: e.matmul(
                b[:, (tt % 2) * 128:(tt % 2 + 1) * 128],
                POOLB[:, TAp + tt * 128:TAp + (tt + 1) * 128], IDENT,
                start=True, stop=True, skip_group_check=True),
               reads=[f"ta{pb}", "cb"], writes=[bres], tag="vtr")
            if tt % 2 == 1:
                op("vector", lambda e, tt=tt, b=bnk, TDp=TDp: e.tensor_copy(
                    POOLB[:, TDp + (tt - 1) * 128:TDp + (tt + 1) * 128],
                    b[:, 0:256]),
                   reads=[bres], writes=[f"td{pb}"])

    def stageC(h):
        pb = h % 2
        TDp, TGp = TDb[pb], TGb[pb]
        kb = KVSB[pb]
        for i in range(NB):
            dbk = 2 + i % 2
            for j in range(2):
                op("tensor", lambda e, i=i, j=j, TGp=TGp, TDp=TDp, dbk=dbk: e.matmul(
                    PS[dbk][:, 0:HD],
                    POOLB[:, TGp + (2 * i + j) * 128:TGp + (2 * i + j + 1) * 128],
                    POOLB[:, TDp + (2 * i + j) * 128:TDp + (2 * i + j + 1) * 128],
                    start=(j == 0), stop=(j == 1)),
                   reads=[f"tg{pb}", f"td{pb}"], writes=[f"PS{dbk}"])
            if i == 0:
                op("vector", lambda e, h=h, dbk=dbk: e.tensor_copy(
                    POOLA[:, KVc + h * HD:KVc + (h + 1) * HD], PS[dbk][:, 0:HD]),
                   reads=[f"PS{dbk}"], writes=[f"kv{h}"])
            else:
                op("vector", lambda e, h=h, dbk=dbk: e.scalar_tensor_tensor(
                    POOLA[:, KVc + h * HD:KVc + (h + 1) * HD],
                    POOLA[:, KVc + h * HD:KVc + (h + 1) * HD], BDB[:, h:h + 1],
                    PS[dbk][:, 0:HD], ALU.mult, ALU.add),
                   reads=[f"kv{h}", "bdb", f"PS{dbk}"], writes=[f"kv{h}"])
            if i < NB - 1:
                op("vector", lambda e, h=h, i=i, kb=kb: e.tensor_copy(
                    POOLB[:, kb + i * HD:kb + (i + 1) * HD],
                    POOLA[:, KVc + h * HD:KVc + (h + 1) * HD]),
                   reads=[f"kv{h}"], writes=[f"kvsb{pb}"],
                   extra_deps=(sqgbar if h < 2 else []))

    def stageD(h):
        pb = h % 2
        TDp, TEp, QKp = TDb[pb], TEb[pb], QKb[pb]
        kb = KVSB[pb]
        for i in range(NB):
            bank = 5 if i % 2 == 0 else 6
            if i > 0:
                op("tensor", lambda e, i=i, TEp=TEp, kb=kb, bank=bank: e.matmul(
                    PS[bank][:, 0:BLK], POOLB[:, kb + (i - 1) * HD:kb + i * HD],
                    POOLB[:, TEp + i * BLK:TEp + (i + 1) * BLK],
                    start=True, stop=False),
                   reads=[f"kvsb{pb}", f"te{pb}"], writes=[f"PS{bank}"])
            for j in range(2):
                op("tensor", lambda e, i=i, j=j, TDp=TDp, QKp=QKp, bank=bank: e.matmul(
                    PS[bank][:, 0:BLK],
                    POOLB[:, TDp + (2 * i + j) * 128:TDp + (2 * i + j + 1) * 128],
                    POOLB[:, QKp + (2 * i + j) * BLK:QKp + (2 * i + j + 1) * BLK],
                    start=(i == 0 and j == 0), stop=(j == 1)),
                   reads=[f"td{pb}", f"qkd{pb}"], writes=[f"PS{bank}"])
            op("vector", lambda e, h=h, i=i, bank=bank: e.tensor_copy(
                O[:, h * T + i * BLK:h * T + (i + 1) * BLK], PS[bank][:, 0:BLK]),
               reads=[f"PS{bank}"], writes=[f"o{h}"])

    sqgbar = B.barrier("xdbg")
    for h in range(NH):
        stageA1(h)
        if h >= 1 and "cd" not in _skip:
            stageC(h - 1)
            stageD(h - 1)
        stageA2(h)
    if "cd" not in _skip:
        stageC(NH - 1)
        stageD(NH - 1)

    if cfg.debug:
        dump_fm(dbg["d_o1"], lambda ccc, w0: O[:, ccc * T + w0: ccc * T + w0 + MW],
                NH, T, lambda ccc: f"o{ccc}", chan="dbg2")

    if upto == "attn":
        return _finish()

    def kcbar(kc):
        c0, c1_ = Hb + kc * FTS, Hb + (kc + 1) * FTS
        bars = []
        if c0 < QCb + NH * T and c1_ > QCb:
            for hh in range(max(0, (c0 - QCb) // T), min(NH, (c1_ - QCb + T - 1) // T)):
                bars += B.barrier(f"qc{hh}")
        if c0 < TR0 + 16 * T and c1_ > TR0:
            for rn in ("ta", "tb", "tc", "td", "te", "tg", "qkd"):
                bars += B.barrier(rn)
            bars += B.barrier("xdbg")
        if c0 < KVIb + CCW and c1_ > KVIb:
            bars += B.barrier("kvin")
        return bars

    def emit_f1(th, kcs):
        tw = th * FTS
        for kc in kcs:
            bank = 2 + kc % 2
            wt = wstream(w1_in[kc, :, :], quant=1, piece=512)
            for fcc in range(FC):
                wap, wres = wt(fcc)
                op("tensor", lambda e, w=wap, f=fcc, bank=bank, tw=tw: e.matmul(
                    PS[bank][:, 0:FTS], w,
                    XN[:, f * T + tw:f * T + tw + FTS],
                    start=(f == 0), stop=(f == FC - 1)),
                   reads=wres + [f"xn{fcc}"], writes=[f"PS{bank}"])
            op("scalar", lambda e, kc=kc, bank=bank: e.activation(
                POOLB[:, Hb + kc * FTS:Hb + (kc + 1) * FTS], PS[bank][:, 0:FTS], AF.Gelu,
                bias=B1C[:, kc:kc + 1], scale=1.0),
               reads=[f"PS{bank}", "b1c"], extra_deps=(kcbar(kc) if th == 0 else []),
               writes=[f"h{kc}"])

    F1SAFE = [kc for kc in range(KC)
              if (Hb + (kc + 1) * FTS <= QCb or Hb + kc * FTS >= QCb + NH * T)
              and (Hb + (kc + 1) * FTS <= KVIb or Hb + kc * FTS >= KVIb + CCW)]
    F1LATE = [kc for kc in range(KC) if kc not in F1SAFE]

    op("vector", lambda e: e.tensor_copy(POOLB[:, KVIb:KVIb + CCW], POOLA[:, KVc:KVc + CCW]),
       reads=[f"kv{h}" for h in range(NH)],
       extra_deps=B.barrier("qkd") + B.barrier("tg") + B.barrier("td"),
       writes=["kvin"])
    op("gpsimd", lambda e: e.dma_start(out=cc_in[:, :], in_=POOLB[:, KVIb:KVIb + CCW]),
       reads=["kvin"], writes=["ccin"], dma="cc")
    if _os.environ.get("NOCC"):
        cctok = op("gpsimd", lambda e: e.dma_start(out=cc_out[0:P, :], in_=cc_in[:, :]),
                   reads=["ccin"], writes=["ccout"], dma="nocc1")
    else:
        cctok = op("gpsimd", lambda e: e.collective_compute(
            "AllGather", ALU.bypass, replica_groups=[list(range(8))],
            ins=[cc_in.ap()], outs=[cc_out.ap()]),
            reads=["ccin"], writes=["ccout"], dma="ccol1")
    emit_f1(0, F1SAFE[:8])
    if upto == "cc":
        return _finish()
    if cfg.debug:
        op("vector", lambda e: e.tensor_copy(xbuf(3)[0:1, 0:2], SCAL[0:1, 2:4]),
           reads=["scal2"], writes=["xdbg"])
        op("gpsimd", lambda e: e.dma_start(out=dbg["d_invs"][:, :], in_=xbuf(3)[0:1, 0:2]),
           reads=["xdbg"], dma="dbg3")

    kvbar = B.barrier("kv")
    KPW = SLOTW * 2
    NKH = (CCW + KPW - 1) // KPW
    for r in range(8):
        for hf in range(NKH):
            w = min(KPW, CCW - hf * KPW)
            s = (r * NKH + hf) % NSLOT
            op("gpsimd", lambda e, r=r, s=s, hf=hf, w=w: e.dma_start(
                out=WL[s].bitcast(BF16)[:, 0:w],
                in_=cc_out[r * P:(r + 1) * P, hf * KPW:hf * KPW + w]),
               writes=[f"wl{s}"], extra_deps=[cctok], reads=["ccout"], dma=f"kv{s}")
            acc_dst = POOLA[:, KVc + (r % 2) * CCW + hf * KPW: KVc + (r % 2) * CCW + hf * KPW + w]
            acc_src = POOLA[:, KVc + ((r + 1) % 2) * CCW + hf * KPW: KVc + ((r + 1) % 2) * CCW + hf * KPW + w]
            if r == 0:
                op("vector", lambda e, s=s, d=acc_dst, w=w: e.tensor_scalar_mul(
                    d, WL[s].bitcast(BF16)[:, 0:w], KVSELB[:, 0:1]),
                   reads=[f"wl{s}", "kvselb"], extra_deps=kvbar,
                   writes=["acc0"])
            else:
                op("vector", lambda e, s=s, r=r, d=acc_dst, a=acc_src, w=w: e.scalar_tensor_tensor(
                    d, WL[s].bitcast(BF16)[:, 0:w], KVSELB[:, r:r + 1], a,
                    ALU.mult, ALU.add),
                   reads=[f"wl{s}", "kvselb", f"acc{(r + 1) % 2}"], writes=[f"acc{r % 2}"])
    op("vector", lambda e: e.tensor_copy(POOLB[:, KVIb:KVIb + CCW],
                                         POOLA[:, KVc + CCW:KVc + 2 * CCW]),
       reads=["acc1", "kvin"], writes=["kvin"])
    emit_f1(0, F1SAFE[8:])
    if upto == "kvin":
        return _finish()
    if cfg.debug:
        pass

    for h in range(NH):
        for mt in range(NMW):
            bank = mt
            op("tensor", lambda e, h=h, m=mt, bank=bank: e.matmul(
                PS[bank][:, 0:MW], POOLB[:, KVIb + h * HD:KVIb + (h + 1) * HD],
                POOLB[:, QCb + h * T + m * MW:QCb + h * T + (m + 1) * MW],
                start=True, stop=True),
               reads=["kvin", f"qc{h}"], writes=[f"PS{bank}"])
            op("vector", lambda e, h=h, m=mt, bank=bank: e.tensor_add(
                O[:, h * T + m * MW:h * T + (m + 1) * MW],
                O[:, h * T + m * MW:h * T + (m + 1) * MW], PS[bank][:, 0:MW]),
               reads=[f"PS{bank}", f"o{h}"], writes=[f"o{h}"])

    if upto == "corr":
        return _finish()

    for h in range(NH):
        op("scalar", lambda e, h=h: e.activation(POOLB[:, SQb:SQb + T], O[:, h * T:(h + 1) * T],
                                                 AF.Square, bias=0.0, scale=1.0),
           reads=[f"o{h}", "sqb"], writes=["sqb"],
           extra_deps=(B.barrier("kvsb") if h == 0 else []))
        for mt in range(NMW):
            op("tensor", lambda e, h=h, m=mt: e.matmul(
                PS[5 + m][0:1, 0:MW], ONESC, POOLB[:, SQb + m * MW:SQb + (m + 1) * MW],
                start=(h == 0), stop=(h == NH - 1)),
               reads=["sqb", "cb"], writes=[f"PS{5 + mt}"])
    for mt in range(NMW):
        op("vector", lambda e, m=mt: e.tensor_scalar(
            rowm(0, m), PS[5 + m][0:1, 0:MW], 1.0 / (NH * HD), 1e-6, ALU.mult, ALU.add),
           reads=[f"PS{5 + mt}"], extra_deps=B.barrier("row_"),
           writes=["rms_v"])
    rsqrt_row(0, 3, 2, "rms_v", "rms_r", "rms_t")
    if cfg.debug:
        op("vector", lambda e: e.tensor_copy(xbuf(3)[0:1, 0:min(T, MW)], row(3)[0:1, 0:min(T, MW)]),
           reads=["rms_r"], writes=["xdbg"])
        op("gpsimd", lambda e: e.dma_start(out=dbg["d_rstd"][0:1, 0:min(T, MW)], in_=xbuf(3)[0:1, 0:min(T, MW)]),
           reads=["xdbg"], dma="dbg3")

    for mt in range(NMW):
        op("tensor", lambda e, m=mt: e.matmul(PS[4 + m][:, 0:MW], ONESR, rowm(3, m),
                                              start=True, stop=True),
           reads=["rms_r", "cf"], writes=[f"PS{4 + mt}"])

    if upto == "rms":
        return _finish()

    for gc in range(NH):
        gp = gc % 2
        gb = 2 * gp
        gtc = GTb
        gsl = 2 if (gp == 0 or cfg.debug) else 3
        wt = wstream(wg_in[gc, :, :], piece=2048)
        for fcc in range(FC):
            wap, wres = wt(fcc)
            for mt in range(NMW):
                op("tensor", lambda e, w=wap, f=fcc, m=mt, gb=gb: e.matmul(
                    PS[gb + m][:, 0:MW], w,
                    XN[:, f * T + m * MW:f * T + m * MW + MW],
                    start=(f == 0), stop=(f == FC - 1)),
                   reads=wres + [f"xn{fcc}"], writes=[f"PS{gb + mt}"])
        for mt in range(NMW):
            op("scalar", lambda e, m=mt, gb=gb, gtc=gtc: e.activation(
                POOLB[:, gtc + m * MW:gtc + (m + 1) * MW], PS[gb + m][:, 0:MW], AF.Sigmoid,
                bias=0.0, scale=1.0),
               reads=[f"PS{gb + mt}"], writes=["gt"],
               extra_deps=((B.barrier("sqb") + B.barrier("kvsb")) if gc < 2 and mt == 0 else []))
            op("vector", lambda e, m=mt, gc=gc, gsl=gsl: e.tensor_mul(
                xbuf(gsl), O[:, gc * T + m * MW:gc * T + (m + 1) * MW], PS[4 + m][:, 0:MW]),
               reads=[f"o{gc}", f"PS{4 + mt}"],
               extra_deps=(B.barrier("acc") + B.barrier("xdbg") if gc < 2 and mt == 0 else []),
               writes=[f"gsc{gp}"])
            op("vector", lambda e, m=mt, gc=gc, gsl=gsl: e.tensor_scalar_mul(
                xbuf(gsl), xbuf(gsl), NWC[:, gc:gc + 1]),
               reads=[f"gsc{gp}", "nwc"], writes=[f"gsc{gp}"])
            op("vector", lambda e, m=mt, gc=gc, gsl=gsl, gtc=gtc: e.tensor_mul(
                O[:, gc * T + m * MW:gc * T + (m + 1) * MW], xbuf(gsl),
                POOLB[:, gtc + m * MW:gtc + (m + 1) * MW]),
               reads=[f"gsc{gp}", "gt"], writes=[f"o{gc}"])

    if cfg.debug:
        dump_fm(dbg["d_og"], lambda ccc, w0: O[:, ccc * T + w0: ccc * T + w0 + MW],
                NH, T, lambda ccc: f"o{ccc}", chan="dbg4")

    if upto == "mid":
        return _finish()

    NKQ = 4
    KCQ = KC // NKQ
    sbar = B.barrier("row_") + B.barrier("rms_") + B.barrier("acc")
    for th in range(2):
        tw = th * FTS
        for fcc in range(FC):
            bank = fcc % 2
            wt = wstream(wo_in[fcc, :, :], piece=2048)
            for oc in range(NH):
                wap, wres = wt(oc)
                op("tensor", lambda e, w=wap, oc=oc, bank=bank, tw=tw: e.matmul(
                    PS[bank][:, 0:FTS], w,
                    O[:, oc * T + tw:oc * T + tw + FTS],
                    start=(oc == 0), stop=(oc == NH - 1)),
                   reads=wres + [f"o{oc}"], writes=[f"PS{bank}"])
            op("vector", lambda e, f=fcc, bank=bank: e.tensor_copy(
                S_ap(f * FTS, FTS), PS[bank][:, 0:FTS]),
               reads=[f"PS{bank}"], extra_deps=(sbar if th == 0 else []),
               writes=[f"s{fcc}"])
        if cfg.debug:
            for fcc in range(FC):
                op("scalar", lambda e, f=fcc: e.activation(
                    xbuf(3)[:, 0:FTS], S_ap(f * FTS, FTS), AF.Identity,
                    bias=0.0, scale=1.0),
                   reads=[f"s{fcc}"], writes=["xdbg"])
                op("gpsimd", lambda e, f=fcc, tw=tw: e.dma_start(
                    out=dbg["d_s"][f * 128:(f + 1) * 128, tw:tw + FTS], in_=xbuf(3)[:, 0:FTS]),
                   reads=["xdbg"], dma="dbg5")
        emit_f1(th, F1LATE if th == 0 else list(range(KC)))
        for dc in range(FC):
            bank = 4 + dc % 2
            for q in range(NKQ):
                wt = wstream(w2_in[dc * NKQ + q, :, :], quant=2, piece=512)
                for kq in range(KCQ):
                    kc = q * KCQ + kq
                    wap, wres = wt(kq)
                    op("tensor", lambda e, w=wap, kc=kc, bank=bank: e.matmul(
                        PS[bank][:, 0:FTS], w,
                        POOLB[:, Hb + kc * FTS:Hb + (kc + 1) * FTS],
                        start=(kc == 0), stop=(kc == KC - 1)),
                       reads=wres + [f"h{kc}"], writes=[f"PS{bank}"])
            op("scalar", lambda e, dc=dc, bank=bank: e.activation(
                xbuf(2)[:, 0:FTS], PS[bank][:, 0:FTS], AF.Silu,
                bias=B2C[:, dc:dc + 1], scale=1.0),
               reads=[f"PS{bank}", "b2c"], writes=["gsc"])
            op("gpsimd", lambda e, dc=dc, tw=tw: e.dma_start(
                out=xbuf(0)[:, 0:FTS], in_=x_in[dc * 128:(dc + 1) * 128, tw:tw + FTS]),
               writes=["xb0"],
               extra_deps=(B.barrier("acc") if (th == 0 and dc == 0) else []),
               dma="xf")
            op("vector", lambda e, dc=dc: e.tensor_mul(
                xbuf(2)[:, 0:FTS], xbuf(2)[:, 0:FTS], S_ap(dc * FTS, FTS)),
               reads=["gsc", f"s{dc}"], writes=["gsc"])
            op("vector", lambda e, dc=dc: e.tensor_add(
                xbuf(1)[:, 0:FTS], xbuf(2)[:, 0:FTS], xbuf(0)[:, 0:FTS]),
               reads=["gsc", "xb0", "outb"], writes=["outb"])
            op("gpsimd", lambda e, dc=dc, tw=tw: e.dma_start(
                out=out_t[dc * 128:(dc + 1) * 128, tw:tw + FTS], in_=xbuf(1)[:, 0:FTS]),
               reads=["outb"], dma="out")

    fin = B.dma_sem("out")
    op("gpsimd", lambda e: e.nop(), extra_deps=[(fin, B.dcount["out"], "gpsimd", True)])

    return _finish()


def _xbuf_doc():
    pass


def prep_in_maps(cfg, L0_attn_hidden_states, L0_attn_slope_rate, L0_attn_w_qkv,
                 L0_attn_w_gate, L0_attn_w_out, L0_attn_norm_w,
                 L0_ffn_w1, L0_ffn_b1, L0_ffn_w2, L0_ffn_b2):
    P, D, T, NH, HD, DFF = cfg.P, cfg.D, cfg.T, cfg.NH, cfg.HD, cfg.DFF
    FC, KC = cfg.FC, cfg.KC
    f32 = np.float32
    x = np.ascontiguousarray(np.asarray(L0_attn_hidden_states, dtype=f32)).reshape(-1, D)
    wq_t = np.ascontiguousarray(np.asarray(L0_attn_w_qkv, dtype=f32)
                                .reshape(NH, 3, HD, FC, P)[:, 0]
                                .transpose(0, 3, 2, 1)).reshape(NH, P, FC * 128)
    wk_t = np.ascontiguousarray(np.asarray(L0_attn_w_qkv, dtype=f32)
                                .reshape(NH, 3, HD, FC, P)[:, 1]
                                .transpose(0, 3, 2, 1)).reshape(NH, P, FC * 128)
    wv_t = np.ascontiguousarray(np.asarray(L0_attn_w_qkv, dtype=f32)
                                .reshape(NH, 3, HD, FC, P)[:, 2]
                                .transpose(0, 3, 2, 1)).reshape(NH, P, FC * 128)
    wg_t = np.ascontiguousarray(np.asarray(L0_attn_w_gate, dtype=f32)
                                .reshape(FC, 128, FC, P).transpose(0, 3, 2, 1)).reshape(FC, P, FC * 128)
    wo_t = np.ascontiguousarray(np.asarray(L0_attn_w_out, dtype=f32)
                                .reshape(FC, 128, FC, P).transpose(0, 3, 2, 1)).reshape(FC, P, FC * 128)
    w1_t = np.ascontiguousarray(np.asarray(L0_ffn_w1, dtype=f32)
                                .reshape(KC, 128, FC, P).transpose(0, 3, 2, 1)).reshape(KC, P, FC * 128)
    NKQ = 4
    KCQ = KC // NKQ
    w2r = np.asarray(L0_ffn_w2, dtype=f32).reshape(FC, 128, KC, P)
    w2_t = np.ascontiguousarray(
        w2r.transpose(0, 2, 3, 1)
        .reshape(FC, NKQ, KCQ, P, 128)
        .transpose(0, 1, 3, 2, 4)
    ).reshape(FC * NKQ, P, KCQ * 128)
    b1_t = np.ascontiguousarray(np.asarray(L0_ffn_b1, dtype=f32).reshape(KC, P).T)
    b2_t = np.ascontiguousarray(np.asarray(L0_ffn_b2, dtype=f32).reshape(FC, P).T)
    nw_t = np.ascontiguousarray(np.asarray(L0_attn_norm_w, dtype=f32).reshape(FC, P).T)
    slopes = np.ascontiguousarray(np.asarray(L0_attn_slope_rate, dtype=f32).reshape(1, NH))

    n1t = max(1, (DFF * D // 8) // (128 * 2048))
    c1 = (DFF * D // 8) // (128 * n1t)
    w1f = np.asarray(L0_ffn_w1, dtype=f32).reshape(-1)
    w2f = np.asarray(L0_ffn_w2, dtype=f32).reshape(-1)

    in_maps = []
    ncore = cfg.n_cores
    for c in range(ncore):
        xs = np.ascontiguousarray(x[c * T:(c + 1) * T].T)
        sel = np.zeros((1, 8), dtype=f32)
        if c % 2 == 1:
            sel[0, c - 1] = 1.0
        sz = DFF * D // 8
        m = {
            "x_fm": xs, "wq_t": wq_t, "wk_t": wk_t, "wv_t": wv_t,
            "wg_t": wg_t, "wo_t": wo_t, "w1_t": w1_t, "w2_t": w2_t,
            "w1sl": np.ascontiguousarray(w1f[c * sz:(c + 1) * sz].reshape(n1t, P, c1)),
            "w2sl": np.ascontiguousarray(w2f[c * sz:(c + 1) * sz].reshape(n1t, P, c1)),
            "b1_t": b1_t, "b2_t": b2_t, "nw_t": nw_t, "slopes": slopes,
            "kvsel": sel,
        }
        in_maps.append(m)
    return in_maps


def kernel(**inputs):
    cfg = CFG(D=2048, T=1024, NH=16, DFF=8192)
    in_maps = prep_in_maps(cfg, **inputs)
    nc, _ = build_nc(cfg)
    res = run_bass_kernel_spmd(nc, in_maps, core_ids=list(range(cfg.n_cores)))
    outs = [np.ascontiguousarray(r["out_fm"].T) for r in res.results]
    full = np.concatenate(outs, axis=0)
    return full.reshape(4, 2048, 2048)


# version 08
# speedup vs baseline: 3.0172x; 1.0293x over previous
import os as _os
import numpy as np
import ml_dtypes
from contextlib import ExitStack
from collections import defaultdict

import concourse.bass as bass
import concourse.mybir as mybir
from concourse.bass_utils import run_bass_kernel_spmd

F32 = mybir.dt.float32
BF16 = mybir.dt.bfloat16
AF = mybir.ActivationFunctionType
ALU = mybir.AluOpType
AX = mybir.AxisListType

MAGIC = float(np.float32(1.5 * 2 ** 23))


class PB:
    ENGS = ("sync", "vector", "scalar", "tensor", "gpsimd")

    def __init__(self, nc, ctx):
        self.nc = nc
        self.ctx = ctx
        self.items = {e: [] for e in self.ENGS}
        self.esem = {e: ctx.enter_context(nc.semaphore(f"S_{e}")) for e in self.ENGS}
        self.dsem = {}
        self.ecount = {e: 0 for e in self.ENGS}
        self.dcount = defaultdict(int)
        self.res_w = {}
        self.res_r = defaultdict(list)

    def dma_sem(self, name):
        if name not in self.dsem:
            self.dsem[name] = self.ctx.enter_context(self.nc.semaphore(f"D_{name}"))
        return self.dsem[name]

    def op(self, eng, fn, reads=(), writes=(), extra_deps=(), dma=None, tag=None):
        deps = set()
        for r in reads:
            t = self.res_w.get(r)
            if t is not None:
                deps.add(t)
        for r in writes:
            t = self.res_w.get(r)
            if t is not None:
                deps.add(t)
            for t2 in self.res_r.get(r, ()):
                deps.add(t2)
        for t in extra_deps:
            if t is not None:
                deps.add(t)
        if dma is None:
            sem = self.esem[eng]
            self.ecount[eng] += 1
            tok = (sem, self.ecount[eng], eng, False)
        else:
            step = 1 if dma.startswith("ccol") else 16
            sem = self.dma_sem(dma)
            self.dcount[dma] += step
            tok = (sem, self.dcount[dma], eng, True)
        deps.discard(tok)
        self.items[eng].append((fn, sorted(deps, key=lambda t: (t[0].name, t[1])),
                               sem, (1 if (dma and dma.startswith("ccol")) else (16 if dma else 1)), tag))
        for r in reads:
            self.res_r[r].append(tok)
        for r in writes:
            self.res_w[r] = tok
            self.res_r[r] = [tok]
        return tok

    def barrier(self, prefix):
        toks = set()
        for r in set(list(self.res_w) + list(self.res_r)):
            if r.startswith(prefix):
                t = self.res_w.get(r)
                if t is not None:
                    toks.add(t)
                for t2 in self.res_r.get(r, ()):
                    toks.add(t2)
        return list(toks)

    def emit(self):
        nc = self.nc
        stats = {e: {"ops": len(self.items[e]), "waits": 0} for e in self.ENGS}
        with nc.Block() as block:
            for eng in self.ENGS:
                items = self.items[eng]
                if not items:
                    continue

                def make(eng=eng, items=items):
                    def f(proxy):
                        waited = {}
                        for (fn, deps, sem, inc, tag) in items:
                            for (dsm, val, peng, isdma) in deps:
                                if peng == "tensor" and eng == "tensor" and not isdma:
                                    continue
                                if waited.get(dsm.name, 0) >= val:
                                    continue
                                proxy.wait_ge(dsm, val)
                                waited[dsm.name] = val
                                stats[eng]["waits"] += 1
                            fn(proxy).then_inc(sem, inc)
                    return f

                getattr(block, eng)(make())
        return stats


class CFG:
    def __init__(self, D=2048, T=1024, NH=16, DFF=8192, debug=False):
        self.P = 128
        self.D, self.T, self.NH, self.HD, self.DFF = D, T, NH, 128, DFF
        self.BLK = 256
        self.FTS = T // 2
        self.n_cores = 8
        self.debug = debug
        self.FC = D // 128
        self.KC = DFF // 128
        self.NB = T // self.BLK
        self.NTC = T // 128
        self.MW = min(512, T)
        self.NMW = T // self.MW
        assert T % self.BLK == 0


def build_nc(cfg, upto='all'):
    P, D, T, NH, HD, DFF = cfg.P, cfg.D, cfg.T, cfg.NH, cfg.HD, cfg.DFF
    FC, KC, NB, BLK, NTC = cfg.FC, cfg.KC, cfg.NB, cfg.BLK, cfg.NTC
    FTS, MW, NMW = cfg.FTS, cfg.MW, cfg.NMW
    CCW = NH * HD

    nc = bass.Bass("TRN2", target_bir_lowering=False)

    x_in = nc.dram_tensor("x_fm", [D, T], F32, kind="ExternalInput")
    wq_in = nc.dram_tensor("wq_t", [NH, P, FC * 128], F32, kind="ExternalInput")
    wk_in = nc.dram_tensor("wk_t", [NH, P, FC * 128], F32, kind="ExternalInput")
    wv_in = nc.dram_tensor("wv_t", [NH, P, FC * 128], F32, kind="ExternalInput")
    wg_in = nc.dram_tensor("wg_t", [FC, P, FC * 128], F32, kind="ExternalInput")
    wo_in = nc.dram_tensor("wo_t", [FC, P, FC * 128], F32, kind="ExternalInput")
    w1_in = nc.dram_tensor("w1_t", [KC, P, FC * 128], F32, kind="ExternalInput")
    w2_in = nc.dram_tensor("w2_t", [FC * 4, P, KC * 32], F32, kind="ExternalInput")
    n1t = max(1, (DFF * D // 8) // (128 * 2048))
    c1 = (DFF * D // 8) // (128 * n1t)
    w1sl_in = nc.dram_tensor("w1sl", [n1t, P, c1], F32, kind="ExternalInput")
    w2sl_in = nc.dram_tensor("w2sl", [n1t, P, c1], F32, kind="ExternalInput")
    b1_in = nc.dram_tensor("b1_t", [P, KC], F32, kind="ExternalInput")
    b2_in = nc.dram_tensor("b2_t", [P, FC], F32, kind="ExternalInput")
    nw_in = nc.dram_tensor("nw_t", [P, FC], F32, kind="ExternalInput")
    slo_in = nc.dram_tensor("slopes", [1, NH], F32, kind="ExternalInput")
    kvsel_in = nc.dram_tensor("kvsel", [1, 8], F32, kind="ExternalInput")
    out_t = nc.dram_tensor("out_fm", [D, T], F32, kind="ExternalOutput")

    w1q_s = nc.dram_tensor("w1q_s", [KC, P, FC * 128], BF16, kind="Internal")
    w2q_s = nc.dram_tensor("w2q_s", [FC * 4, P, (KC // 4) * 128], BF16, kind="Internal")
    cc_in = nc.dram_tensor("cc_in", [P, CCW], BF16, kind="Internal")
    cc_out = nc.dram_tensor("cc_out", [8 * P, CCW], BF16, kind="Internal")
    cc_in2 = nc.dram_tensor("cc_in2", [16, 128], F32, kind="Internal")
    cc_out2 = nc.dram_tensor("cc_out2", [128, 128], F32, kind="Internal")

    arrq_np = (np.arange(BLK) + 1).astype(np.float32)
    arrk_np = (BLK - 1 - np.arange(BLK)).astype(np.float32)
    dm = np.arange(BLK)[None, :] - np.arange(P)[:, None]
    didx_np = np.concatenate([np.maximum(dm - j * P, 0) for j in range(BLK // P)], 1)
    dmsk_np = np.concatenate([(dm - j * P >= 0) for j in range(BLK // P)], 1)
    cb_np = np.concatenate([
        np.broadcast_to(arrq_np, (P, BLK)),
        np.broadcast_to(arrk_np, (P, BLK)),
        didx_np.astype(np.float32),
        dmsk_np.astype(np.float32),
        np.eye(P, dtype=np.float32),
        np.ones((P, 1), dtype=np.float32),
    ], axis=1).astype(ml_dtypes.bfloat16)
    cf_np = np.concatenate([
        np.ones((P, P), dtype=np.float32),
        np.broadcast_to((np.arange(NB) * BLK).astype(np.float32), (P, NB)),
        np.full((P, 1), float(BLK), np.float32),
        np.full((P, 1), -MAGIC, np.float32),
    ], axis=1)
    cb_t = nc.inline_tensor(cb_np, name="c_bf")
    cf_t = nc.inline_tensor(np.ascontiguousarray(cf_np), name="c_f32")

    dbg = {}
    if cfg.debug:
        for nm, shp in [("d_xn", [D, T]), ("d_o1", [NH * HD, T]), ("d_og", [NH * HD, T]),
                        ("d_s", [D, T]), ("d_cc", [P, CCW + 2]), ("d_rstd", [1, T]),
                        ("d_invs", [1, 2]), ("d_kvin", [P, CCW])]:
            dbg[nm] = nc.dram_tensor(nm, shp, F32, kind="ExternalOutput")

    ctx = ExitStack()
    SB = lambda n, s, d=F32: ctx.enter_context(nc.sbuf_tensor(n, s, d))

    XN = SB("XN", [P, FC * T], BF16)
    O = SB("O", [P, NH * T], BF16)
    NA = 2048 + max(FC * FTS // 2, 6 * T)
    POOLA = SB("POOLA", [P, NA], F32)
    KVc = 0
    Sc = 2048

    def S_ap(c0, w):
        return POOLA[:, :].bitcast(BF16)[:, 2 * Sc + c0:2 * Sc + c0 + w]
    NBC = max(KC * FTS, NH * T + 16 * T + max(T, 2 * NB * HD, NMW * MW), 2048)
    POOLB = SB("POOLB", [P, NBC], BF16)
    QCb = 0
    TR0 = NH * T
    TAb = [TR0 + pb * 8 * T for pb in range(2)]
    TBb = [TR0 + pb * 8 * T + T for pb in range(2)]
    TCb = [TR0 + pb * 8 * T + 2 * T for pb in range(2)]
    TDb = [TR0 + pb * 8 * T + 3 * T for pb in range(2)]
    TEb = [TR0 + pb * 8 * T + 4 * T for pb in range(2)]
    TGb = [TR0 + pb * 8 * T + 5 * T for pb in range(2)]
    QKb = [TR0 + pb * 8 * T + 6 * T for pb in range(2)]
    SQb = TR0 + 16 * T
    GTb = SQb
    KVIb = QKb[0]
    XQb = 0
    Hb = 0
    ENDB = TR0 + 16 * T + max(T, 2 * NB * HD, NMW * MW)
    assert ENDB <= NBC, ENDB

    NWC = SB("NWC", [P, FC], F32)
    B1C = SB("B1C", [P, KC], F32)
    B2C = SB("B2C", [P, FC], F32)
    SLO = SB("SLO", [1, NH], F32)
    NEGR = SB("NEGR", [1, NH], F32)
    NEGSB = SB("NEGSB", [P, NH], F32)
    BDB = SB("BDB", [P, NH], F32)
    KVSEL = SB("KVSEL", [1, 8], F32)
    KVSELB = SB("KVSELB", [P, 8], F32)
    SCAL = SB("SCAL", [1, 4], F32)
    PART8 = SB("PART8", [8, 2], F32)
    INVSB = SB("INVSB", [P, 2], F32)
    WSUM = SB("WSUM", [P, 4], F32)
    KVBF = SB("KVBF", [P, HD], BF16)
    PSTG = SB("PSTG", [P, 128], F32)
    NSLOT = 10
    SLOTW = 512
    WLBUF = SB("WLBUF", [P, NSLOT * SLOTW], F32)
    WBBUF = SB("WBBUF", [P, NSLOT * SLOTW], BF16)
    WL = [WLBUF[:, i * SLOTW:(i + 1) * SLOTW] for i in range(NSLOT)]
    WB = [WBBUF[:, i * SLOTW:(i + 1) * SLOTW] for i in range(NSLOT)]
    CONSTB = SB("CONSTB", [P, 6 * BLK + P + 1], BF16)
    CONSTF = SB("CONSTF", [P, P + NB + 2], F32)
    ARRQ = CONSTB[:, 0:BLK]
    ARRK = CONSTB[:, BLK:2 * BLK]
    DIDX = CONSTB[:, 2 * BLK:4 * BLK]
    DMSK = CONSTB[:, 4 * BLK:6 * BLK]
    IDENT = CONSTB[:, 6 * BLK:6 * BLK + P]
    ONESC = CONSTB[:, 6 * BLK + P:6 * BLK + P + 1]
    ONESR = CONSTF[0:1, 0:P]
    ONESCF = CONSTF[:, 0:1]
    BLKI = CONSTF[:, P:P + NB]
    BLKCOL = CONSTF[:, P + NB:P + NB + 1]
    NEGM = CONSTF[:, P + NB + 1:P + NB + 2]
    DECQT = SB("DECQT", [P, 2 * BLK], BF16)
    DECKT = SB("DECKT", [P, 2 * BLK], BF16)
    DIAGT = SB("DIAGT", [P, 4 * BLK], BF16)
    BDPT = SB("BDPT", [P, 2 * NB], F32)
    DECQ = [DECQT[:, i * BLK:(i + 1) * BLK] for i in range(2)]
    DECK = [DECKT[:, i * BLK:(i + 1) * BLK] for i in range(2)]
    DIAG = [DIAGT[:, i * 2 * BLK:(i + 1) * 2 * BLK] for i in range(2)]
    BDP = [BDPT[:, i * NB:(i + 1) * NB] for i in range(2)]

    PS = [ctx.enter_context(nc.psum_tensor(f"PS{i}", [P, 512], F32)) for i in range(7)]
    PT = ctx.enter_context(nc.psum_tensor("PT", [P, 512], F32))

    B = PB(nc, ctx)
    op = B.op

    def row(k):
        return POOLA[0:1, Sc + k * T: Sc + (k + 1) * T]

    def rowm(k, mt):
        return POOLA[0:1, Sc + k * T + mt * MW: Sc + k * T + mt * MW + MW]

    def _finish():
        toks = []
        for ch, cnt in list(B.dcount.items()):
            if cnt:
                toks.append((B.dsem[ch], cnt, "sync", True))
        for eng in B.ENGS:
            if B.ecount[eng] and eng != "gpsimd":
                toks.append((B.esem[eng], B.ecount[eng], eng, False))
        op("gpsimd", lambda e: e.nop(), extra_deps=toks)
        stats = B.emit()
        ctx.close()
        return nc, stats

    op("gpsimd", lambda e: e.dma_start(out=CONSTB[:, :], in_=cb_t[:, :]),
       writes=["cb"], dma="ccb")
    op("gpsimd", lambda e: e.dma_start(out=CONSTF[:, :], in_=cf_t[:, :]),
       writes=["cf"], dma="ccf")
    for (dst, srct, res) in [(NWC, nw_in, "nwc"), (B1C, b1_in, "b1c"), (B2C, b2_in, "b2c"),
                             (SLO, slo_in, "slo"), (KVSEL, kvsel_in, "kvsel")]:
        op("gpsimd", lambda e, d=dst, s=srct: e.dma_start(out=d[:, :], in_=s[:, :]),
           writes=[res], dma=f"c_{res}")

    op("vector", lambda e: e.tensor_scalar_mul(NEGR[:, :], SLO[:, :], -1.0),
       reads=["slo"], writes=["negr"])
    for h in range(NH):
        op("tensor", lambda e, h=h: e.matmul(PS[5][:, 0:1], ONESR,
                                             NEGR[0:1, h:h + 1], start=True, stop=True),
           reads=["negr", "cf"], writes=["PS5"])
        op("vector", lambda e, h=h: e.tensor_copy(NEGSB[:, h:h + 1], PS[5][:, 0:1]),
           reads=["PS5"], writes=["negsb"])
    op("tensor", lambda e: e.matmul(PS[5][:, 0:8], ONESR, KVSEL[0:1, :],
                                    start=True, stop=True),
       reads=["kvsel", "cf"], writes=["PS5"])
    op("vector", lambda e: e.tensor_copy(KVSELB[:, :], PS[5][:, 0:8]),
       reads=["PS5"], writes=["kvselb"])
    for h in range(NH):
        op("scalar", lambda e, h=h: e.activation(BDB[:, h:h + 1], BLKCOL, AF.Exp,
                                                 bias=0.0, scale=NEGSB[:, h:h + 1]),
           reads=["negsb", "cf", "bdb"], writes=["bdb"])

    XQs = [XQb + i * 2 * T for i in range(2)]

    def xq_ap(bi, half):
        c = XQs[bi] + half * T
        return POOLB[:, c:c + T]

    def xbuf(bi):
        return POOLA[:, KVc + bi * MW: KVc + (bi + 1) * MW]

    def xrow(bi):
        return POOLA[:, KVc + bi * T: KVc + (bi + 1) * T]

    for fcc in range(FC):
        bi = fcc % 2
        op("gpsimd", lambda e, b=bi, f=fcc: e.dma_start(
            out=xrow(b), in_=x_in[f * 128:(f + 1) * 128, :]),
           writes=[f"xr{bi}"], dma=f"x{bi}")
        op("vector", lambda e, b=bi: e.tensor_copy(xq_ap(b, 0), xrow(b)),
           reads=[f"xr{bi}"], writes=[f"xc{bi}"])
        op("scalar", lambda e, b=bi: e.activation(xq_ap(b, 1), xrow(b), AF.Square,
                                                  bias=0.0, scale=1.0),
           reads=[f"xr{bi}"], writes=[f"xq{bi}"])
        for mt in range(NMW):
            op("tensor", lambda e, b=bi, f=fcc, m=mt: e.matmul(
                PS[0 + m][0:1, 0:MW], ONESC, xq_ap(b, 0)[:, m * MW:(m + 1) * MW],
                start=(f == 0), stop=(f == FC - 1)),
               reads=[f"xc{bi}", "cb"], writes=[f"PS{0 + mt}"])
            op("tensor", lambda e, b=bi, f=fcc, m=mt: e.matmul(
                PS[2 + m][0:1, 0:MW], ONESC, xq_ap(b, 1)[:, m * MW:(m + 1) * MW],
                start=(f == 0), stop=(f == FC - 1)),
               reads=[f"xq{bi}", "cb"], writes=[f"PS{2 + mt}"])
    for mt in range(NMW):
        op("vector", lambda e, m=mt: e.tensor_scalar_mul(rowm(0, m), PS[0 + m][0:1, 0:MW], 1.0 / D),
           reads=[f"PS{0 + mt}"], writes=["row_mean"])
        op("vector", lambda e, m=mt: e.tensor_scalar_mul(rowm(1, m), PS[2 + m][0:1, 0:MW], 1.0 / D),
           reads=[f"PS{2 + mt}"], writes=["row_msq"])
    op("vector", lambda e: e.tensor_mul(row(2), row(0), row(0)),
       reads=["row_mean"], writes=["row_tmp"])
    op("vector", lambda e: e.tensor_sub(row(1), row(1), row(2)),
       reads=["row_msq", "row_tmp"], writes=["row_msq"])
    op("vector", lambda e: e.tensor_scalar_add(row(1), row(1), 1e-5),
       reads=["row_msq"], writes=["row_msq"])

    def rsqrt_row(src_k, dst_k, tmp_k, src_res, dst_res, tmp_res):
        op("scalar", lambda e: e.activation(row(tmp_k), row(src_k), AF.Sqrt,
                                            bias=0.0, scale=1.0),
           reads=[src_res], writes=[tmp_res])
        op("vector", lambda e: e.reciprocal(row(dst_k), row(tmp_k)),
           reads=[tmp_res], writes=[dst_res])
        op("vector", lambda e: e.tensor_mul(row(tmp_k), row(dst_k), row(dst_k)),
           reads=[dst_res], writes=[tmp_res])
        op("vector", lambda e: e.tensor_mul(row(tmp_k), row(tmp_k), row(src_k)),
           reads=[tmp_res, src_res], writes=[tmp_res])
        op("vector", lambda e: e.tensor_scalar(row(tmp_k), row(tmp_k), -0.5, 1.5,
                                               ALU.mult, ALU.add),
           reads=[tmp_res], writes=[tmp_res])
        op("vector", lambda e: e.tensor_mul(row(dst_k), row(dst_k), row(tmp_k)),
           reads=[dst_res, tmp_res], writes=[dst_res])

    rsqrt_row(1, 3, 2, "row_msq", "row_arow", "row_tmp")
    op("vector", lambda e: e.scalar_tensor_tensor(row(4), row(0), -1.0, row(3),
                                                  ALU.mult, ALU.mult),
       reads=["row_mean", "row_arow"], writes=["row_brow"])
    BB = [PS[6], PT]
    BBres = ["PS6", "PTb"]
    for mt in range(NMW):
        op("tensor", lambda e, m=mt: e.matmul(PS[4 + m][:, 0:MW], ONESR, rowm(3, m),
                                              start=True, stop=True),
           reads=["row_arow", "cf"], writes=[f"PS{4 + mt}"])
        op("tensor", lambda e, m=mt: e.matmul(BB[m][:, 0:MW], ONESR, rowm(4, m),
                                              start=True, stop=True),
           reads=["row_brow", "cf"], writes=[BBres[mt]])
    for fcc in range(FC):
        bi = fcc % 2
        op("gpsimd", lambda e, b=bi, f=fcc: e.dma_start(
            out=xrow(b), in_=x_in[f * 128:(f + 1) * 128, :]),
           writes=[f"xr{bi}"], dma=f"x{bi}")
        for mt in range(NMW):
            op("vector", lambda e, b=bi, m=mt: e.tensor_mul(
                xrow(b)[:, m * MW:(m + 1) * MW], xrow(b)[:, m * MW:(m + 1) * MW],
                PS[4 + m][:, 0:MW]),
               reads=[f"xr{bi}", f"PS{4 + mt}"], writes=[f"xr{bi}"])
            op("vector", lambda e, b=bi, f=fcc, m=mt: e.tensor_add(
                XN[:, f * T + m * MW: f * T + (m + 1) * MW],
                xrow(b)[:, m * MW:(m + 1) * MW], BB[m][:, 0:MW]),
               reads=[f"xr{bi}", BBres[mt]], writes=[f"xn{fcc}"])

    if cfg.debug:
        def dump_fm(dramt, src_ap_fn, nchunk, width, res_of, chan="dbg"):
            for ccc in range(nchunk):
                for w0 in range(0, width, MW):
                    op("scalar", lambda e, ccc=ccc, w0=w0: e.activation(
                        xbuf(3), src_ap_fn(ccc, w0), AF.Identity, bias=0.0, scale=1.0),
                       reads=[res_of(ccc)], writes=["xdbg"])
                    op("gpsimd", lambda e, ccc=ccc, w0=w0, d=dramt: e.dma_start(
                        out=d[ccc * 128:(ccc + 1) * 128, w0:w0 + MW], in_=xbuf(3)),
                       reads=["xdbg"], dma=chan)

        dump_fm(dbg["d_xn"], lambda ccc, w0: XN[:, ccc * T + w0: ccc * T + w0 + MW],
                FC, T, lambda ccc: f"xn{ccc}")

    if upto == "p0":
        return _finish()

    wuc = [0]
    op("gpsimd", lambda e: e.memset(WSUM[:, :], 0.0), writes=["wsum", "wsumt"])
    for wi, wsl in enumerate([w1sl_in, w2sl_in]):
        for t in range(n1t):
            for c0 in range(0, c1, SLOTW):
                w = min(SLOTW, c1 - c0)
                s = wuc[0] % NSLOT
                wuc[0] += 1
                op("scalar", lambda e, s=s, t=t, wt=wsl, c0=c0, w=w: e.dma_start(
                    out=WL[s][:, 0:w], in_=wt[t, :, c0:c0 + w]),
                   writes=[f"wl{s}"], dma=f"wsl{s}")
                op("vector", lambda e, s=s, wi=wi, w=w: e.tensor_reduce(
                    WSUM[:, 2 + wi:3 + wi], WL[s][:, 0:w], AX.X, ALU.add,
                    apply_absolute_value=True),
                   reads=[f"wl{s}"], writes=["wsumt"])
                op("vector", lambda e, wi=wi: e.tensor_add(
                    WSUM[:, wi:wi + 1], WSUM[:, wi:wi + 1], WSUM[:, 2 + wi:3 + wi]),
                   reads=["wsum", "wsumt"], writes=["wsum"])
    op("tensor", lambda e: e.matmul(PS[6][0:1, 0:1], WSUM[:, 0:1], ONESCF,
                                    start=True, stop=True),
       reads=["wsum", "cf"], writes=["PS6"])
    op("tensor", lambda e: e.matmul(PS[6][0:1, 1:2], WSUM[:, 1:2], ONESCF,
                                    start=False, stop=True, skip_group_check=True),
       reads=["wsum", "cf"], writes=["PS6"])
    op("vector", lambda e: e.tensor_copy(SCAL[0:1, 0:2], PS[6][0:1, 0:2]),
       reads=["PS6"], writes=["scal"])
    op("gpsimd", lambda e: e.memset(PSTG[:, :], 0.0), writes=["pstg"])
    op("vector", lambda e: e.tensor_copy(PSTG[0:1, 0:2], SCAL[0:1, 0:2]),
       reads=["scal", "pstg"], writes=["pstg"])
    op("gpsimd", lambda e: e.dma_start(out=cc_in2[:, :], in_=PSTG[0:16, :]),
       reads=["pstg"], writes=["ccin2"], dma="cc2")
    if _os.environ.get("NOCC"):
        cc2tok = op("gpsimd", lambda e: e.dma_start(out=cc_out2[0:16, :], in_=cc_in2[:, :]),
                    reads=["ccin2"], writes=["ccout2"], dma="nocc2")
    else:
        cc2tok = op("gpsimd", lambda e: e.collective_compute(
            "AllGather", ALU.bypass, replica_groups=[list(range(8))],
            ins=[cc_in2.ap()], outs=[cc_out2.ap()]),
            reads=["ccin2"], writes=["ccout2"], dma="ccol2")
    op("gpsimd", lambda e: e.dma_start(
        out=PART8[:, :], in_=cc_out2.ap().rearrange("(r p) c -> r p c", p=16)[:, 0, 0:2]),
       reads=["ccout2"], extra_deps=[cc2tok], writes=["part8"], dma="p8")
    op("tensor", lambda e: e.matmul(PS[6][0:1, 0:1], PART8[:, 0:1], ONESCF[0:8, 0:1],
                                    start=True, stop=True),
       reads=["part8", "cf"], writes=["PS6"])
    op("tensor", lambda e: e.matmul(PS[6][0:1, 1:2], PART8[:, 1:2], ONESCF[0:8, 0:1],
                                    start=False, stop=True, skip_group_check=True),
       reads=["part8", "cf"], writes=["PS6"])
    op("vector", lambda e, n=float(DFF * D): e.tensor_scalar_mul(
        SCAL[0:1, 2:4], PS[6][0:1, 0:2], 1.0 / n),
       reads=["PS6"], writes=["scal2"])
    op("vector", lambda e: e.tensor_scalar_max(SCAL[0:1, 2:4], SCAL[0:1, 2:4], 1e-8),
       reads=["scal2"], writes=["scal2"])
    op("vector", lambda e: e.reciprocal(SCAL[0:1, 2:4], SCAL[0:1, 2:4]),
       reads=["scal2"], writes=["scal2"])
    op("tensor", lambda e: e.matmul(PS[6][:, 0:2], ONESR, SCAL[0:1, 2:4],
                                    start=True, stop=True),
       reads=["scal2", "cf"], writes=["PS6"])
    op("vector", lambda e: e.tensor_copy(INVSB[:, :], PS[6][:, 0:2]),
       reads=["PS6"], writes=["invsb"])

    xqbar = B.barrier("xc") + B.barrier("xq")

    def wstream(src_ap, quant=0, piece=1024, spill_ap=None, bf16_src=False):
        cols = src_ap.shape[-1]
        spp = piece // SLOTW
        parts = []
        for c0 in range(0, cols, piece):
            w = min(piece, cols - c0)
            if (wuc[0] % NSLOT) + spp > NSLOT:
                wuc[0] += NSLOT - (wuc[0] % NSLOT)
            s = wuc[0] % NSLOT
            wuc[0] += spp
            wres = [f"wl{s + k}" for k in range(spp)]
            bres = [f"wb{s + k}" for k in range(spp)]
            wap = WLBUF[:, s * SLOTW:s * SLOTW + w]
            bap = WBBUF[:, s * SLOTW:s * SLOTW + w]
            if bf16_src:
                op("sync", lambda e, a=src_ap, c0=c0, w=w, d=bap: e.dma_start(
                    out=d, in_=a[:, c0:c0 + w]),
                   writes=bres, dma=f"w{s}")
                parts.append((s, c0, w, bres))
                continue
            op("sync", lambda e, a=src_ap, c0=c0, w=w, d=wap: e.dma_start(
                out=d, in_=a[:, c0:c0 + w]),
               writes=wres, dma=f"w{s}")
            if not quant:
                op("scalar", lambda e, d=bap, sv=wap: e.activation(
                    d, sv, AF.Identity, bias=0.0, scale=1.0),
                   reads=wres, writes=bres)
            else:
                op("scalar", lambda e, d=wap, q=quant: e.activation(
                    d, d, AF.Identity, bias=0.0, scale=INVSB[:, q - 1:q]),
                   reads=wres + ["invsb"], writes=wres)
                op("vector", lambda e, d=wap: e.tensor_scalar(
                    d, d, -1.25, 1.25, ALU.max, ALU.min),
                   reads=wres, writes=wres)
                op("vector", lambda e, d=bap, sv=wap: e.tensor_scalar(
                    d, sv, MAGIC, MAGIC, ALU.add, ALU.subtract),
                   reads=wres, writes=bres)
                if spill_ap is not None:
                    op("gpsimd", lambda e, sp=spill_ap, c0=c0, w=w, sv=bap: e.dma_start(
                        out=sp[:, c0:c0 + w], in_=sv),
                       reads=bres, dma=f"sp{s}")
            parts.append((s, c0, w, bres))

        def tile(j, parts=parts):
            c = j * 128
            for (s, c0, w, bres) in parts:
                if c0 <= c < c0 + w:
                    return WBBUF[:, s * SLOTW + (c - c0):s * SLOTW + (c - c0) + 128], bres
            raise IndexError(c)
        return tile

    def proj(col_dst, res_dst, w_src, psb, act=AF.Silu):
        wt = wstream(w_src, piece=2048)
        for fcc in range(FC):
            wap, wres = wt(fcc)
            for mt in range(NMW):
                op("tensor", lambda e, w=wap, f=fcc, m=mt: e.matmul(
                    PS[psb + m][:, 0:MW], w,
                    XN[:, f * T + m * MW:f * T + m * MW + MW],
                    start=(f == 0), stop=(f == FC - 1)),
                   reads=wres + [f"xn{fcc}"], writes=[f"PS{psb + mt}"])
        for mt in range(NMW):
            op("scalar", lambda e, m=mt, d=col_dst: e.activation(
                POOLB[:, d + m * MW:d + (m + 1) * MW], PS[psb + m][:, 0:MW], act,
                bias=0.0, scale=1.0),
               reads=[f"PS{psb + mt}"], writes=[res_dst])

    KVSB = [SQb + pb * NB * HD for pb in range(2)]
    _skip = set(_os.environ.get("ATTN_SKIP", "").split(","))

    def stageA1(h):
        pb = h % 2
        TAp, TBp, TCp, TEp, TGp, QKp = (TAb[pb], TBb[pb], TCb[pb],
                                        TEb[pb], TGb[pb], QKb[pb])
        TDp = TDb[pb]
        op("scalar", lambda e, h=h, pb=pb: e.activation(DECQ[pb], ARRQ, AF.Exp,
                                                        bias=0.0, scale=NEGSB[:, h:h + 1]),
           reads=["cb", "negsb", f"decq{pb}"], writes=[f"decq{pb}"])
        op("scalar", lambda e, h=h, pb=pb: e.activation(DECK[pb], ARRK, AF.Exp,
                                                        bias=0.0, scale=NEGSB[:, h:h + 1]),
           reads=["cb", "negsb", f"deck{pb}"], writes=[f"deck{pb}"])
        op("scalar", lambda e, h=h, pb=pb: e.activation(DIAG[pb], DIDX, AF.Exp,
                                                        bias=0.0, scale=NEGSB[:, h:h + 1]),
           reads=["cb", "negsb", f"diag{pb}"], writes=[f"diag{pb}"])
        op("vector", lambda e, pb=pb: e.tensor_mul(DIAG[pb], DIAG[pb], DMSK),
           reads=[f"diag{pb}", "cb"], writes=[f"diag{pb}"])
        op("scalar", lambda e, h=h, pb=pb: e.activation(BDP[pb], BLKI, AF.Exp,
                                                        bias=0.0, scale=NEGSB[:, h:h + 1]),
           reads=["cf", "negsb", f"bdp{pb}"], writes=[f"bdp{pb}"])
        proj(TAp, f"ta{pb}", wq_in[h, :, :], 0)
        for i in range(NB):
            op("vector", lambda e, i=i, pb=pb, TEp=TEp, TAp=TAp: e.tensor_mul(
                POOLB[:, TEp + i * BLK:TEp + (i + 1) * BLK],
                POOLB[:, TAp + i * BLK:TAp + (i + 1) * BLK], DECQ[pb]),
               reads=[f"ta{pb}", f"decq{pb}"], writes=[f"te{pb}"])
            op("vector", lambda e, i=i, h=h, pb=pb, TEp=TEp: e.tensor_scalar_mul(
                POOLB[:, QCb + h * T + i * BLK:QCb + h * T + (i + 1) * BLK],
                POOLB[:, TEp + i * BLK:TEp + (i + 1) * BLK], BDP[pb][:, i:i + 1]),
               reads=[f"te{pb}", f"bdp{pb}"],
               extra_deps=(xqbar if h < 4 else []), writes=[f"qc{h}"])
    def stageA2(h):
        pb = h % 2
        TAp, TBp, TCp, TEp, TGp, QKp = (TAb[pb], TBb[pb], TCb[pb],
                                        TEb[pb], TGb[pb], QKb[pb])
        TDp = TDb[pb]
        proj(TBp, f"tb{pb}", wk_in[h, :, :], 2)
        for i in range(NB):
            for j in range(2):
                bank = 5 + j
                op("tensor", lambda e, i=i, j=j, bank=bank, TBp=TBp, TAp=TAp: e.matmul(
                    PS[bank][:, 0:BLK],
                    POOLB[:, TBp + (2 * i + j) * 128:TBp + (2 * i + j + 1) * 128],
                    POOLB[:, TAp + i * BLK:TAp + (i + 1) * BLK],
                    start=True, stop=True),
                   reads=[f"tb{pb}", f"ta{pb}"], writes=[f"PS{bank}"])
                op("vector", lambda e, i=i, j=j, bank=bank, pb=pb, QKp=QKp: e.tensor_mul(
                    POOLB[:, QKp + (2 * i + j) * BLK:QKp + (2 * i + j + 1) * BLK],
                    PS[bank][:, 0:BLK], DIAG[pb][:, j * BLK:(j + 1) * BLK]),
                   reads=[f"PS{bank}", f"diag{pb}"], writes=[f"qkd{pb}"])
        for i in range(NB):
            op("vector", lambda e, i=i, pb=pb, TCp=TCp, TBp=TBp: e.tensor_mul(
                POOLB[:, TCp + i * BLK:TCp + (i + 1) * BLK],
                POOLB[:, TBp + i * BLK:TBp + (i + 1) * BLK], DECK[pb]),
               reads=[f"tb{pb}", f"deck{pb}"], writes=[f"tc{pb}"])
        proj(TAp, f"ta{pb}", wv_in[h, :, :], 0)
        if "trans" in _skip:
            return
        for tt in range(NTC):
            bnk = PT if (tt // 2) % 2 == 0 else PS[4]
            bres = "PTb" if (tt // 2) % 2 == 0 else "PS4"
            op("tensor", lambda e, tt=tt, b=bnk, TCp=TCp: e.matmul(
                b[:, (tt % 2) * 128:(tt % 2 + 1) * 128],
                POOLB[:, TCp + tt * 128:TCp + (tt + 1) * 128], IDENT,
                start=True, stop=True, skip_group_check=True),
               reads=[f"tc{pb}", "cb"], writes=[bres], tag="ktr")
            if tt % 2 == 1:
                op("vector", lambda e, tt=tt, b=bnk, TGp=TGp: e.tensor_copy(
                    POOLB[:, TGp + (tt - 1) * 128:TGp + (tt + 1) * 128],
                    b[:, 0:256]),
                   reads=[bres], writes=[f"tg{pb}"])
        for tt in range(NTC):
            bnk = PT if (tt // 2) % 2 == 0 else PS[4]
            bres = "PTb" if (tt // 2) % 2 == 0 else "PS4"
            op("tensor", lambda e, tt=tt, b=bnk, TAp=TAp: e.matmul(
                b[:, (tt % 2) * 128:(tt % 2 + 1) * 128],
                POOLB[:, TAp + tt * 128:TAp + (tt + 1) * 128], IDENT,
                start=True, stop=True, skip_group_check=True),
               reads=[f"ta{pb}", "cb"], writes=[bres], tag="vtr")
            if tt % 2 == 1:
                op("vector", lambda e, tt=tt, b=bnk, TDp=TDp: e.tensor_copy(
                    POOLB[:, TDp + (tt - 1) * 128:TDp + (tt + 1) * 128],
                    b[:, 0:256]),
                   reads=[bres], writes=[f"td{pb}"])

    def stageC(h):
        pb = h % 2
        TDp, TGp = TDb[pb], TGb[pb]
        kb = KVSB[pb]
        for i in range(NB):
            dbk = 2 + i % 2
            for j in range(2):
                op("tensor", lambda e, i=i, j=j, TGp=TGp, TDp=TDp, dbk=dbk: e.matmul(
                    PS[dbk][:, 0:HD],
                    POOLB[:, TGp + (2 * i + j) * 128:TGp + (2 * i + j + 1) * 128],
                    POOLB[:, TDp + (2 * i + j) * 128:TDp + (2 * i + j + 1) * 128],
                    start=(j == 0), stop=(j == 1)),
                   reads=[f"tg{pb}", f"td{pb}"], writes=[f"PS{dbk}"])
            if i == 0:
                op("vector", lambda e, h=h, dbk=dbk: e.tensor_copy(
                    POOLA[:, KVc + h * HD:KVc + (h + 1) * HD], PS[dbk][:, 0:HD]),
                   reads=[f"PS{dbk}"], writes=[f"kv{h}"])
            else:
                op("vector", lambda e, h=h, dbk=dbk: e.scalar_tensor_tensor(
                    POOLA[:, KVc + h * HD:KVc + (h + 1) * HD],
                    POOLA[:, KVc + h * HD:KVc + (h + 1) * HD], BDB[:, h:h + 1],
                    PS[dbk][:, 0:HD], ALU.mult, ALU.add),
                   reads=[f"kv{h}", "bdb", f"PS{dbk}"], writes=[f"kv{h}"])
            if i < NB - 1:
                op("vector", lambda e, h=h, i=i, kb=kb: e.tensor_copy(
                    POOLB[:, kb + i * HD:kb + (i + 1) * HD],
                    POOLA[:, KVc + h * HD:KVc + (h + 1) * HD]),
                   reads=[f"kv{h}"], writes=[f"kvsb{pb}"],
                   extra_deps=(sqgbar if h < 2 else []))

    def stageD(h):
        pb = h % 2
        TDp, TEp, QKp = TDb[pb], TEb[pb], QKb[pb]
        kb = KVSB[pb]
        for i in range(NB):
            bank = 5 if i % 2 == 0 else 6
            if i > 0:
                op("tensor", lambda e, i=i, TEp=TEp, kb=kb, bank=bank: e.matmul(
                    PS[bank][:, 0:BLK], POOLB[:, kb + (i - 1) * HD:kb + i * HD],
                    POOLB[:, TEp + i * BLK:TEp + (i + 1) * BLK],
                    start=True, stop=False),
                   reads=[f"kvsb{pb}", f"te{pb}"], writes=[f"PS{bank}"])
            for j in range(2):
                op("tensor", lambda e, i=i, j=j, TDp=TDp, QKp=QKp, bank=bank: e.matmul(
                    PS[bank][:, 0:BLK],
                    POOLB[:, TDp + (2 * i + j) * 128:TDp + (2 * i + j + 1) * 128],
                    POOLB[:, QKp + (2 * i + j) * BLK:QKp + (2 * i + j + 1) * BLK],
                    start=(i == 0 and j == 0), stop=(j == 1)),
                   reads=[f"td{pb}", f"qkd{pb}"], writes=[f"PS{bank}"])
            op("vector", lambda e, h=h, i=i, bank=bank: e.tensor_copy(
                O[:, h * T + i * BLK:h * T + (i + 1) * BLK], PS[bank][:, 0:BLK]),
               reads=[f"PS{bank}"], writes=[f"o{h}"])

    sqgbar = B.barrier("xdbg")
    for h in range(NH):
        stageA1(h)
        stageA2(h)
        if h >= 1 and "cd" not in _skip:
            stageC(h - 1)
            stageD(h - 1)
    if "cd" not in _skip:
        stageC(NH - 1)
        stageD(NH - 1)

    if cfg.debug:
        dump_fm(dbg["d_o1"], lambda ccc, w0: O[:, ccc * T + w0: ccc * T + w0 + MW],
                NH, T, lambda ccc: f"o{ccc}", chan="dbg2")

    if upto == "attn":
        return _finish()

    def kcbar(kc):
        c0, c1_ = Hb + kc * FTS, Hb + (kc + 1) * FTS
        bars = []
        if c0 < QCb + NH * T and c1_ > QCb:
            for hh in range(max(0, (c0 - QCb) // T), min(NH, (c1_ - QCb + T - 1) // T)):
                bars += B.barrier(f"qc{hh}")
        if c0 < TR0 + 16 * T and c1_ > TR0:
            for rn in ("ta", "tb", "tc", "td", "te", "tg", "qkd"):
                bars += B.barrier(rn)
            bars += B.barrier("xdbg")
        if c0 < KVIb + CCW and c1_ > KVIb:
            bars += B.barrier("kvin")
        return bars

    def emit_f1(th, kcs):
        tw = th * FTS
        for kc in kcs:
            bank = 2 + kc % 2
            wt = wstream(w1_in[kc, :, :], quant=1, piece=512)
            for fcc in range(FC):
                wap, wres = wt(fcc)
                op("tensor", lambda e, w=wap, f=fcc, bank=bank, tw=tw: e.matmul(
                    PS[bank][:, 0:FTS], w,
                    XN[:, f * T + tw:f * T + tw + FTS],
                    start=(f == 0), stop=(f == FC - 1)),
                   reads=wres + [f"xn{fcc}"], writes=[f"PS{bank}"])
            op("scalar", lambda e, kc=kc, bank=bank: e.activation(
                POOLB[:, Hb + kc * FTS:Hb + (kc + 1) * FTS], PS[bank][:, 0:FTS], AF.Gelu,
                bias=B1C[:, kc:kc + 1], scale=1.0),
               reads=[f"PS{bank}", "b1c"], extra_deps=(kcbar(kc) if th == 0 else []),
               writes=[f"h{kc}"])

    F1SAFE = [kc for kc in range(KC)
              if (Hb + (kc + 1) * FTS <= QCb or Hb + kc * FTS >= QCb + NH * T)
              and (Hb + (kc + 1) * FTS <= KVIb or Hb + kc * FTS >= KVIb + CCW)]
    F1LATE = [kc for kc in range(KC) if kc not in F1SAFE]

    op("vector", lambda e: e.tensor_copy(POOLB[:, KVIb:KVIb + CCW], POOLA[:, KVc:KVc + CCW]),
       reads=[f"kv{h}" for h in range(NH)],
       extra_deps=B.barrier("qkd") + B.barrier("tg") + B.barrier("td"),
       writes=["kvin"])
    op("gpsimd", lambda e: e.dma_start(out=cc_in[:, :], in_=POOLB[:, KVIb:KVIb + CCW]),
       reads=["kvin"], writes=["ccin"], dma="cc")
    if _os.environ.get("NOCC"):
        cctok = op("gpsimd", lambda e: e.dma_start(out=cc_out[0:P, :], in_=cc_in[:, :]),
                   reads=["ccin"], writes=["ccout"], dma="nocc1")
    else:
        cctok = op("gpsimd", lambda e: e.collective_compute(
            "AllGather", ALU.bypass, replica_groups=[list(range(8))],
            ins=[cc_in.ap()], outs=[cc_out.ap()]),
            reads=["ccin"], writes=["ccout"], dma="ccol1")
    emit_f1(0, F1SAFE[:8])
    if upto == "cc":
        return _finish()
    if cfg.debug:
        op("vector", lambda e: e.tensor_copy(xbuf(3)[0:1, 0:2], SCAL[0:1, 2:4]),
           reads=["scal2"], writes=["xdbg"])
        op("gpsimd", lambda e: e.dma_start(out=dbg["d_invs"][:, :], in_=xbuf(3)[0:1, 0:2]),
           reads=["xdbg"], dma="dbg3")

    kvbar = B.barrier("kv")
    KPW = SLOTW * 2
    NKH = (CCW + KPW - 1) // KPW
    for r in range(8):
        for hf in range(NKH):
            w = min(KPW, CCW - hf * KPW)
            s = (r * NKH + hf) % NSLOT
            op("gpsimd", lambda e, r=r, s=s, hf=hf, w=w: e.dma_start(
                out=WL[s].bitcast(BF16)[:, 0:w],
                in_=cc_out[r * P:(r + 1) * P, hf * KPW:hf * KPW + w]),
               writes=[f"wl{s}"], extra_deps=[cctok], reads=["ccout"], dma=f"kv{s}")
            acc_dst = POOLA[:, KVc + (r % 2) * CCW + hf * KPW: KVc + (r % 2) * CCW + hf * KPW + w]
            acc_src = POOLA[:, KVc + ((r + 1) % 2) * CCW + hf * KPW: KVc + ((r + 1) % 2) * CCW + hf * KPW + w]
            if r == 0:
                op("vector", lambda e, s=s, d=acc_dst, w=w: e.tensor_scalar_mul(
                    d, WL[s].bitcast(BF16)[:, 0:w], KVSELB[:, 0:1]),
                   reads=[f"wl{s}", "kvselb"], extra_deps=kvbar,
                   writes=["acc0"])
            else:
                op("vector", lambda e, s=s, r=r, d=acc_dst, a=acc_src, w=w: e.scalar_tensor_tensor(
                    d, WL[s].bitcast(BF16)[:, 0:w], KVSELB[:, r:r + 1], a,
                    ALU.mult, ALU.add),
                   reads=[f"wl{s}", "kvselb", f"acc{(r + 1) % 2}"], writes=[f"acc{r % 2}"])
    op("vector", lambda e: e.tensor_copy(POOLB[:, KVIb:KVIb + CCW],
                                         POOLA[:, KVc + CCW:KVc + 2 * CCW]),
       reads=["acc1", "kvin"], writes=["kvin"])
    emit_f1(0, F1SAFE[8:])
    if upto == "kvin":
        return _finish()
    if cfg.debug:
        pass

    for h in range(NH):
        for mt in range(NMW):
            bank = mt
            op("tensor", lambda e, h=h, m=mt, bank=bank: e.matmul(
                PS[bank][:, 0:MW], POOLB[:, KVIb + h * HD:KVIb + (h + 1) * HD],
                POOLB[:, QCb + h * T + m * MW:QCb + h * T + (m + 1) * MW],
                start=True, stop=True),
               reads=["kvin", f"qc{h}"], writes=[f"PS{bank}"])
            op("vector", lambda e, h=h, m=mt, bank=bank: e.tensor_add(
                O[:, h * T + m * MW:h * T + (m + 1) * MW],
                O[:, h * T + m * MW:h * T + (m + 1) * MW], PS[bank][:, 0:MW]),
               reads=[f"PS{bank}", f"o{h}"], writes=[f"o{h}"])

    if upto == "corr":
        return _finish()

    for h in range(NH):
        op("scalar", lambda e, h=h: e.activation(POOLB[:, SQb:SQb + T], O[:, h * T:(h + 1) * T],
                                                 AF.Square, bias=0.0, scale=1.0),
           reads=[f"o{h}", "sqb"], writes=["sqb"],
           extra_deps=(B.barrier("kvsb") if h == 0 else []))
        for mt in range(NMW):
            op("tensor", lambda e, h=h, m=mt: e.matmul(
                PS[5 + m][0:1, 0:MW], ONESC, POOLB[:, SQb + m * MW:SQb + (m + 1) * MW],
                start=(h == 0), stop=(h == NH - 1)),
               reads=["sqb", "cb"], writes=[f"PS{5 + mt}"])
    for mt in range(NMW):
        op("vector", lambda e, m=mt: e.tensor_scalar(
            rowm(0, m), PS[5 + m][0:1, 0:MW], 1.0 / (NH * HD), 1e-6, ALU.mult, ALU.add),
           reads=[f"PS{5 + mt}"], extra_deps=B.barrier("row_"),
           writes=["rms_v"])
    rsqrt_row(0, 3, 2, "rms_v", "rms_r", "rms_t")
    if cfg.debug:
        op("vector", lambda e: e.tensor_copy(xbuf(3)[0:1, 0:min(T, MW)], row(3)[0:1, 0:min(T, MW)]),
           reads=["rms_r"], writes=["xdbg"])
        op("gpsimd", lambda e: e.dma_start(out=dbg["d_rstd"][0:1, 0:min(T, MW)], in_=xbuf(3)[0:1, 0:min(T, MW)]),
           reads=["xdbg"], dma="dbg3")

    for mt in range(NMW):
        op("tensor", lambda e, m=mt: e.matmul(PS[4 + m][:, 0:MW], ONESR, rowm(3, m),
                                              start=True, stop=True),
           reads=["rms_r", "cf"], writes=[f"PS{4 + mt}"])

    if upto == "rms":
        return _finish()

    for gc in range(NH):
        gp = gc % 2
        gb = 2 * gp
        gtc = GTb
        gsl = 2 if (gp == 0 or cfg.debug) else 3
        wt = wstream(wg_in[gc, :, :], piece=2048)
        for fcc in range(FC):
            wap, wres = wt(fcc)
            for mt in range(NMW):
                op("tensor", lambda e, w=wap, f=fcc, m=mt, gb=gb: e.matmul(
                    PS[gb + m][:, 0:MW], w,
                    XN[:, f * T + m * MW:f * T + m * MW + MW],
                    start=(f == 0), stop=(f == FC - 1)),
                   reads=wres + [f"xn{fcc}"], writes=[f"PS{gb + mt}"])
        for mt in range(NMW):
            op("scalar", lambda e, m=mt, gb=gb, gtc=gtc: e.activation(
                POOLB[:, gtc + m * MW:gtc + (m + 1) * MW], PS[gb + m][:, 0:MW], AF.Sigmoid,
                bias=0.0, scale=1.0),
               reads=[f"PS{gb + mt}"], writes=["gt"],
               extra_deps=((B.barrier("sqb") + B.barrier("kvsb")) if gc < 2 and mt == 0 else []))
            op("vector", lambda e, m=mt, gc=gc, gsl=gsl: e.tensor_mul(
                xbuf(gsl), O[:, gc * T + m * MW:gc * T + (m + 1) * MW], PS[4 + m][:, 0:MW]),
               reads=[f"o{gc}", f"PS{4 + mt}"],
               extra_deps=(B.barrier("acc") + B.barrier("xdbg") if gc < 2 and mt == 0 else []),
               writes=[f"gsc{gp}"])
            op("vector", lambda e, m=mt, gc=gc, gsl=gsl: e.tensor_scalar_mul(
                xbuf(gsl), xbuf(gsl), NWC[:, gc:gc + 1]),
               reads=[f"gsc{gp}", "nwc"], writes=[f"gsc{gp}"])
            op("vector", lambda e, m=mt, gc=gc, gsl=gsl, gtc=gtc: e.tensor_mul(
                O[:, gc * T + m * MW:gc * T + (m + 1) * MW], xbuf(gsl),
                POOLB[:, gtc + m * MW:gtc + (m + 1) * MW]),
               reads=[f"gsc{gp}", "gt"], writes=[f"o{gc}"])

    if cfg.debug:
        dump_fm(dbg["d_og"], lambda ccc, w0: O[:, ccc * T + w0: ccc * T + w0 + MW],
                NH, T, lambda ccc: f"o{ccc}", chan="dbg4")

    if upto == "mid":
        return _finish()

    NKQ = 4
    KCQ = KC // NKQ
    sbar = B.barrier("row_") + B.barrier("rms_") + B.barrier("acc")
    for th in range(2):
        tw = th * FTS
        for fcc in range(FC):
            bank = fcc % 2
            wt = wstream(wo_in[fcc, :, :], piece=2048)
            for oc in range(NH):
                wap, wres = wt(oc)
                op("tensor", lambda e, w=wap, oc=oc, bank=bank, tw=tw: e.matmul(
                    PS[bank][:, 0:FTS], w,
                    O[:, oc * T + tw:oc * T + tw + FTS],
                    start=(oc == 0), stop=(oc == NH - 1)),
                   reads=wres + [f"o{oc}"], writes=[f"PS{bank}"])
            op("vector", lambda e, f=fcc, bank=bank: e.tensor_copy(
                S_ap(f * FTS, FTS), PS[bank][:, 0:FTS]),
               reads=[f"PS{bank}"], extra_deps=(sbar if th == 0 else []),
               writes=[f"s{fcc}"])
        if cfg.debug:
            for fcc in range(FC):
                op("scalar", lambda e, f=fcc: e.activation(
                    xbuf(3)[:, 0:FTS], S_ap(f * FTS, FTS), AF.Identity,
                    bias=0.0, scale=1.0),
                   reads=[f"s{fcc}"], writes=["xdbg"])
                op("gpsimd", lambda e, f=fcc, tw=tw: e.dma_start(
                    out=dbg["d_s"][f * 128:(f + 1) * 128, tw:tw + FTS], in_=xbuf(3)[:, 0:FTS]),
                   reads=["xdbg"], dma="dbg5")
        emit_f1(th, F1LATE if th == 0 else list(range(KC)))
        for dc in range(FC):
            bank = 4 + dc % 2
            for q in range(NKQ):
                wt = wstream(w2_in[dc * NKQ + q, :, :], quant=2, piece=512)
                for kq in range(KCQ):
                    kc = q * KCQ + kq
                    wap, wres = wt(kq)
                    op("tensor", lambda e, w=wap, kc=kc, bank=bank: e.matmul(
                        PS[bank][:, 0:FTS], w,
                        POOLB[:, Hb + kc * FTS:Hb + (kc + 1) * FTS],
                        start=(kc == 0), stop=(kc == KC - 1)),
                       reads=wres + [f"h{kc}"], writes=[f"PS{bank}"])
            op("scalar", lambda e, dc=dc, bank=bank: e.activation(
                xbuf(2)[:, 0:FTS], PS[bank][:, 0:FTS], AF.Silu,
                bias=B2C[:, dc:dc + 1], scale=1.0),
               reads=[f"PS{bank}", "b2c"], writes=["gsc"])
            op("gpsimd", lambda e, dc=dc, tw=tw: e.dma_start(
                out=xbuf(0)[:, 0:FTS], in_=x_in[dc * 128:(dc + 1) * 128, tw:tw + FTS]),
               writes=["xb0"],
               extra_deps=(B.barrier("acc") if (th == 0 and dc == 0) else []),
               dma="xf")
            op("vector", lambda e, dc=dc: e.tensor_mul(
                xbuf(2)[:, 0:FTS], xbuf(2)[:, 0:FTS], S_ap(dc * FTS, FTS)),
               reads=["gsc", f"s{dc}"], writes=["gsc"])
            op("vector", lambda e, dc=dc: e.tensor_add(
                xbuf(1)[:, 0:FTS], xbuf(2)[:, 0:FTS], xbuf(0)[:, 0:FTS]),
               reads=["gsc", "xb0", "outb"], writes=["outb"])
            op("gpsimd", lambda e, dc=dc, tw=tw: e.dma_start(
                out=out_t[dc * 128:(dc + 1) * 128, tw:tw + FTS], in_=xbuf(1)[:, 0:FTS]),
               reads=["outb"], dma="out")

    fin = B.dma_sem("out")
    op("gpsimd", lambda e: e.nop(), extra_deps=[(fin, B.dcount["out"], "gpsimd", True)])

    return _finish()


def _xbuf_doc():
    pass


def prep_in_maps(cfg, L0_attn_hidden_states, L0_attn_slope_rate, L0_attn_w_qkv,
                 L0_attn_w_gate, L0_attn_w_out, L0_attn_norm_w,
                 L0_ffn_w1, L0_ffn_b1, L0_ffn_w2, L0_ffn_b2):
    P, D, T, NH, HD, DFF = cfg.P, cfg.D, cfg.T, cfg.NH, cfg.HD, cfg.DFF
    FC, KC = cfg.FC, cfg.KC
    f32 = np.float32
    x = np.ascontiguousarray(np.asarray(L0_attn_hidden_states, dtype=f32)).reshape(-1, D)
    wq_t = np.ascontiguousarray(np.asarray(L0_attn_w_qkv, dtype=f32)
                                .reshape(NH, 3, HD, FC, P)[:, 0]
                                .transpose(0, 3, 2, 1)).reshape(NH, P, FC * 128)
    wk_t = np.ascontiguousarray(np.asarray(L0_attn_w_qkv, dtype=f32)
                                .reshape(NH, 3, HD, FC, P)[:, 1]
                                .transpose(0, 3, 2, 1)).reshape(NH, P, FC * 128)
    wv_t = np.ascontiguousarray(np.asarray(L0_attn_w_qkv, dtype=f32)
                                .reshape(NH, 3, HD, FC, P)[:, 2]
                                .transpose(0, 3, 2, 1)).reshape(NH, P, FC * 128)
    wg_t = np.ascontiguousarray(np.asarray(L0_attn_w_gate, dtype=f32)
                                .reshape(FC, 128, FC, P).transpose(0, 3, 2, 1)).reshape(FC, P, FC * 128)
    wo_t = np.ascontiguousarray(np.asarray(L0_attn_w_out, dtype=f32)
                                .reshape(FC, 128, FC, P).transpose(0, 3, 2, 1)).reshape(FC, P, FC * 128)
    w1_t = np.ascontiguousarray(np.asarray(L0_ffn_w1, dtype=f32)
                                .reshape(KC, 128, FC, P).transpose(0, 3, 2, 1)).reshape(KC, P, FC * 128)
    NKQ = 4
    KCQ = KC // NKQ
    w2r = np.asarray(L0_ffn_w2, dtype=f32).reshape(FC, 128, KC, P)
    w2_t = np.ascontiguousarray(
        w2r.transpose(0, 2, 3, 1)
        .reshape(FC, NKQ, KCQ, P, 128)
        .transpose(0, 1, 3, 2, 4)
    ).reshape(FC * NKQ, P, KCQ * 128)
    b1_t = np.ascontiguousarray(np.asarray(L0_ffn_b1, dtype=f32).reshape(KC, P).T)
    b2_t = np.ascontiguousarray(np.asarray(L0_ffn_b2, dtype=f32).reshape(FC, P).T)
    nw_t = np.ascontiguousarray(np.asarray(L0_attn_norm_w, dtype=f32).reshape(FC, P).T)
    slopes = np.ascontiguousarray(np.asarray(L0_attn_slope_rate, dtype=f32).reshape(1, NH))

    n1t = max(1, (DFF * D // 8) // (128 * 2048))
    c1 = (DFF * D // 8) // (128 * n1t)
    w1f = np.asarray(L0_ffn_w1, dtype=f32).reshape(-1)
    w2f = np.asarray(L0_ffn_w2, dtype=f32).reshape(-1)

    in_maps = []
    ncore = cfg.n_cores
    for c in range(ncore):
        xs = np.ascontiguousarray(x[c * T:(c + 1) * T].T)
        sel = np.zeros((1, 8), dtype=f32)
        if c % 2 == 1:
            sel[0, c - 1] = 1.0
        sz = DFF * D // 8
        m = {
            "x_fm": xs, "wq_t": wq_t, "wk_t": wk_t, "wv_t": wv_t,
            "wg_t": wg_t, "wo_t": wo_t, "w1_t": w1_t, "w2_t": w2_t,
            "w1sl": np.ascontiguousarray(w1f[c * sz:(c + 1) * sz].reshape(n1t, P, c1)),
            "w2sl": np.ascontiguousarray(w2f[c * sz:(c + 1) * sz].reshape(n1t, P, c1)),
            "b1_t": b1_t, "b2_t": b2_t, "nw_t": nw_t, "slopes": slopes,
            "kvsel": sel,
        }
        in_maps.append(m)
    return in_maps


def kernel(**inputs):
    cfg = CFG(D=2048, T=1024, NH=16, DFF=8192)
    in_maps = prep_in_maps(cfg, **inputs)
    nc, _ = build_nc(cfg)
    res = run_bass_kernel_spmd(nc, in_maps, core_ids=list(range(cfg.n_cores)))
    outs = [np.ascontiguousarray(r["out_fm"].T) for r in res.results]
    full = np.concatenate(outs, axis=0)
    return full.reshape(4, 2048, 2048)
